# version 17
# baseline (speedup 1.0000x reference)
import numpy as np
import ml_dtypes
import concourse.bass as bass
import concourse.mybir as mybir
from concourse.bass_utils import run_bass_kernel_spmd
from concourse.ap import AP
from contextlib import ExitStack

F32 = mybir.dt.float32
BF16 = mybir.dt.bfloat16
AF = mybir.ActivationFunctionType
ALU = mybir.AluOpType
AX = mybir.AxisListType

S, D, NH, DH, FFD, NE = 4096, 1024, 16, 64, 4096, 8
TOK = 2048
NTI = 16
EPS = 1e-5
NEG = -30000.0
GAMMAS = [1.0 - 2.0 ** (-5 - (h // 2)) for h in range(NH)]

ENGS = ("sync", "tensor", "vector", "scalar", "gpsimd")


class Sem:
    __slots__ = ("h", "n", "name")

    def __init__(self, h, name):
        self.h = h
        self.n = 0
        self.name = name


COND_ENGS = ("tensor", "vector", "sync")


class KB:
    def __init__(self, nc, es):
        self.nc = nc
        self.es = es
        self.ops = {e: [] for e in ENGS}
        self.bar = None
        self.region = None
        self.regs = None

    def sem(self, name):
        return Sem(self.es.enter_context(self.nc.semaphore(name)), name)

    def add(self, eng, build, waits=(), inc=None, inc_n=1):
        wl = [(s.h, v) for (s, v) in waits]

        def f(e, wl=wl, build=build, inc=inc, inc_n=inc_n):
            for h, v in wl:
                e.wait_ge(h, v)
            ins = build(e)
            if inc is not None:
                ins.then_inc(inc.h, inc_n)
        self.ops[eng].append([f, build, wl, inc, inc_n])
        if inc is not None:
            inc.n += inc_n
            if self.region is not None and eng in self.region:
                t = self.region[eng]
                t[inc] = t.get(inc, 0) + inc_n

    def begin_region(self, flag_ap_fn, ready):
        assert self.region is None
        self.region = {eng: {} for eng in COND_ENGS}
        for eng in COND_ENGS:
            self.ops[eng].append(["IF", flag_ap_fn, (ready[0].h, ready[1])])

    def end_region(self):
        for eng in COND_ENGS:
            comp = self.region[eng]
            self.ops[eng].append(["ELSE"])
            for sem, n in comp.items():
                self.ops[eng].append(["COMP", sem.h, n])
            self.ops[eng].append(["END"])
        self.region = None

    def w(self, *sems):
        return [(s, s.n) for s in sems]

    def wait_at(self, eng, sem, v):
        self.ops[eng].append([lambda e, h=sem.h, vv=v: e.wait_ge(h, vv), None, [], None, 0])

    def barrier(self, drain=()):
        if self.bar is None:
            self.bar = self.sem("bar")
        for eng in ENGS:
            q = self.ops[eng]
            if eng in ("sync", "gpsimd"):
                for s in drain:
                    self.wait_at(eng, s, s.n)
            q.append([lambda e: e.sem_inc(self.bar.h, 1), None, [], None, 0])
            self.bar.n += 1
        target = self.bar.n
        for eng in ENGS:
            self.wait_at(eng, self.bar, target)

    def emit(self):
        nreg = {eng: [getattr(self.nc, eng).alloc_register(f"fl_{eng}{i}")
                      for i in range(4)] for eng in COND_ENGS}
        with self.nc.Block() as blk:
            for eng in ENGS:
                items = self.ops[eng]
                regs = nreg.get(eng)

                def run(e, items=items, regs=regs):
                    guard = None
                    ri = 0
                    for it in items:
                        tag = it[0]
                        if tag == "IF":
                            _, apf, rd = it
                            e.wait_ge(rd[0], rd[1])
                            reg = regs[ri % 4]
                            ri += 1
                            e.reg_load(reg, apf())
                            guard = e.If_ne(reg, 0)
                            guard.__enter__()
                        elif tag == "ELSE":
                            guard.__exit__(None, None, None)
                            guard = e.Else()
                            guard.__enter__()
                        elif tag == "COMP":
                            e.sem_inc(it[1], it[2])
                        elif tag == "END":
                            guard.__exit__(None, None, None)
                            guard = None
                        else:
                            it[0](e)
                getattr(blk, eng)(run)


def cdiv(a, b):
    return (a + b - 1) // b



def build_program(debug_phase=None):
    nc = bass.Bass("TRN2", target_bir_lowering=False, debug=False)
    es = ExitStack()
    k = KB(nc, es)
    A = k.add
    W = k.w

    def inp(name, shape, dt=F32):
        return nc.dram_tensor(name, list(shape), dt, kind="ExternalInput").ap()

    xb = inp("xb", [S, D])
    xq = inp("xq", [TOK, D])
    peT = inp("peT", [128, 8, S + 256], BF16)
    amask = inp("amask", [128, 256])
    rmask = inp("rmask", [128, 128])
    ident_i = inp("ident", [128, 128])
    selc = inp("selc", [64, 2])
    gq_t = inp("gq_t", [64, NH, 512])
    gk_t = inp("gk_t", [64, NH, 512])
    wq_a = inp("wq_a", [NH, D, DH])
    wk_a = inp("wk_a", [NH, D, DH])
    wv_a = inp("wv_a", [NH, D, DH])
    gnw_t = inp("gnw_t", [64, NH])
    gnb_t = inp("gnb_t", [64, NH])
    wout_a = inp("wout_a", [D, D])
    wqkv = inp("wqkv", [D, 3 * D])
    wr = inp("wr", [D, D])
    ub_t = inp("ub_t", [DH, NH])
    vb_t = inp("vb_t", [DH, NH])
    wo = inp("wo", [D, D])
    ln_g = inp("ln_g", [128, 3, 8])
    ln_b = inp("ln_b", [128, 3, 8])
    wff1 = inp("wff1", [D, FFD])
    bff1_c = inp("bff1_c", [128, 32])
    wff2 = inp("wff2", [FFD, D])
    bff2_c = inp("bff2_c", [128, 8])
    wg = inp("wg", [D, NE])
    we1 = inp("we1", [NE, D, FFD])
    be1_c = inp("be1_c", [NE, 128, 32])
    we2 = inp("we2", [NE, FFD, D])
    be2_r = inp("be2_r", [NE, D])
    wout_f = inp("wout_f", [D, D])
    trio = inp("trio", [128, 128], BF16)
    aon1 = inp("aon1", [128, 128], BF16)
    eoffm = inp("eoffm", [128, 16, NE])
    offm = inp("offm", [NE, 128])
    iom = inp("iom", [128, 512])
    iosc = inp("iosc", [128, 1])
    tiebm = inp("tiebm", [128, 16, NE])
    spillq = inp("spillq", [128, 128], BF16)
    pmq = inp("pmq", [128, 128], BF16)
    negt = inp("negt", [128, 128], BF16)

    out = nc.dram_tensor("out", [TOK, D], F32, kind="ExternalOutput").ap()

    def scr(name, shape, dt=F32):
        return nc.dram_tensor(name, list(shape), dt, kind="Internal").ap()

    pskew = [scr("pskew0", [128 * (S + 256)], BF16), scr("pskew1", [128 * (S + 256)], BF16)]
    pskew_dbg = scr("pskew_dbg", [128 * 256], BF16)
    ctx_park = scr("ctx_park", [128, 8, TOK], BF16)
    cn_park = scr("cn_park", [128, 8, TOK], BF16)
    h1_park = scr("h1_park", [128, 8, TOK], F32)
    h2b_park = scr("h2b_park", [128, 8, TOK], BF16)
    h3_park = scr("h3_park", [128, 8, TOK], F32)

    dbg_outs = {}

    def dbg(name, shape, dt=F32):
        ap = nc.dram_tensor("dbg_" + name, list(shape), dt, kind="ExternalOutput").ap()
        dbg_outs[name] = ap
        return ap

    SBp = lambda n, s, d=F32: es.enter_context(nc.sbuf_tensor(n, list(s), d))
    sb_ident = SBp("c_ident", [128, 128])
    sb_ident16 = SBp("c_ident16", [128, 128], BF16)
    sb_rmask = SBp("c_rmask", [128, 128])
    sb_ones_c = SBp("c_ones_c", [128, 1])
    sb_ones_r = SBp("c_ones_r", [1, 128])
    sb_sel = SBp("c_sel", [64, 2])
    sb_scrv = SBp("c_scrv", [128, 16])
    sb_scrs = SBp("c_scrs", [128, 16])
    sb_lng = SBp("c_lng", [128, 3, 8])
    sb_lnb = SBp("c_lnb", [128, 3, 8])

    ld = k.sem("ld")
    gld = k.sem("gld")
    st = k.sem("st")
    tS = k.sem("tS")
    vS = k.sem("vS")
    sS = k.sem("sS")
    pS = k.sem("pS")
    bS = k.sem("bS")
    aS = k.sem("aS")

    A("sync", lambda e: e.dma_start(out=sb_ident[:], in_=ident_i[:]), inc=ld, inc_n=16)
    A("sync", lambda e: e.dma_start(out=sb_rmask[:], in_=rmask[:]), inc=ld, inc_n=16)
    A("sync", lambda e: e.dma_start(out=sb_sel[:], in_=selc[:]), inc=ld, inc_n=16)
    A("sync", lambda e: e.dma_start(out=sb_lng[:], in_=ln_g[:]), inc=ld, inc_n=16)
    A("sync", lambda e: e.dma_start(out=sb_lnb[:], in_=ln_b[:]), inc=ld, inc_n=16)
    A("vector", lambda e: e.memset(sb_ones_c[:], 1.0), inc=vS)
    A("vector", lambda e: e.memset(sb_ones_r[:], 1.0), inc=vS)
    A("vector", lambda e: e.tensor_copy(sb_ident16[:], sb_ident[:]),
      waits=[(ld, 16)], inc=vS)

    sAC = ExitStack()
    xnT = sAC.enter_context(nc.sbuf_tensor("xnT", [128, 8, S], BF16))
    xnqT16 = sAC.enter_context(nc.sbuf_tensor("xnqT16", [128, 8, TOK], BF16))

    sA = ExitStack()
    SA = lambda n, s, d=F32: sA.enter_context(nc.sbuf_tensor(n, list(s), d))
    PA = lambda n, s, d=F32: sA.enter_context(nc.psum_tensor(n, list(s), d))
    a_xt = [SA(f"a_xt{i}", [128, D]) for i in range(2)]
    a_xn = [SA(f"a_xn{i}", [128, D]) for i in range(2)]
    a_sq = SA("a_sqs", [128, D])
    a_stat = SA("a_stat", [128, 8 * 32])
    ps_tr = [PA(f"a_ptr{i}", [128, 128]) for i in range(2)]

    def fv():
        A("vector", lambda e: e.tensor_copy(sb_scrv[:], sb_ident[:, 0:16]), inc=vS)

    def fs():
        A("scalar", lambda e: e.copy(sb_scrs[:], sb_ident[:, 0:16]), inc=sS)

    def emit_ln_pass(src_fn, nt, stat, xtb, xnb, sqb, trb, dst16, dst32):
        n = nt
        for j in range(nt):
            bi = j % 2
            A("sync", lambda e, j=j, bi=bi: e.dma_start(out=xtb[bi][:], in_=src_fn(j)),
              waits=W(vS, sS), inc=ld, inc_n=16)
            A("vector", lambda e, j=j, bi=bi: e.reduce_sum(
                stat[:, j:j + 1], xtb[bi][:], axis=AX.X), waits=W(ld), inc=vS)
            A("scalar", lambda e, j=j, bi=bi: e.activation(
                sqb[:], xtb[bi][:], AF.Square, accum_out=stat[:, n + j:n + j + 1]),
              waits=W(ld), inc=sS)
        fs()
        fv()
        A("vector", lambda e: e.tensor_scalar_mul(stat[:, 2 * n:3 * n], stat[:, 0:n],
                                                  1.0 / D), waits=W(sS), inc=vS)
        A("vector", lambda e: e.tensor_scalar(stat[:, 7 * n:8 * n], stat[:, n:2 * n],
                                              1.0 / D, EPS, ALU.mult, ALU.add), inc=vS)
        A("vector", lambda e: e.tensor_tensor(stat[:, 6 * n:7 * n], stat[:, 2 * n:3 * n],
                                              stat[:, 2 * n:3 * n], ALU.mult), inc=vS)
        fv()
        A("vector", lambda e: e.tensor_sub(stat[:, 3 * n:4 * n], stat[:, 7 * n:8 * n],
                                           stat[:, 6 * n:7 * n]), inc=vS)
        fv()
        A("scalar", lambda e: e.activation(stat[:, 4 * n:5 * n], stat[:, 3 * n:4 * n],
                                           AF.Sqrt), waits=W(vS), inc=sS)
        fs()
        A("vector", lambda e: e.reciprocal(stat[:, 4 * n:5 * n], stat[:, 4 * n:5 * n]),
          waits=W(sS), inc=vS)
        fv()
        A("vector", lambda e: e.scalar_tensor_tensor(
            stat[:, 5 * n:6 * n], stat[:, 2 * n:3 * n], -1.0, stat[:, 4 * n:5 * n],
            ALU.mult, ALU.mult), inc=vS)
        fv()
        fv()
        fmark = vS.n
        for j in range(nt):
            bi = j % 2
            A("sync", lambda e, j=j, bi=bi: e.dma_start(out=xtb[bi][:], in_=src_fn(j)),
              waits=W(vS, sS), inc=ld, inc_n=16)
            A("scalar", lambda e, j=j, bi=bi: e.activation(
                xnb[bi][:], xtb[bi][:], AF.Identity, bias=stat[:, 5 * n + j:5 * n + j + 1],
                scale=stat[:, 4 * n + j:4 * n + j + 1]),
              waits=[(vS, fmark), (ld, ld.n), (tS, tS.n)], inc=sS)
            for cch in range(8):
                pb = cch % 2
                A("tensor", lambda e, cch=cch, pb=pb, bi=bi: e.transpose(
                    trb[pb], xnb[bi][:, 128 * cch:128 * (cch + 1)], sb_ident[:]),
                  waits=W(sS, vS), inc=tS)
                first = True
                for dd in (dst16, dst32):
                    if dd is None:
                        continue
                    A("vector", lambda e, cch=cch, pb=pb, j=j, dd=dd: e.tensor_copy(
                        dd[:, cch, 128 * j:128 * (j + 1)], trb[pb]),
                      waits=(W(tS) if first else []), inc=vS)
                    first = False

    emit_ln_pass(lambda j: xb[128 * j:128 * (j + 1), :], 32, a_stat, a_xt, a_xn,
                 a_sq, [p[:] for p in ps_tr], xnT, None)
    emit_ln_pass(lambda j: xq[128 * j:128 * (j + 1), :], 16, a_stat, a_xt, a_xn,
                 a_sq, [p[:] for p in ps_tr], xnqT16, None)

    if debug_phase == "A":
        d1 = dbg("xnT", [128, 8, S], BF16)
        d2 = dbg("xnqT", [128, 8, TOK], BF16)
        k.wait_at("sync", vS, vS.n)
        A("sync", lambda e: e.dma_start(out=d1[:], in_=xnT[:]), inc=st, inc_n=16)
        A("sync", lambda e: e.dma_start(out=d2[:], in_=xnqT16[:]), inc=st, inc_n=16)
        k.wait_at("sync", st, st.n)
        k.emit()
        return nc, dbg_outs

    k.barrier(drain=[ld])
    sA.close()
    ln_pass = emit_ln_pass

    sB = ExitStack()
    SB_ = lambda n, s, d=F32: sB.enter_context(nc.sbuf_tensor(n, list(s), d))
    PB = lambda n, s, d=F32: sB.enter_context(nc.psum_tensor(n, list(s), d))

    wqh = SB_("b_wqh", [128, 8, DH], BF16)
    wkh = SB_("b_wkh", [128, 8, DH], BF16)
    wvh = SB_("b_wvh", [128, 8, DH], BF16)
    wrh = SB_("b_wrh", [128, 8, DH], BF16)
    ubv = SB_("b_ubv", [DH, 2 * NH])
    qu = SB_("b_qu", [DH, TOK], BF16)
    qv = SB_("b_qv", [DH, TOK], BF16)
    kvT = SB_("b_kvT", [128, S], BF16)
    vtok = SB_("b_vtok", [128, S // 128, 72], BF16)
    rcT = SB_("b_rcT", [DH, S + 256], BF16)
    pech = [SB_(f"b_pech{i}", [128, 8, 256], BF16) for i in range(2)]
    pbuf = [SB_(f"b_pbuf{i}", [128, S + 256], BF16) for i in range(2)]
    bdb = [SB_(f"b_bdb{i}", [128, S + 256], BF16) for i in range(2)]
    asbT = SB_("b_asbT", [128, 8, 256], BF16)
    sb_negt = SB_("b_negt", [128, 128], BF16)
    rrall = SB_("b_rrall", [1, TOK])
    bcball = SB_("b_bcball", [DH, TOK])
    ctxu65 = SB_("b_ctxu65", [DH + 1, TOK])
    ctxh = SB_("b_ctxh", [DH, TOK], BF16)
    sb_spq = SB_("b_spq", [128, 128], BF16)
    sb_pmq = SB_("b_pmq", [128, 128], BF16)

    bigS = [PB(f"b_bigs{i}", [128, 1024]) for i in range(2)]
    bps = [PB("b_ps0", [128, 512])]
    msc = [bigS[1][:, 0:512], bigS[1][:, 512:1024]]
    ps_t = [PB(f"b_pst{i}", [128, 128], BF16) for i in range(2)]
    ps_ctx = PB("b_psctx", [DH + 1, 256])

    A("sync", lambda e: e.dma_start(out=ubv[:, 0:NH], in_=ub_t[:]), inc=ld, inc_n=16)
    A("sync", lambda e: e.dma_start(out=ubv[:, NH:2 * NH], in_=vb_t[:]), inc=ld, inc_n=16)
    A("sync", lambda e: e.dma_start(out=sb_spq[:], in_=spillq[:]), inc=ld, inc_n=16)
    A("sync", lambda e: e.dma_start(out=sb_pmq[:], in_=pmq[:]), inc=ld, inc_n=16)
    A("sync", lambda e: e.dma_start(out=sb_negt[:], in_=negt[:]), inc=ld, inc_n=16)
    for cch in range(S // 128):
        A("vector", lambda e, cch=cch: e.memset(vtok[:, cch, 64:65], 1.0), inc=vS)

    wqkv_r = wqkv.rearrange("(c p) e -> p c e", p=128)
    wr_r = wr.rearrange("(c p) e -> p c e", p=128)

    M = {}
    BM = {}

    def bw(key):
        m = BM.get(key)
        return [m] if m else []

    def mw(key):
        m = M.get(key)
        return [m] if m else []

    nh_b = 0 if debug_phase in ("C0", "C0a", "C0b") else (1 if debug_phase == "B0" else NH)
    for h in range(nh_b):
        A("gpsimd", lambda e, h=h: e.dma_start(
            out=wqh[:], in_=wqkv_r[:, :, DH * h:DH * (h + 1)]),
          waits=mw("proj_done"), inc=gld, inc_n=16)
        A("gpsimd", lambda e, h=h: e.dma_start(
            out=wkh[:], in_=wqkv_r[:, :, D + DH * h:D + DH * (h + 1)]), inc=gld, inc_n=16)
        A("gpsimd", lambda e, h=h: e.dma_start(
            out=wvh[:], in_=wqkv_r[:, :, 2 * D + DH * h:2 * D + DH * (h + 1)]),
          inc=gld, inc_n=16)
        A("gpsimd", lambda e, h=h: e.dma_start(
            out=wrh[:], in_=wr_r[:, :, DH * h:DH * (h + 1)]), inc=gld, inc_n=16)
        w_loaded = (gld, gld.n)

        def bproj(w_sb, src, ntok, evac, free_mark):
            for j in range(ntok // 512):
                pj = j % 2
                for cch in range(8):
                    A("tensor", lambda e, j=j, cch=cch, pj=pj, w_sb=w_sb, src=src:
                      e.matmul(msc[pj][0:64, :], w_sb[:, cch, :],
                               src[:, cch, 512 * j:512 * (j + 1)],
                               start=(cch == 0), stop=(cch == 7)),
                      waits=(bw(("b", pj)) + [w_loaded] + (mw(free_mark) if j < 2 else [])
                             if cch == 0 else []), inc=tS)
                mmk = (tS, tS.n)
                evac(j, msc[pj][0:64, :], mmk)

        def tpos(t):
            return 2 * (15 - t) if t >= 8 else 2 * t + 1

        def evac_q(j, ps, mmk, h=h):
            for c4 in range(4):
                t4 = 4 * j + c4
                A("scalar", lambda e, c4=c4, t4=t4, ps=ps: e.activation(
                    qu[:, 128 * tpos(t4):128 * tpos(t4) + 128],
                    ps[0:64, 128 * c4:128 * (c4 + 1)], AF.Identity,
                    bias=ubv[:, h:h + 1], scale=1.0),
                  waits=([mmk, (ld, 32)] + (mw("qu_free") if j == 0 else [])
                         if c4 == 0 else []), inc=sS)
            A("scalar", lambda e, j=j, ps=ps: e.activation(
                qv[:, 512 * j:512 * (j + 1)], ps, AF.Identity,
                bias=ubv[:, NH + h:NH + h + 1], scale=1.0), inc=sS)
            BM[("b", j % 2)] = (sS, sS.n)
        bproj(wqh, xnqT16, TOK, evac_q, "qu_free")

        def evac_k(j, ps, mmk):
            A("vector", lambda e, j=j, ps=ps: e.tensor_copy(
                kvT[0:64, 512 * j:512 * (j + 1)], ps),
              waits=[mmk] + (mw("kvT_free") if j == 0 else []), inc=vS)
            BM[("b", j % 2)] = (vS, vS.n)
        bproj(wkh, xnT, S, evac_k, "kvT_free")

        def evac_v(j, ps, mmk):
            A("vector", lambda e, j=j, ps=ps: e.tensor_copy(
                kvT[64:128, 512 * j:512 * (j + 1)], ps), waits=[mmk], inc=vS)
            BM[("b", j % 2)] = (vS, vS.n)
        bproj(wvh, xnT, S, evac_v, None)
        kv_ready = (vS, vS.n)

        for cch in range(S // 128):
            pb = cch % 2
            A("tensor", lambda e, cch=cch, pb=pb: e.transpose(
                ps_t[pb][:], kvT[:, 128 * cch:128 * (cch + 1)], sb_ident16[:]),
              waits=[kv_ready] + bw(("t", pb)) + (mw("vtok_free") if cch == 0 else []),
              inc=tS)
            A("vector", lambda e, cch=cch, pb=pb: e.tensor_copy(
                vtok[:, cch, 0:64], ps_t[pb][:, 64:128]),
              waits=[(tS, tS.n)], inc=vS)
            BM[("t", pb)] = (vS, vS.n)

        for wch in range(cdiv(S + 256, 256)):
            cw = min(256, S + 256 - 256 * wch)
            pe_b = wch % 2
            A("sync", lambda e, wch=wch, cw=cw, pe_b=pe_b: e.dma_start(
                out=pech[pe_b][:, :, 0:cw], in_=peT[:, :, 256 * wch:256 * wch + cw]),
              waits=bw(("pe", pe_b)), inc=ld, inc_n=16)
            pk_ld = (ld, ld.n)
            pj = wch % 2
            for cch in range(8):
                A("tensor", lambda e, wch=wch, cw=cw, cch=cch, pj=pj, pe_b=pe_b: e.matmul(
                    msc[pj][0:64, 0:cw], wrh[:, cch, :], pech[pe_b][:, cch, 0:cw],
                    start=(cch == 0), stop=(cch == 7)),
                  waits=([pk_ld] + bw(("b", pj)) + (mw("rcT_free") if wch == 0 else [])
                         if cch == 0 else []), inc=tS)
            BM[("pe", pe_b)] = (tS, tS.n)
            A("vector", lambda e, wch=wch, cw=cw, pj=pj: e.tensor_copy(
                rcT[:, 256 * wch:256 * wch + cw], msc[pj][0:64, 0:cw]),
              waits=[(tS, tS.n)], inc=vS)
            BM[("b", pj)] = (vS, vS.n)
        rc_ready = (vS, vS.n)
        M["proj_done"] = (tS, tS.n)

        def p_stage(t, pp, off, pkey):
            E = 256 * (t + 1)
            W0 = S - 128 - 256 * t
            nw = cdiv(E, 512)
            for w in range(nw):
                cw = min(512, E - 512 * w)
                pj = w % 2
                A("tensor", lambda e, w=w, cw=cw, t=t, W0=W0, pj=pj: e.matmul(
                    msc[pj][:, 0:cw], qv[:, 128 * t:128 * (t + 1)],
                    rcT[:, W0 + 512 * w:W0 + 512 * w + cw], start=True, stop=True),
                  waits=[rc_ready] + bw(("b", pj)), inc=tS)
                mmk = (tS, tS.n)
                A("vector", lambda e, w=w, cw=cw, pp=pp, off=off, pj=pj: e.tensor_copy(
                    pbuf[pp][:, off + 512 * w:off + 512 * w + cw], msc[pj][:, 0:cw]),
                  waits=[mmk] + (mw(("pbuf_free", pp)) if w == 0 else []), inc=vS)
                BM[("b", pj)] = (vS, vS.n)
            A("vector", lambda e, E=E, pp=pp, off=off: e.tensor_tensor(
                pbuf[pp][:, off + E - 128:off + E],
                pbuf[pp][:, off + E - 128:off + E], sb_pmq[:],
                ALU.add), waits=W(ld), inc=vS)
            pev = (vS, vS.n)
            A("sync", lambda e, E=E, pp=pp, off=off: e.dma_start(
                out=AP(pskew[pp].tensor, 128 * off, [[E, 128], [1, E]]),
                in_=pbuf[pp][:, off:off + E]),
              waits=[pev] + mw(("skew_done", pp)), inc=pS, inc_n=16)
            pok = (pS, pS.n)
            A("gpsimd", lambda e, E=E, pp=pp, off=off: e.dma_start(
                out=bdb[pp][:, off:off + E],
                in_=AP(pskew[pp].tensor, 128 * off + 127, [[E - 1, 128], [1, E]])),
              waits=[pok] + mw(("bd_used", pp)), inc=bS, inc_n=16)
            A("vector", lambda e, E=E, pp=pp, off=off: e.tensor_tensor(
                bdb[pp][:, off + E - 128:off + E],
                bdb[pp][:, off + E - 128:off + E], sb_spq[:],
                ALU.add), waits=[(bS, bS.n)], inc=vS)
            M[("bd_ready", pkey)] = (vS, vS.n)
            M[("skew_done", pp)] = (bS, bS.n)
            if off == 0:
                M[("pbuf_free", pp)] = pok

        def s_pair(u, pos):
            pp = pos % 2
            tl, ts = 15 - u, u
            El, Es = 256 * (tl + 1), 256 * (ts + 1)
            ncc, ncs = El // 128, Es // 128
            psS = (bigS[0][:, 0:512], bigS[0][:, 512:1024], bps[0][:, 0:512])
            ng = cdiv(ncc, 2)

            def avs(g):
                ncg = min(2, ncc - 2 * g)
                for c2 in range(ncg):
                    kc = 2 * g + c2
                    A("tensor", lambda e, kc=kc, ncc=ncc: e.matmul(
                        ps_ctx[:], vtok[:, kc, 0:65], asbT[:, kc % 8, :],
                        start=(kc == 0), stop=(kc == ncc - 1)),
                      waits=([M[("exp", u, g)]] if c2 == 0 else [])
                            + (mw("ctx_free") if kc == 0 else []), inc=tS)
                    M[("asbs", kc % 8)] = (tS, tS.n)

            prevg = None
            for g in range(ng):
                ncg = min(2, ncc - 2 * g)
                bank = psS[g % 3]
                for c2 in range(ncg):
                    kc = 2 * g + c2
                    A("tensor", lambda e, kc=kc, c2=c2, bank=bank, u=u: e.matmul(
                        bank[:, 256 * c2:256 * c2 + 256],
                        kvT[0:64, 128 * kc:128 * (kc + 1)],
                        qu[:, 256 * u:256 * u + 256],
                        start=True, stop=False),
                      waits=((bw(("G", g % 3))
                              + ([M[("bd_ready", (u, 1))]] if g == 0 else []))
                             if c2 == 0 else []), inc=tS)
                    A("tensor", lambda e, kc=kc, c2=c2, bank=bank, pp=pp: e.matmul(
                        bank[:, 256 * c2:256 * c2 + 128],
                        bdb[pp][:, 128 * kc:128 * (kc + 1)], sb_ident16[:],
                        start=False, stop=False), inc=tS)
                    if kc < ncs:
                        A("tensor", lambda e, kc=kc, c2=c2, bank=bank, pp=pp, El=El:
                          e.matmul(
                            bank[:, 256 * c2 + 128:256 * c2 + 256],
                            bdb[pp][:, El + 128 * kc:El + 128 * (kc + 1)],
                            sb_ident16[:], start=False, stop=True), inc=tS)
                    else:
                        A("tensor", lambda e, kc=kc, c2=c2, bank=bank: e.matmul(
                            bank[:, 256 * c2 + 128:256 * c2 + 256],
                            sb_negt[:], sb_ident16[:],
                            start=False, stop=True), inc=tS)
                ek = (tS, tS.n)
                sl = (2 * g) % 8
                A("scalar", lambda e, g=g, ncg=ncg, sl=sl, bank=bank: e.activation(
                    asbT[:, sl:sl + ncg, :], bank[:, 0:256 * ncg], AF.Exp,
                    scale=0.125),
                  waits=[ek] + mw(("asbs", sl)) + mw(("asbs", sl + 1)), inc=sS)
                M[("exp", u, g)] = (sS, sS.n)
                BM[("G", g % 3)] = (sS, sS.n)
                if prevg is not None:
                    avs(prevg)
                prevg = g
            avs(prevg)
            avk = (tS, tS.n)
            M[("bd_used", pp)] = avk
            A("vector", lambda e, tl=tl: e.tensor_copy(
                ctxu65[:, 128 * tl:128 * (tl + 1)], ps_ctx[:, 0:128]),
              waits=[avk], inc=vS)
            A("vector", lambda e, ts=ts: e.tensor_copy(
                ctxu65[:, 128 * ts:128 * (ts + 1)], ps_ctx[:, 128:256]), inc=vS)
            M["ctx_free"] = (vS, vS.n)

        p_stage(15, 0, 0, (0, 0))
        p_stage(0, 0, 256 * 16, (0, 1))
        for u in range(1, 8):
            p_stage(15 - u, u % 2, 0, (u, 0))
            p_stage(u, u % 2, 256 * (16 - u), (u, 1))
            s_pair(u - 1, u - 1)
        s_pair(7, 7)

        fv()
        A("vector", lambda e: e.tensor_copy(rrall[:], ctxu65[64:65, :]), inc=vS)
        fv()
        fv()
        A("vector", lambda e: e.reciprocal(rrall[:], rrall[:]), inc=vS)
        fv()
        fv()
        rr_ready = (vS, vS.n)
        for w in range(4):
            pj = w % 2
            A("tensor", lambda e, w=w, pj=pj: e.matmul(
                msc[pj][0:64, 0:512], sb_ones_r[0:1, 0:64],
                rrall[0:1, 512 * w:512 * (w + 1)], start=True, stop=True),
              waits=[rr_ready] + bw(("b", pj)), inc=tS)
            A("vector", lambda e, w=w, pj=pj: e.tensor_copy(
                bcball[:, 512 * w:512 * (w + 1)], msc[pj][0:64, 0:512]),
              waits=[(tS, tS.n)], inc=vS)
            BM[("b", pj)] = (vS, vS.n)
        A("vector", lambda e: e.tensor_tensor(ctxh[:], ctxu65[0:64, :], bcball[:],
                                              ALU.mult), inc=vS)
        A("sync", lambda e, h=h: e.dma_start(
            out=ctx_park[64 * (h % 2):64 * (h % 2) + 64, h // 2, :], in_=ctxh[:]),
          waits=[(vS, vS.n)], inc=ld, inc_n=16)
        M["ctx_free"] = (ld, ld.n)
        M["qu_free"] = (tS, tS.n)
        M["kvT_free"] = (tS, tS.n)
        M["vtok_free"] = (tS, tS.n)
        M["rcT_free"] = (tS, tS.n)

    if debug_phase == "B0":
        d1 = dbg("ctxu", [DH + 1, TOK])
        d2 = dbg("rs", [1, TOK])
        k.wait_at("sync", vS, vS.n)
        k.wait_at("sync", bS, bS.n)
        A("sync", lambda e: e.dma_start(out=d1[:], in_=ctxu65[:]), inc=st, inc_n=16)
        A("sync", lambda e: e.dma_start(out=d2[:], in_=rrall[:]), inc=st, inc_n=16)
        k.wait_at("sync", st, st.n)
        k.emit()
        return nc, dbg_outs

    if debug_phase == "B":
        d1 = dbg("ctx", [128, 8, TOK], BF16)
        k.wait_at("sync", ld, ld.n)
        A("sync", lambda e: e.dma_start(out=d1[:], in_=ctx_park[:]), inc=st, inc_n=16)
        k.wait_at("sync", st, st.n)
        k.emit()
        return nc, dbg_outs


    k.barrier(drain=[ld, pS, bS, gld])
    sB.close()

    sC = ExitStack()
    SC = lambda n, s, d=F32: sC.enter_context(nc.sbuf_tensor(n, list(s), d))
    PC = lambda n, s, d=F32: sC.enter_context(nc.psum_tensor(n, list(s), d))

    rwq = SC("c_rwq", [128, 8, DH], BF16)
    rwk = SC("c_rwk", [128, 8, DH], BF16)
    rwv = SC("c_rwv", [128, 8, DH], BF16)
    gqh = SC("c_gqh", [64, 512])
    gkh = SC("c_gkh", [64, 512])
    qgT = SC("c_qgT", [DH, S], BF16)
    rkvT = SC("c_rkvT", [128, S], BF16)
    kvtok = SC("c_kvtok", [128, S // 128, 128], BF16)
    stT = SC("c_stT", [64, 64])
    stT16 = SC("c_stT16", [64, 64], BF16)
    sc16 = [SC(f"c_sc16{i}", [128, 128], BF16) for i in range(2)]
    atr = [SC(f"c_atr{i}", [128, 128], BF16) for i in range(2)]
    ob = [SC(f"c_ob{i}", [64, TOK]) for i in range(2)]
    stats = SC("c_stats", [64, 64])
    sqsc = SC("c_sqsc", [64, 128])
    mrow = SC("c_mrow", [1, 8])
    rncol = SC("c_rncol", [64, 2])
    scbi = SC("c_scbi", [64, 2])
    gw = SC("c_gw", [64, 2 * NH])
    cnh = SC("c_cnh", [64, TOK], BF16)

    cps = [PC(f"c_ps{i}", [128, 512]) for i in range(2)]
    ctk = [PC(f"c_ptk{i}", [128, 128], BF16) for i in range(2)]
    cpo = [PC(f"c_po{i}", [64, 128]) for i in range(2)]
    cpu = [PC(f"c_pu{i}", [64, 128]) for i in range(2)]
    ps_u = cpu[0]

    A("sync", lambda e: e.dma_start(out=gw[:, 0:NH], in_=gnw_t[:]), inc=ld, inc_n=16)
    A("sync", lambda e: e.dma_start(out=gw[:, NH:2 * NH], in_=gnb_t[:]), inc=ld, inc_n=16)

    CM = {}

    def cmw(key):
        m = CM.get(key)
        return [m] if m else []

    def projc(w_sb, src, ntok, evac, wk_ld):
        for j in range(ntok // 512):
            pj = j % 2
            for cch in range(8):
                A("tensor", lambda e, j=j, cch=cch, pj=pj: e.matmul(
                    cps[pj][0:64, :], w_sb[:, cch, :], src[:, cch, 512 * j:512 * (j + 1)],
                    start=(cch == 0), stop=(cch == 7)),
                  waits=([wk_ld] + cmw(("cb", pj)) if cch == 0 else []), inc=tS)
            evac(j, cps[pj][0:64, :], (tS, tS.n))
            CM[("cb", pj)] = (vS, vS.n)

    c_dbg = debug_phase in ("C0", "C0a", "C0b")
    for h in range(1 if c_dbg else NH):
        g = GAMMAS[h]
        gC = g ** 128
        A("gpsimd", lambda e, h=h: e.dma_start(
            out=rwq[:], in_=wq_a[h].rearrange("(c p) e -> p c e", p=128)),
          waits=cmw("w_free"), inc=gld, inc_n=16)
        A("gpsimd", lambda e, h=h: e.dma_start(
            out=rwk[:], in_=wk_a[h].rearrange("(c p) e -> p c e", p=128)), inc=gld, inc_n=16)
        A("gpsimd", lambda e, h=h: e.dma_start(
            out=rwv[:], in_=wv_a[h].rearrange("(c p) e -> p c e", p=128)), inc=gld, inc_n=16)
        cw_ld = (gld, gld.n)
        A("sync", lambda e, h=h: e.dma_start(out=gqh[:], in_=gq_t[:, h, :]),
          waits=cmw("g_free"), inc=ld, inc_n=16)
        A("sync", lambda e, h=h: e.dma_start(out=gkh[:], in_=gk_t[:, h, :]), inc=ld, inc_n=16)
        g_ld = (ld, ld.n)

        def evac_qg(j, ps, mmk):
            A("vector", lambda e, j=j, ps=ps: e.tensor_tensor(
                qgT[:, 512 * j:512 * (j + 1)], ps, gqh[:], ALU.mult),
              waits=[mmk, g_ld] + (cmw("qg_free") if j == 0 else []), inc=vS)
        projc(rwq, xnT, S, evac_qg, cw_ld)

        def evac_kg(j, ps, mmk):
            A("vector", lambda e, j=j, ps=ps: e.tensor_tensor(
                rkvT[0:64, 512 * j:512 * (j + 1)], ps, gkh[:], ALU.mult),
              waits=[mmk] + (cmw("kv_free") if j == 0 else []), inc=vS)
        projc(rwk, xnT, S, evac_kg, cw_ld)

        def evac_rv(j, ps, mmk):
            A("vector", lambda e, j=j, ps=ps: e.tensor_copy(
                rkvT[64:128, 512 * j:512 * (j + 1)], ps), waits=[mmk], inc=vS)
        projc(rwv, xnT, S, evac_rv, cw_ld)
        qgkv_ready = (vS, vS.n)
        CM["w_free"] = (tS, tS.n)

        A("vector", lambda e: e.memset(stT[:], 0.0), waits=cmw("st_free"), inc=vS)
        A("vector", lambda e: e.memset(stT16[:], 0.0), inc=vS)
        st_ready = (vS, vS.n)

        for j in range(0 if debug_phase == "C0a" else S // 128):
            jb = j % 2
            A("tensor", lambda e, j=j, jb=jb: e.transpose(
                ctk[jb][:], rkvT[:, 128 * j:128 * (j + 1)], sb_ident16[:]),
              waits=[qgkv_ready] + cmw(("tk", jb)), inc=tS)
            A("vector", lambda e, j=j, jb=jb: e.tensor_copy(kvtok[:, j, :], ctk[jb][:]),
              waits=[(tS, tS.n)], inc=vS)
            A("vector", lambda e, j=j, gC=gC: e.tensor_scalar_mul(
                kvtok[:, j, 0:64], kvtok[:, j, 0:64], gC), inc=vS)
            kvtok_k = (vS, vS.n)
            A("tensor", lambda e, j=j, jb=jb: e.matmul(
                cps[jb][0:128, 0:128], qgT[:, 128 * j:128 * (j + 1)],
                rkvT[0:64, 128 * j:128 * (j + 1)],
                start=True, stop=True), waits=[qgkv_ready] + cmw(("cb", jb)), inc=tS)
            A("vector", lambda e, jb=jb: e.tensor_tensor(
                sc16[jb][:], cps[jb][0:128, 0:128], sb_rmask[:], ALU.mult),
              waits=[(tS, tS.n)] + cmw(("sc16u", jb)), inc=vS)
            CM[("cb", jb)] = (vS, vS.n)
            A("tensor", lambda e, jb=jb: e.transpose(ctk[jb][:], sc16[jb][:],
                                                     sb_ident16[:]),
              waits=[(vS, vS.n)], inc=tS)
            CM[("sc16u", jb)] = (tS, tS.n)
            A("vector", lambda e, jb=jb: e.tensor_copy(atr[jb][:], ctk[jb][:]),
              waits=[(tS, tS.n)] + cmw(("atru", jb)), inc=vS)
            CM[("tk", jb)] = (vS, vS.n)
            atr_k = (vS, vS.n)
            A("tensor", lambda e, j=j, jb=jb: e.matmul(
                cpo[jb][:, 0:128], kvtok[:, j, 64:128], atr[jb][:],
                start=True, stop=False),
              waits=[atr_k, kvtok_k if j == 0 else st_ready] + cmw(("ob", jb)), inc=tS)
            A("tensor", lambda e, j=j, jb=jb: e.matmul(
                cpo[jb][:, 0:128], stT16[:], qgT[:, 128 * j:128 * (j + 1)],
                start=False, stop=True), inc=tS)
            CM[("atru", jb)] = (tS, tS.n)
            ok_ = (tS, tS.n)
            A("vector", lambda e, j=j, jb=jb: e.tensor_copy(
                ob[jb][:, 128 * (j // 2):128 * (j // 2 + 1)], cpo[jb][:, 0:128]),
              waits=[ok_], inc=vS)
            CM[("ob", jb)] = (vS, vS.n)
            A("vector", lambda e, j=j, jb=jb: e.reduce_sum(
                stats[:, j:j + 1], ob[jb][:, 128 * (j // 2):128 * (j // 2 + 1)],
                axis=AX.X), inc=vS)
            A("scalar", lambda e, j=j, jb=jb: e.activation(
                sqsc[:], ob[jb][:, 128 * (j // 2):128 * (j // 2 + 1)], AF.Square,
                accum_out=stats[:, 32 + j:33 + j]), waits=[(vS, vS.n)], inc=sS)
            A("tensor", lambda e, j=j, jb=jb: e.matmul(
                cpu[jb][:, 0:64], kvtok[:, j, 0:64], kvtok[:, j, 64:128],
                start=True, stop=True),
              waits=[kvtok_k] + cmw(("ub", jb)), inc=tS)
            upd_k = (tS, tS.n)
            A("vector", lambda e, gC=gC, jb=jb: e.scalar_tensor_tensor(
                stT16[:], stT[:], gC, cpu[jb][:, 0:64], ALU.mult, ALU.add),
              waits=[upd_k, st_ready], inc=vS)
            A("vector", lambda e, gC=gC, jb=jb: e.scalar_tensor_tensor(
                stT[:], stT[:], gC, cpu[jb][:, 0:64], ALU.mult, ALU.add), inc=vS)
            CM[("ub", jb)] = (vS, vS.n)
            st_ready = (vS, vS.n)

        if debug_phase in ("C0a", "C0b"):
            continue
        fs()
        A("vector", lambda e: e.reduce_sum(rncol[:, 0:1], stats[:, 0:32], axis=AX.X),
          waits=W(sS), inc=vS)
        A("vector", lambda e: e.reduce_sum(rncol[:, 1:2], stats[:, 32:64], axis=AX.X),
          inc=vS)
        fv()
        A("tensor", lambda e: e.matmul(ps_u[0:1, 64:66], sb_ones_c[0:64, 0:1],
                                       rncol[:, 0:2], start=True, stop=True),
          waits=W(vS), inc=tS)
        A("vector", lambda e: e.tensor_copy(mrow[0:1, 0:2], ps_u[0:1, 64:66]),
          waits=W(tS), inc=vS)
        fv()
        A("vector", lambda e: e.tensor_scalar_mul(mrow[0:1, 2:4], mrow[0:1, 0:2],
                                                  1.0 / (64.0 * S)), inc=vS)
        fv()
        A("vector", lambda e: e.tensor_tensor(mrow[0:1, 4:5], mrow[0:1, 2:3],
                                              mrow[0:1, 2:3], ALU.mult), inc=vS)
        fv()
        A("vector", lambda e: e.tensor_sub(mrow[0:1, 5:6], mrow[0:1, 3:4],
                                           mrow[0:1, 4:5]), inc=vS)
        fv()
        A("vector", lambda e: e.tensor_scalar_add(mrow[0:1, 5:6], mrow[0:1, 5:6], EPS),
          inc=vS)
        fv()
        A("scalar", lambda e: e.activation(mrow[0:1, 6:7], mrow[0:1, 5:6], AF.Sqrt),
          waits=W(vS), inc=sS)
        fs()
        A("vector", lambda e: e.reciprocal(mrow[0:1, 7:8], mrow[0:1, 6:7]),
          waits=W(sS), inc=vS)
        fv()
        A("vector", lambda e: e.tensor_scalar_mul(mrow[0:1, 2:3], mrow[0:1, 2:3], -1.0),
          inc=vS)
        A("vector", lambda e: e.tensor_copy(mrow[0:1, 3:4], mrow[0:1, 7:8]), inc=vS)
        fv()
        A("tensor", lambda e: e.matmul(ps_u[0:64, 66:68], sb_ones_r[0:1, 0:64],
                                       mrow[0:1, 2:4], start=True, stop=True),
          waits=W(vS), inc=tS)
        A("vector", lambda e: e.tensor_copy(rncol[:], ps_u[0:64, 66:68]),
          waits=W(tS), inc=vS)
        fv()
        A("vector", lambda e, h=h: e.tensor_tensor(scbi[:, 0:1], gw[:, h:h + 1],
                                                   rncol[:, 1:2], ALU.mult), inc=vS)
        fv()
        A("vector", lambda e, h=h: e.scalar_tensor_tensor(
            scbi[:, 1:2], scbi[:, 0:1], rncol[:, 0:1], gw[:, NH + h:NH + h + 1],
            ALU.mult, ALU.add), inc=vS)
        A("vector", lambda e: e.tensor_scalar(ob[0][:], ob[0][:], sb_sel[:, 0:1], None,
                                              ALU.mult), inc=vS)
        A("vector", lambda e: e.scalar_tensor_tensor(ob[0][:], ob[1][:], sb_sel[:, 1:2],
                                                     ob[0][:], ALU.mult, ALU.add), inc=vS)
        fv()
        A("scalar", lambda e: e.activation(cnh[:], ob[0][:], AF.Identity,
                                           bias=scbi[:, 1:2], scale=scbi[:, 0:1]),
          waits=W(vS, ld), inc=sS)
        A("sync", lambda e, h=h: e.dma_start(
            out=cn_park[64 * (h % 2):64 * (h % 2) + 64, h // 2, :], in_=cnh[:]),
          waits=W(sS), inc=ld, inc_n=16)
        CM["g_free"] = (vS, vS.n)
        CM["qg_free"] = (tS, tS.n)
        CM["kv_free"] = (tS, tS.n)
        CM["st_free"] = (vS, vS.n)

    if debug_phase in ("C0a", "C0b"):
        d1 = dbg("qg", [DH, S], BF16)
        d2 = dbg("rkv", [128, S], BF16)
        k.wait_at("sync", vS, vS.n)
        A("sync", lambda e: e.dma_start(out=d1[:], in_=qgT[:]), inc=st, inc_n=16)
        A("sync", lambda e: e.dma_start(out=d2[:], in_=rkvT[:]), inc=st, inc_n=16)
        if debug_phase == "C0b":
            d3 = dbg("ob0", [DH, TOK])
            d4 = dbg("ob1", [DH, TOK])
            d5 = dbg("stats", [DH, 64])
            A("sync", lambda e: e.dma_start(out=d3[:], in_=ob[0][:]), inc=st, inc_n=16)
            A("sync", lambda e: e.dma_start(out=d4[:], in_=ob[1][:]), inc=st, inc_n=16)
            A("sync", lambda e: e.dma_start(out=d5[:], in_=stats[:]), inc=st, inc_n=16)
        k.wait_at("sync", st, st.n)
        k.emit()
        return nc, dbg_outs

    if debug_phase in ("C", "C0"):
        dm = dbg("mrow", [1, 8])
        dsc = dbg("scbi", [DH, 2])
        k.wait_at("sync", vS, vS.n)
        A("sync", lambda e: e.dma_start(out=dm[:], in_=mrow[:]), inc=st, inc_n=16)
        A("sync", lambda e: e.dma_start(out=dsc[:], in_=scbi[:]), inc=st, inc_n=16)
        d1 = dbg("cn", [128, 8, TOK], BF16)
        k.wait_at("sync", ld, ld.n)
        A("sync", lambda e: e.dma_start(out=d1[:], in_=cn_park[:]), inc=st, inc_n=16)
        k.wait_at("sync", st, st.n)
        k.emit()
        return nc, dbg_outs

    k.barrier(drain=[ld, gld])
    sC.close()
    sAC.close()

    sEF = ExitStack()
    PSB = [sEF.enter_context(nc.psum_tensor(f"e_psb{i}", [128, 512], F32))
           for i in range(8)]

    def mk_lnbufs(stack, tag):
        SE_ = lambda n, s, d=F32: stack.enter_context(nc.sbuf_tensor(n, list(s), d))
        return dict(
            f32t=SE_(f"{tag}_f32t", [128, 8, 512]),
            sqch=[SE_(f"{tag}_sqch{i}", [128, 512]) for i in range(2)],
            lrow=SE_(f"{tag}_lrow", [1, 8, 512]),
            bcn=SE_(f"{tag}_bcn", [128, 512]),
            lntmp=SE_(f"{tag}_lntmp", [128, 512]),
        )

    def ln_feature(lb, src_fn, j, ln_idx, out16_fn, park32_ap, park16_ap, park16_src):
        f32t, sqch, lrow, bcn, lntmp = (lb["f32t"], lb["sqch"], lb["lrow"],
                                        lb["bcn"], lb["lntmp"])
        for cch in range(8):
            A("tensor", lambda e, cch=cch: e.matmul(
                PSB[0][0:1, :], sb_ones_c[:], src_fn(cch),
                start=(cch == 0), stop=(cch == 7)),
              waits=(W(vS, sS, ld) if cch == 0 else []), inc=tS)
        for cch in range(8):
            pb = cch % 2
            A("scalar", lambda e, cch=cch, pb=pb: e.activation(
                sqch[pb][:], src_fn(cch), AF.Square), waits=W(vS, tS, ld), inc=sS)
            A("tensor", lambda e, cch=cch, pb=pb: e.matmul(
                PSB[1][0:1, :], sb_ones_c[:], sqch[pb][:],
                start=(cch == 0), stop=(cch == 7)), waits=W(sS), inc=tS)
        A("vector", lambda e: e.tensor_scalar_mul(lrow[0:1, 0, :], PSB[0][0:1, :], 1.0 / D),
          waits=W(tS), inc=vS)
        A("vector", lambda e: e.tensor_scalar(lrow[0:1, 1, :], PSB[1][0:1, :], 1.0 / D,
                                              EPS, ALU.mult, ALU.add), inc=vS)
        A("vector", lambda e: e.tensor_tensor(lrow[0:1, 2, :], lrow[0:1, 0, :],
                                              lrow[0:1, 0, :], ALU.mult), inc=vS)
        fv()
        fv()
        A("vector", lambda e: e.tensor_sub(lrow[0:1, 5, :], lrow[0:1, 1, :],
                                           lrow[0:1, 2, :]), inc=vS)
        fv()
        fv()
        A("scalar", lambda e: e.activation(lrow[0:1, 3, :], lrow[0:1, 5, :], AF.Sqrt),
          waits=W(vS), inc=sS)
        fs()
        fs()
        A("vector", lambda e: e.reciprocal(lrow[0:1, 3, :], lrow[0:1, 3, :]),
          waits=W(sS), inc=vS)
        fv()
        fv()
        A("vector", lambda e: e.scalar_tensor_tensor(
            lrow[0:1, 4, :], lrow[0:1, 0, :], -1.0, lrow[0:1, 3, :],
            ALU.mult, ALU.mult), inc=vS)
        fv()
        fv()
        A("tensor", lambda e: e.matmul(PSB[0][:], sb_ones_r[:], lrow[0:1, 3, :],
                                       start=True, stop=True), waits=W(vS), inc=tS)
        A("tensor", lambda e: e.matmul(PSB[1][:], sb_ones_r[:], lrow[0:1, 4, :],
                                       start=True, stop=True), inc=tS)
        A("vector", lambda e: e.tensor_copy(bcn[:], PSB[1][:]), waits=W(tS), inc=vS)
        for cch in range(8):
            A("vector", lambda e, cch=cch: e.tensor_tensor(
                lntmp[:], src_fn(cch), PSB[0][:], ALU.mult), waits=W(sS, ld), inc=vS)
            A("vector", lambda e: e.tensor_add(lntmp[:], lntmp[:], bcn[:]), inc=vS)
            A("scalar", lambda e, cch=cch: e.activation(
                f32t[:, cch, :], lntmp[:], AF.Identity,
                bias=sb_lnb[:, ln_idx, cch:cch + 1], scale=sb_lng[:, ln_idx, cch:cch + 1]),
              waits=W(vS, ld, tS), inc=sS)
            if out16_fn is not None:
                A("vector", lambda e, cch=cch: e.tensor_copy(
                    out16_fn(cch), f32t[:, cch, :]), waits=W(sS, ld), inc=vS)
        if park32_ap is not None:
            A("sync", lambda e: e.dma_start(
                out=park32_ap[:, :, 512 * j:512 * (j + 1)], in_=f32t[:]),
              waits=W(sS, vS), inc=ld, inc_n=16)
        if park16_ap is not None:
            A("sync", lambda e: e.dma_start(
                out=park16_ap[:, :, 512 * j:512 * (j + 1)], in_=park16_src),
              waits=W(vS), inc=ld, inc_n=16)

    we1b = scr("we1b", [NE, 128, 8, FFD], BF16)
    we2b = scr("we2b", [NE, 128, 32, D], BF16)
    for ee in range(NE):
        A("gpsimd", lambda e, ee=ee: e.dma_start(
            out=we1b[ee], in_=we1[ee].rearrange("(c p) f -> p c f", p=128)),
          inc=gld, inc_n=16)
        A("gpsimd", lambda e, ee=ee: e.dma_start(
            out=we2b[ee], in_=we2[ee].rearrange("(c p) d2 -> p c d2", p=128)),
          inc=gld, inc_n=16)
    wpark_k = (gld, gld.n)

    sE25 = ExitStack()
    resf = sE25.enter_context(nc.sbuf_tensor("e_resf", [128, 8, TOK], F32))
    sE12 = ExitStack()
    SE1 = lambda n, s, d=F32: sE12.enter_context(nc.sbuf_tensor(n, list(s), d))
    xnqT32 = SE1("e_xnq32", [128, 8, TOK])
    ext = [SE1(f"e_xt{i}", [128, D]) for i in range(2)]
    exc = [SE1(f"e_xc{i}", [128, D]) for i in range(2)]
    esq = SE1("e_sqs", [128, D])
    e_stat = SE1("e_stat", [128, 8 * 16])

    ln_pass(lambda j: xq[128 * j:128 * (j + 1), :], 16, e_stat, ext, exc, esq,
            [PSB[0][:, 0:128], PSB[1][:, 0:128]], None, xnqT32)

    wo_sb = SE1("e_wo", [128, 8, D], BF16)
    ctxch = [SE1(f"e_ctxch{i}", [128, 8, 512], BF16) for i in range(2)]
    wo_r = wo.rearrange("(c p) e -> p c e", p=128)
    A("gpsimd", lambda e: e.dma_start(out=wo_sb[:], in_=wo_r[:]), inc=gld, inc_n=16)

    EM = {}

    def emw(key):
        m = EM.get(key)
        return [m] if m else []

    wo_ld = (gld, gld.n)
    for j in range(TOK // 512):
        A("sync", lambda e, j=j: e.dma_start(
            out=ctxch[j % 2][:], in_=ctx_park[:, :, 512 * j:512 * (j + 1)]),
          waits=emw(("ctxch", j % 2)), inc=ld, inc_n=16)
        ctx_ld = (ld, ld.n)
        for m in range(8):
            pm = 2 + (m % 2)
            for cch in range(8):
                A("tensor", lambda e, j=j, m=m, cch=cch, pm=pm: e.matmul(
                    PSB[pm][:], wo_sb[:, cch, 128 * m:128 * (m + 1)],
                    ctxch[j % 2][:, cch, :], start=(cch == 0), stop=(cch == 7)),
                  waits=([ctx_ld, wo_ld, (vS, vS.n)] if (j + m == 0) else
                         ([ctx_ld] + emw(("eb", pm)) if cch == 0 else [])), inc=tS)
            mmk = (tS, tS.n)
            if m == 7:
                EM[("ctxch", j % 2)] = mmk
            A("vector", lambda e, j=j, m=m, pm=pm: e.tensor_tensor(
                resf[:, m, 512 * j:512 * (j + 1)], PSB[pm][:],
                xnqT32[:, m, 512 * j:512 * (j + 1)], ALU.add), waits=[mmk], inc=vS)
            EM[("eb", pm)] = (vS, vS.n)

    k.barrier(drain=[ld, gld])
    sE12.close()

    sE35 = ExitStack()
    SE3 = lambda n, s, d=F32: sE35.enter_context(nc.sbuf_tensor(n, list(s), d))
    h1_16 = SE3("e_h116", [128, 8, TOK], BF16)
    lnA = mk_lnbufs(sE35, "lnA")
    for j in range(TOK // 512):
        ln_feature(lnA, lambda cch, j=j: resf[:, cch, 512 * j:512 * (j + 1)], j, 0,
                   lambda cch, j=j: h1_16[:, cch, 512 * j:512 * (j + 1)],
                   h1_park, None, None)

    if debug_phase == "E3":
        d1 = dbg("h1", [128, 8, TOK], BF16)
        k.wait_at("sync", vS, vS.n)
        A("sync", lambda e: e.dma_start(out=d1[:], in_=h1_16[:]), inc=st, inc_n=16)
        k.wait_at("sync", st, st.n)
        k.emit()
        return nc, dbg_outs

    sE4 = ExitStack()
    SE4 = lambda n, s, d=F32: sE4.enter_context(nc.sbuf_tensor(n, list(s), d))
    hid = SE4("e_hid", [128, 32, 512], BF16)
    wsl = [SE4(f"e_wsl{i}", [128, 8, 128], BF16) for i in range(2)]
    w2sl = [SE4(f"e_w2sl{i}", [128, D], BF16) for i in range(2)]
    h1c = SE4("e_h1c", [128, 8, 512])
    bffs = SE4("e_bffs", [128, 40])
    A("sync", lambda e: e.dma_start(out=bffs[:, 0:32], in_=bff1_c[:]), inc=ld, inc_n=16)
    A("sync", lambda e: e.dma_start(out=bffs[:, 32:40], in_=bff2_c[:]), inc=ld, inc_n=16)
    wff1_r = wff1.rearrange("(c p) f -> p c f", p=128)

    bff_ld = None
    for j in range(TOK // 512):
        A("sync", lambda e, j=j: e.dma_start(
            out=h1c[:], in_=h1_park[:, :, 512 * j:512 * (j + 1)]),
          waits=emw("h1c_used"), inc=ld, inc_n=16)
        h1c_ld = (ld, ld.n)
        if bff_ld is None:
            bff_ld = h1c_ld
        for fc in range(32):
            wb = fc % 2
            A("gpsimd", lambda e, fc=fc, wb=wb: e.dma_start(
                out=wsl[wb][:], in_=wff1_r[:, :, 128 * fc:128 * (fc + 1)]),
              waits=emw(("wsl", wb)), inc=gld, inc_n=16)
            sl_ld = (gld, gld.n)
            pf = fc % 4
            for cch in range(8):
                A("tensor", lambda e, fc=fc, cch=cch, pf=pf, wb=wb, j=j: e.matmul(
                    PSB[pf][:], wsl[wb][:, cch, :], h1_16[:, cch, 512 * j:512 * (j + 1)],
                    start=(cch == 0), stop=(cch == 7)),
                  waits=(([sl_ld] + emw(("fb", pf)) +
                          (emw(("res", pf)) if fc < 4 else []) +
                          ([(vS, vS.n), (sS, sS.n)] if (j + fc == 0) else []))
                         if cch == 0 else []), inc=tS)
            EM[("wsl", wb)] = (tS, tS.n)
            A("scalar", lambda e, fc=fc, pf=pf: e.activation(
                hid[:, fc, :], PSB[pf][:], AF.Relu, bias=bffs[:, fc:fc + 1], scale=1.0),
              waits=[(tS, tS.n), bff_ld] + emw(("hid_used", fc)), inc=sS)
            EM[("fb", pf)] = (sS, sS.n)
        for fc in range(32):
            wb = fc % 2
            A("gpsimd", lambda e, fc=fc, wb=wb: e.dma_start(
                out=w2sl[wb][:], in_=wff2[128 * fc:128 * (fc + 1), :]),
              waits=emw(("w2sl", wb)), inc=gld, inc_n=16)
            sl_ld = (gld, gld.n)
            relu_k = EM[("fb", fc % 2)] if fc >= 30 else None
            for m in range(8):
                A("tensor", lambda e, fc=fc, m=m, wb=wb: e.matmul(
                    PSB[m][:], w2sl[wb][:, 128 * m:128 * (m + 1)], hid[:, fc, :],
                    start=(fc == 0), stop=(fc == 31)),
                  waits=([sl_ld] + ([(sS, sS.n)] if fc == 0 else []) +
                         (emw(("res", m)) if fc == 0 else [])
                         if (m == 0 or fc == 0) else []), inc=tS)
            EM[("w2sl", wb)] = (tS, tS.n)
            EM[("hid_used", fc)] = (tS, tS.n)
        ff2k = (tS, tS.n)
        for m in range(8):
            A("vector", lambda e, j=j, m=m: e.scalar_tensor_tensor(
                resf[:, m, 512 * j:512 * (j + 1)], PSB[m][:], bffs[:, 32 + m:33 + m],
                h1c[:, m, :], ALU.add, ALU.add),
              waits=([ff2k] if m == 0 else []), inc=vS)
            EM[("res", m)] = (vS, vS.n)
        EM["h1c_used"] = (vS, vS.n)

    k.barrier(drain=[ld, gld])
    sE4.close()

    if debug_phase == "E4":
        d1 = dbg("h2raw", [128, 8, TOK])
        k.wait_at("sync", vS, vS.n)
        A("sync", lambda e: e.dma_start(out=d1[:], in_=resf[:]), inc=st, inc_n=16)
        k.wait_at("sync", st, st.n)
        k.emit()
        return nc, dbg_outs

    mgm16_park = scr("mgm16_park", [128, 16, NE], BF16)
    mgm32_park = scr("mgm32_park", [128, 16, NE], F32)
    gatec_park = scr("gatec_park", [128, 16], F32)
    h2tb = SE3("e_h2tb", [128, 8, 512], BF16)
    wg_sb = SE3("e_wg", [128, 8, NE])
    lgs = SE3("e_lgs", [8, 512])
    lgtok = SE3("e_lgtok", [128, 16, NE])
    gmx = SE3("e_gmx", [128, 16, 3])
    getm = SE3("e_getm", [128, NE])
    mgm = SE3("e_mgm", [128, 16, NE])
    mgm16 = SE3("e_mgm16", [128, 16, NE], BF16)
    gatecs = SE3("e_gatecs", [128, 16])
    sb_tiebm = SE3("e_tiebm", [128, 16, NE])
    wg_r = wg.rearrange("(c p) e -> p c e", p=128)
    A("sync", lambda e: e.dma_start(out=wg_sb[:], in_=wg_r[:]), inc=ld, inc_n=16)
    A("sync", lambda e: e.dma_start(out=sb_tiebm[:], in_=tiebm[:]), inc=ld, inc_n=16)

    for j in range(TOK // 512):
        ln_feature(lnA, lambda cch, j=j: resf[:, cch, 512 * j:512 * (j + 1)], j, 1,
                   lambda cch: h2tb[:, cch, :], h3_park, h2b_park, h2tb[:])
        for cch in range(8):
            A("tensor", lambda e, cch=cch: e.matmul(
                PSB[4][0:8, :], wg_sb[:, cch, :], lnA["f32t"][:, cch, :],
                start=(cch == 0), stop=(cch == 7)),
              waits=(W(vS, sS, ld) if cch == 0 else []), inc=tS)
        A("vector", lambda e: e.tensor_copy(lgs[:], PSB[4][0:8, :]), waits=W(tS), inc=vS)
        for c2 in range(4):
            A("tensor", lambda e, c2=c2: e.transpose(
                PSB[5][0:128, 0:8], lgs[:, 128 * c2:128 * (c2 + 1)], sb_ident[0:8, 0:8]),
              waits=W(vS), inc=tS)
            A("vector", lambda e, j=j, c2=c2: e.tensor_copy(
                lgtok[:, 4 * j + c2, :], PSB[5][0:128, 0:8]), waits=W(tS), inc=vS)

    A("vector", lambda e: e.tensor_tensor(
        lgtok[:], lgtok[:], sb_tiebm[:], ALU.add), waits=W(ld), inc=vS)
    for ch in range(16):
        A("vector", lambda e, ch=ch: e.reduce_max(gmx[:, ch, 0:1], lgtok[:, ch, :],
                                                  axis=AX.X), inc=vS)
        A("vector", lambda e, ch=ch: e.reduce_max(gmx[:, ch, 1:2], lgtok[:, ch, :],
                                                  axis=AX.X, negate=True), inc=vS)
        fv()
        A("scalar", lambda e, ch=ch: e.activation(getm[:], lgtok[:, ch, :], AF.Exp,
                                                  bias=gmx[:, ch, 1:2],
                                                  accum_out=gmx[:, ch, 2:3]),
          waits=W(vS), inc=sS)
        fs()
        fs()
        A("vector", lambda e, ch=ch: e.reciprocal(gmx[:, ch, 2:3], gmx[:, ch, 2:3]),
          waits=W(sS), inc=vS)
        for ee in range(NE):
            A("vector", lambda e, ch=ch, ee=ee: e.tensor_tensor(
                mgm[:, ch, ee:ee + 1], lgtok[:, ch, ee:ee + 1], gmx[:, ch, 0:1],
                ALU.is_ge), inc=vS)
    A("vector", lambda e: e.tensor_copy(mgm16[:], mgm[:]), inc=vS)
    A("vector", lambda e: e.tensor_copy(gatecs[:], gmx[:, :, 2:3]), inc=vS)
    fv()
    fv()
    A("sync", lambda e: e.dma_start(out=mgm16_park[:], in_=mgm16[:]),
      waits=W(vS), inc=ld, inc_n=16)
    A("sync", lambda e: e.dma_start(out=mgm32_park[:], in_=mgm[:]),
      inc=ld, inc_n=16)
    A("sync", lambda e: e.dma_start(out=gatec_park[:], in_=gatecs[:]),
      inc=ld, inc_n=16)

    if debug_phase == "E5":
        d1 = dbg("h2", [128, 8, TOK], BF16)
        d2 = dbg("mgm", [128, 16, NE], BF16)
        d3 = dbg("gatec", [128, 16])
        k.wait_at("sync", vS, vS.n)
        k.wait_at("sync", ld, ld.n)
        A("sync", lambda e: e.dma_start(out=d1[:], in_=h2b_park[:]), inc=st, inc_n=16)
        A("sync", lambda e: e.dma_start(out=d2[:], in_=mgm16_park[:]), inc=st, inc_n=16)
        A("sync", lambda e: e.dma_start(out=d3[:], in_=gatec_park[:]), inc=st, inc_n=16)
        k.wait_at("sync", st, st.n)
        k.emit()
        return nc, dbg_outs

    k.barrier(drain=[ld, gld])
    sE35.close()
    sE25.close()

    flgd = scr("flgd", [NE, 128], mybir.dt.int32)
    sMO = ExitStack()
    moe_acc = sMO.enter_context(nc.sbuf_tensor("e_moeacc", [128, 8, TOK], BF16))
    sE6 = ExitStack()
    SE6 = lambda n, s, d=F32: sE6.enter_context(nc.sbuf_tensor(n, list(s), d))
    hT = SE6("e_hT", [128, NTI, D], BF16)
    grank_bc = SE6("e_grankbc", [128, TOK])
    gate_bc = SE6("e_gatebc", [128, TOK], BF16)
    we1h = [SE6(f"e_we1h{i}", [128, 8, 512], BF16) for i in range(2)]
    we2h = [SE6(f"e_we2h{i}", [128, 8, 512], BF16) for i in range(2)]
    mhid = SE6("e_mhid", [128, 32, 512], BF16)
    ghc = SE6("e_ghc", [128, 8, 512], BF16)
    eout = SE6("e_eout", [128, 4, D], BF16)
    psm = SE6("e_psm", [128, 4, TOK], BF16)
    ptb = [SE6(f"e_ptb{i}", [128, 512], BF16) for i in range(2)]
    hstrf = [SE6(f"e_hstrf{i}", [128, 8, 128]) for i in range(2)]
    mgm16s = SE6("e_mgm16s", [128, 16, NE], BF16)
    mgm32s = SE6("e_mgm32s", [128, 16, NE])
    gatecb = SE6("e_gatecb", [128, 16])
    rankT = SE6("e_rankT", [128, 16, NE])
    grankT = SE6("e_grankT", [128, 16])
    growrow = SE6("e_growrow", [1, TOK])
    gaterow = SE6("e_gaterow", [1, TOK], BF16)
    cntc = SE6("e_cntc", [8, 1])
    e_flg = SE6("e_flg", [8, 128])
    e_flgi = sE6.enter_context(nc.sbuf_tensor("e_flgi", [8, 128], mybir.dt.int32))
    sb_trio = SE6("e_trio", [128, 128], BF16)
    sb_aon1 = SE6("e_aon1", [128, 128], BF16)
    sb_eoffm = SE6("e_eoffm", [128, 16, NE])
    sb_offm = SE6("e_offm", [NE, 128])
    sb_iom = SE6("e_iom", [128, 512])
    sb_iosc = SE6("e_iosc", [128, 1])
    bex = [SE6(f"e_bex{i}", [128, 32]) for i in range(2)]
    be2rw = [SE6("e_be2rw0", [1, D])] * 2

    A("sync", lambda e: e.dma_start(out=sb_trio[:], in_=trio[:]), inc=ld, inc_n=16)
    A("sync", lambda e: e.dma_start(out=sb_aon1[:], in_=aon1[:]), inc=ld, inc_n=16)
    A("sync", lambda e: e.dma_start(out=sb_eoffm[:], in_=eoffm[:]), inc=ld, inc_n=16)
    A("sync", lambda e: e.dma_start(out=sb_offm[:], in_=offm[:]), inc=ld, inc_n=16)
    A("sync", lambda e: e.dma_start(out=sb_iom[:], in_=iom[:]), inc=ld, inc_n=16)
    A("sync", lambda e: e.dma_start(out=sb_iosc[:], in_=iosc[:]), inc=ld, inc_n=16)
    A("sync", lambda e: e.dma_start(out=mgm16s[:], in_=mgm16_park[:]), inc=ld, inc_n=16)
    A("sync", lambda e: e.dma_start(out=mgm32s[:], in_=mgm32_park[:]),
      inc=ld, inc_n=16)
    A("sync", lambda e: e.dma_start(out=gatecb[:], in_=gatec_park[:]), inc=ld, inc_n=16)
    pre_ld = (ld, ld.n)
    pre_gld = (gld, gld.n)
    for m in range(8):
        A("vector", lambda e, m=m: e.tensor_tensor(
            moe_acc[:, m, :], moe_acc[:, m, :], moe_acc[:, m, :], ALU.is_lt), inc=vS)
    for tc in range(NTI):
        hb = tc % 2
        A("sync", lambda e, tc=tc, hb=hb: e.dma_start(
            out=hstrf[hb][:], in_=h3_park[:, :, 128 * tc:128 * (tc + 1)]),
          waits=emw(("hstr_used", hb)), inc=ld, inc_n=16)
        hs_ld = (ld, ld.n)
        for cch in range(8):
            pb = cch % 2
            A("tensor", lambda e, tc=tc, cch=cch, pb=pb, hb=hb: e.transpose(
                PSB[6 + pb][:, 0:128], hstrf[hb][:, cch, :], sb_ident[:]),
              waits=([hs_ld] if cch == 0 else []) + emw(("htr", pb)), inc=tS)
            A("vector", lambda e, tc=tc, cch=cch, pb=pb: e.tensor_copy(
                hT[:, tc, 128 * cch:128 * (cch + 1)], PSB[6 + pb][:, 0:128]),
              waits=[(tS, tS.n)], inc=vS)
            EM[("htr", pb)] = (vS, vS.n)
        EM[("hstr_used", hb)] = (tS, tS.n)
    for ch in range(16):
        pb = ch % 2
        for ch2 in range(ch + 1):
            A("tensor", lambda e, ch=ch, ch2=ch2, pb=pb: e.matmul(
                PSB[pb][:, 0:8],
                sb_trio[:] if ch2 == ch else sb_aon1[:],
                mgm16s[:, ch2, :], start=(ch2 == 0), stop=(ch2 == ch)),
              waits=([pre_ld] + emw(("rkb", pb)) if ch2 == 0 else []), inc=tS)
        A("vector", lambda e, ch=ch, pb=pb: e.tensor_copy(
            rankT[:, ch, :], PSB[pb][:, 0:8]), waits=[(tS, tS.n)], inc=vS)
        EM[("rkb", pb)] = (vS, vS.n)
    for ch in range(16):
        A("tensor", lambda e, ch=ch: e.matmul(
            PSB[2][0:8, 0:1], mgm16s[:, ch, :], sb_aon1[:, 0:1],
            start=(ch == 0), stop=(ch == 15)),
          waits=([pre_ld] if ch == 0 else []), inc=tS)
    A("vector", lambda e: e.tensor_copy(cntc[:], PSB[2][0:8, 0:1]),
      waits=[(tS, tS.n)], inc=vS)
    fv()
    A("vector", lambda e: e.tensor_single_scalar(
        e_flg[:], sb_offm[:], cntc[0:8, 0:1], ALU.is_lt), waits=W(ld), inc=vS)
    A("vector", lambda e: e.tensor_copy(e_flgi[:], e_flg[:]), inc=vS)
    fv()
    fv()
    A("sync", lambda e: e.dma_start(out=flgd[:], in_=e_flgi[:]), waits=W(vS),
      inc=ld, inc_n=16)
    flags_ready = (ld, ld.n)
    A("vector", lambda e: e.tensor_tensor(
        rankT[:], rankT[:], sb_eoffm[:], ALU.add), waits=W(ld, gld), inc=vS)
    A("vector", lambda e: e.tensor_tensor(
        rankT[:], rankT[:], mgm32s[:], ALU.mult), inc=vS)
    for ch in range(16):
        A("vector", lambda e, ch=ch: e.reduce_sum(
            grankT[:, ch:ch + 1], rankT[:, ch, :], axis=AX.X), inc=vS)
    fv()
    for ch in range(16):
        pb = ch % 2
        A("tensor", lambda e, ch=ch, pb=pb: e.transpose(
            PSB[4 + pb][0:1, 0:128], grankT[:, ch:ch + 1], sb_ident[:]),
          waits=W(vS) + emw(("grb", pb)), inc=tS)
        A("vector", lambda e, ch=ch, pb=pb: e.tensor_copy(
            growrow[0:1, 128 * ch:128 * (ch + 1)], PSB[4 + pb][0:1, 0:128]),
          waits=[(tS, tS.n)], inc=vS)
        EM[("grb", pb)] = (vS, vS.n)
        A("tensor", lambda e, ch=ch, pb=pb: e.transpose(
            PSB[4 + pb][0:1, 0:128], gatecb[:, ch:ch + 1], sb_ident[:]),
          waits=W(vS) + emw(("gtb", pb)), inc=tS)
        A("vector", lambda e, ch=ch, pb=pb: e.tensor_copy(
            gaterow[0:1, 128 * ch:128 * (ch + 1)], PSB[4 + pb][0:1, 0:128]),
          waits=[(tS, tS.n)], inc=vS)
        EM[("gtb", pb)] = (vS, vS.n)
    fv()
    fv()
    for w in range(4):
        A("tensor", lambda e, w=w: e.matmul(
            PSB[4][:, 0:512], sb_ones_r[:], growrow[0:1, 512 * w:512 * (w + 1)],
            start=True, stop=True), waits=W(vS) + emw("bcb"), inc=tS)
        A("vector", lambda e, w=w: e.tensor_copy(
            grank_bc[:, 512 * w:512 * (w + 1)], PSB[4][:, 0:512]),
          waits=[(tS, tS.n)], inc=vS)
        EM["bcb"] = (vS, vS.n)
        A("tensor", lambda e, w=w: e.matmul(
            PSB[4][:, 0:512], sb_aon1[0:1, 0:128], gaterow[0:1, 512 * w:512 * (w + 1)],
            start=True, stop=True), waits=W(vS) + emw("bcb"), inc=tS)
        A("vector", lambda e, w=w: e.tensor_copy(
            gate_bc[:, 512 * w:512 * (w + 1)], PSB[4][:, 0:512]),
          waits=[(tS, tS.n)], inc=vS)
        EM["bcb"] = (vS, vS.n)
    fv()
    routing_done_v = (vS, vS.n)

    if debug_phase == "E6r":
        d1 = dbg("grankT", [128, 16])
        d2 = dbg("cnt", [8, 1])
        d6 = dbg("mgm32", [128, 16, NE])
        d7 = dbg("rankTf", [128, 16, NE])
        d8 = dbg("h3p", [128, 8, TOK])
        A("sync", lambda e: e.dma_start(out=d8[:], in_=h3_park[:]), inc=st, inc_n=16)
        A("sync", lambda e: e.dma_start(out=d6[:], in_=mgm32s[:]), inc=st, inc_n=16)
        A("sync", lambda e: e.dma_start(out=d7[:], in_=rankT[:]), inc=st, inc_n=16)
        d3 = dbg("grankbc", [128, TOK])
        d4 = dbg("gatebc", [128, TOK], BF16)
        d5 = dbg("hTd", [128, NTI, D], BF16)
        k.wait_at("sync", vS, vS.n)
        k.wait_at("sync", ld, ld.n)
        A("sync", lambda e: e.dma_start(out=d1[:], in_=grankT[:]), inc=st, inc_n=16)
        A("sync", lambda e: e.dma_start(out=d2[:], in_=cntc[:]), inc=st, inc_n=16)
        A("sync", lambda e: e.dma_start(out=d3[:], in_=grank_bc[:]), inc=st, inc_n=16)
        A("sync", lambda e: e.dma_start(out=d4[:], in_=gate_bc[:]), inc=st, inc_n=16)
        A("sync", lambda e: e.dma_start(out=d5[:], in_=hT[:]), inc=st, inc_n=16)
        k.wait_at("sync", st, st.n)
        k.emit()
        return nc, dbg_outs

    JMAP = [0, 1, 2, 2]
    we1r = [we1b[ee] for ee in range(NE)]
    we2r = [we2b[ee] for ee in range(NE)]

    for ee in range(NE):
        eb = ee % 2
        A("sync", lambda e, ee=ee, eb=eb: e.dma_start(
            out=bex[eb][:], in_=be1_c[ee]), waits=emw(("bex_used", eb)),
          inc=ld, inc_n=16)
        A("sync", lambda e, ee=ee, eb=eb: e.dma_start(
            out=be2rw[0][:], in_=be2_r[ee:ee + 1, :]),
          waits=emw("be2u"), inc=ld, inc_n=16)
        bex_ld = (ld, ld.n)
        for si in range(4):
            base_g = 2048 * ee + 512 * si
            fap = (lambda ee=ee, jf=JMAP[si]:
                   flgd[ee:ee + 1, 32 * jf:32 * jf + 1])

            k.begin_region(fap, flags_ready)
            for ph in range(2):
                A("sync", lambda e, ee=ee, ph=ph: e.dma_start(
                    out=we1h[ph][:], in_=we1r[ee][:, :, 512 * ph:512 * (ph + 1)]),
                  waits=[wpark_k] + emw(("w1u", ph)), inc=ld, inc_n=16)
                EM[("w1l", ph)] = (ld, ld.n)
            for tc in range(NTI):
                pb = tc % 2
                A("vector", lambda e, tc=tc, pb=pb, base_g=base_g: e.tensor_scalar(
                    ptb[pb][:], sb_iom[:], grankT[:, tc:tc + 1], float(-base_g),
                    ALU.subtract, ALU.is_equal),
                  waits=emw(("ptu", pb)), inc=vS)
                ptk = (vS, vS.n)
                for m in range(8):
                    A("tensor", lambda e, tc=tc, m=m, pb=pb: e.matmul(
                        PSB[m][:], hT[:, tc, 128 * m:128 * (m + 1)], ptb[pb][:],
                        start=(tc == 0), stop=(tc == NTI - 1)),
                      waits=(([ptk] if m == 0 else [])
                             + (emw(("sct", m)) if tc == 0 else [])), inc=tS)
                EM[("ptu", pb)] = (tS, tS.n)
            gk = (tS, tS.n)
            for m in range(8):
                A("vector", lambda e, m=m: e.tensor_copy(ghc[:, m, :], PSB[m][:]),
                  waits=[gk] + (emw("ghu") if m == 0 else []), inc=vS)
            ghk = (vS, vS.n)
            for fc in range(32):
                pf = fc % 4
                ph = fc // 4
                for cch in range(8):
                    A("tensor", lambda e, fc=fc, cch=cch, pf=pf, ph=ph: e.matmul(
                        PSB[pf][:], we1h[ph % 2][:, cch, 128 * (fc % 4):128 * (fc % 4 + 1)],
                        ghc[:, cch, :], start=(cch == 0), stop=(cch == 7)),
                      waits=(([EM[("w1l", ph % 2)], ghk, bex_ld] + emw(("mb", pf)))
                             if cch == 0 else []), inc=tS)
                ff1k = (tS, tS.n)
                if fc % 4 == 3:
                    EM[("w1u", ph % 2)] = ff1k
                    if ph + 2 < 8:
                        A("sync", lambda e, ee=ee, ph=ph: e.dma_start(
                            out=we1h[ph % 2][:],
                            in_=we1r[ee][:, :, 512 * (ph + 2):512 * (ph + 3)]),
                          waits=emw(("w1u", ph % 2)), inc=ld, inc_n=16)
                        EM[("w1l", ph % 2)] = (ld, ld.n)
                A("vector", lambda e, fc=fc, pf=pf, eb=eb: e.tensor_scalar(
                    mhid[:, fc, :], PSB[pf][:], bex[eb][:, fc:fc + 1], 0.0,
                    ALU.add, ALU.max),
                  waits=[ff1k, bex_ld] + emw(("mhu", fc)), inc=vS)
                EM[("mb", pf)] = (vS, vS.n)
            relu_k = (vS, vS.n)
            EM["ghu"] = (tS, tS.n)
            k.end_region()

            k.begin_region(fap, flags_ready)
            for sc in range(4):
                A("vector", lambda e, sc=sc, base_g=base_g: e.tensor_scalar(
                    psm[:, sc, :], grank_bc[:], sb_iosc[:, 0:1],
                    float(base_g + 128 * sc), ALU.subtract, ALU.is_equal),
                  waits=emw("psmu") if sc == 0 else [], inc=vS)
                A("vector", lambda e, sc=sc: e.tensor_tensor(
                    psm[:, sc, :], psm[:, sc, :], gate_bc[:], ALU.mult), inc=vS)
            psm_k = (vS, vS.n)
            for half in range(2):
                for g4 in range(4):
                    pw = (half * 4 + g4) % 2
                    A("sync", lambda e, ee=ee, g4=g4, half=half, pw=pw: e.dma_start(
                        out=we2h[pw][:],
                        in_=we2r[ee][:, 8 * g4:8 * (g4 + 1), 512 * half:512 * (half + 1)]),
                      waits=emw(("w2u", pw)), inc=ld, inc_n=16)
                    EM[("w2l", pw)] = (ld, ld.n)
                    for fc8 in range(8):
                        fc = 8 * g4 + fc8
                        for sc in range(4):
                            bank = (4 + sc) if half == 0 else sc
                            A("tensor", lambda e, fc=fc, sc=sc, bank=bank, pw=pw,
                              half=half: e.matmul(
                                PSB[bank][:], mhid[:, fc, 128 * sc:128 * (sc + 1)],
                                we2h[pw][:, fc % 8, :],
                                start=(fc == 0), stop=False),
                              waits=(([relu_k, EM[("w2l", pw)]]
                                      + (emw(("scv", 7)) if (fc == 0 and half == 0)
                                         else []))
                                     if (fc8 == 0 and sc == 0) else []), inc=tS)
                        if fc8 == 7:
                            EM[("w2u", pw)] = (tS, tS.n)
                        if half == 1:
                            EM[("mhu", fc)] = (tS, tS.n)
                for sc in range(4):
                    bank = (4 + sc) if half == 0 else sc
                    A("tensor", lambda e, sc=sc, bank=bank, half=half, eb=eb: e.matmul(
                        PSB[bank][:], sb_ones_r[:],
                        be2rw[eb][0:1, 512 * half:512 * (half + 1)],
                        start=False, stop=True), inc=tS)
                ff2k = (tS, tS.n)
                if half == 1:
                    EM["be2u"] = ff2k
                for sc in range(4):
                    bank = (4 + sc) if half == 0 else sc
                    A("vector", lambda e, sc=sc, bank=bank, half=half: e.tensor_copy(
                        eout[:, sc, 512 * half:512 * (half + 1)], PSB[bank][:]),
                      waits=[ff2k] + (emw("eou") if half + sc == 0 else []), inc=vS)
            eo_k = (vS, vS.n)
            for tp in range(4):
                for m in range(8):
                    for sc in range(4):
                        A("tensor", lambda e, tp=tp, m=m, sc=sc: e.matmul(
                            PSB[m][:], eout[:, sc, 128 * m:128 * (m + 1)],
                            psm[:, sc, 512 * tp:512 * (tp + 1)],
                            start=(sc == 0), stop=(sc == 3)),
                          waits=(([eo_k, psm_k] + emw(("scv", m)))
                                 if sc == 0 else []), inc=tS)
                    sck = (tS, tS.n)
                    A("vector", lambda e, tp=tp, m=m: e.tensor_tensor(
                        moe_acc[:, m, 512 * tp:512 * (tp + 1)],
                        moe_acc[:, m, 512 * tp:512 * (tp + 1)], PSB[m][:], ALU.add),
                      waits=[sck], inc=vS)
                    EM[("scv", m)] = (vS, vS.n)
                    if tp == 3:
                        EM[("sct", m)] = (vS, vS.n)
            EM["psmu"] = (tS, tS.n)
            EM["eou"] = (tS, tS.n)
            k.end_region()
        EM[("bex_used", eb)] = (vS, vS.n)

    k.barrier(drain=[ld, gld, aS])

    if debug_phase == "E6":
        d1 = dbg("moe", [128, 8, TOK], BF16)
        d2 = dbg("cnt", [8, 1])
        d6 = dbg("mgm32", [128, 16, NE])
        A("sync", lambda e: e.dma_start(out=d1[:], in_=moe_acc[:]), inc=st, inc_n=16)
        A("sync", lambda e: e.dma_start(out=d2[:], in_=cntc[:]), inc=st, inc_n=16)
        A("sync", lambda e: e.dma_start(out=d6[:], in_=mgm32s[:]), inc=st, inc_n=16)
        k.wait_at("sync", st, st.n)
        k.emit()
        return nc, dbg_outs

    sE6.close()


    sE7F = ExitStack()
    gT = sE7F.enter_context(nc.sbuf_tensor("f_gT", [128, 8, TOK], F32))
    sE7 = ExitStack()
    SE7 = lambda n, s, d=F32: sE7.enter_context(nc.sbuf_tensor(n, list(s), d))
    h316c = [SE7(f"e_h316c{i}", [128, 8, 512], BF16) for i in range(2)]
    h3c = SE7("e_h3c", [128, 8, 512])
    wof_sb = SE7("e_wof", [128, 8, D], BF16)
    lnB = mk_lnbufs(sE7, "lnB")
    A("gpsimd", lambda e: e.dma_start(
        out=wof_sb[:], in_=wout_f.rearrange("(c p) e -> p c e", p=128)),
      inc=gld, inc_n=16)
    wof_ld = (gld, gld.n)

    for j in range(TOK // 512):
        A("sync", lambda e, j=j: e.dma_start(
            out=h3c[:], in_=h3_park[:, :, 512 * j:512 * (j + 1)]),
          waits=W(tS, sS, vS), inc=ld, inc_n=16)
        A("vector", lambda e, j=j: e.tensor_tensor(
            h3c[:], h3c[:], moe_acc[:, :, 512 * j:512 * (j + 1)],
            ALU.add), waits=W(ld), inc=vS)
        ln_feature(lnB, lambda cch: h3c[:, cch, :], j, 2,
                   lambda cch, j=j: h316c[j % 2][:, cch, :],
                   None, None, None)
        ln_k = (vS, vS.n)
        for m in range(8):
            pm = m % 2
            for cch in range(8):
                A("tensor", lambda e, j=j, m=m, cch=cch, pm=pm: e.matmul(
                    PSB[2 + pm][:], wof_sb[:, cch, 128 * m:128 * (m + 1)],
                    h316c[j % 2][:, cch, :],
                    start=(cch == 0), stop=(cch == 7)),
                  waits=(([wof_ld, ln_k] + emw(("gb", pm)) +
                          ([(sS, sS.n)] if (j + m == 0) else []))
                         if cch == 0 else []), inc=tS)
            A("scalar", lambda e, j=j, m=m, pm=pm: e.activation(
                gT[:, m, 512 * j:512 * (j + 1)], PSB[2 + pm][:], AF.Silu),
              waits=[(tS, tS.n)], inc=sS)
            EM[("gb", pm)] = (sS, sS.n)

    k.barrier(drain=[ld, gld])
    sE7.close()

    sF = ExitStack()
    SF = lambda n, s, d=F32: sF.enter_context(nc.sbuf_tensor(n, list(s), d))
    cnc = [SF(f"f_cnc{i}", [128, 8, 512], BF16) for i in range(2)]
    woa_sb = SF("f_woa", [128, 8, D], BF16)
    prod = SF("f_prod", [128, 8, 512])
    otok = SF("f_otok", [128, D])
    xqt = SF("f_xqt", [128, D])
    A("gpsimd", lambda e: e.dma_start(
        out=woa_sb[:], in_=wout_a.rearrange("(c p) e -> p c e", p=128)),
      inc=gld, inc_n=16)

    woa_ld = (gld, gld.n)
    for j in range(TOK // 512):
        A("sync", lambda e, j=j: e.dma_start(
            out=cnc[j % 2][:], in_=cn_park[:, :, 512 * j:512 * (j + 1)]),
          waits=emw(("cnc_used", j % 2)), inc=ld, inc_n=16)
        cn_ld = (ld, ld.n)
        for m in range(8):
            pm = m % 2
            for cch in range(8):
                A("tensor", lambda e, j=j, m=m, cch=cch, pm=pm: e.matmul(
                    PSB[pm][:], woa_sb[:, cch, 128 * m:128 * (m + 1)],
                    cnc[j % 2][:, cch, :], start=(cch == 0), stop=(cch == 7)),
                  waits=(([cn_ld, woa_ld] + emw(("sb", pm)) +
                          ([(vS, vS.n), (sS, sS.n)] if (j + m == 0) else []))
                         if cch == 0 else []), inc=tS)
            if m == 7:
                EM[("cnc_used", j % 2)] = (tS, tS.n)
            A("vector", lambda e, j=j, m=m, pm=pm: e.tensor_tensor(
                prod[:, m, :], PSB[pm][:], gT[:, m, 512 * j:512 * (j + 1)], ALU.mult),
              waits=[(tS, tS.n)] + emw(("prod_used", m)), inc=vS)
            EM[("sb", pm)] = (vS, vS.n)
        for s2 in range(4):
            A("sync", lambda e, j=j, s2=s2: e.dma_start(
                out=xqt[:], in_=xq[512 * j + 128 * s2:512 * j + 128 * (s2 + 1), :]),
              waits=emw("xqt_used"), inc=ld, inc_n=16)
            xq_ld = (ld, ld.n)
            for m in range(8):
                pm = 2 + (m % 2)
                A("tensor", lambda e, s2=s2, m=m, pm=pm, j=j: e.transpose(
                    PSB[pm][:, 0:128], prod[:, m, 128 * s2:128 * (s2 + 1)], sb_ident[:]),
                  waits=(emw(("sb", 0)) + emw(("sb", 1)) if (s2 == 0 and m == 0) else [])
                        + emw(("fb", pm)), inc=tS)
                if s2 == 3:
                    EM[("prod_used", m)] = (tS, tS.n)
                A("vector", lambda e, m=m, pm=pm: e.tensor_copy(
                    otok[:, 128 * m:128 * (m + 1)], PSB[pm][:, 0:128]),
                  waits=[(tS, tS.n)] + (emw("otok_stored") if m == 0 else []), inc=vS)
                EM[("fb", pm)] = (vS, vS.n)
            A("vector", lambda e: e.tensor_add(otok[:], otok[:], xqt[:]),
              waits=[xq_ld], inc=vS)
            EM["xqt_used"] = (vS, vS.n)
            A("sync", lambda e, j=j, s2=s2: e.dma_start(
                out=out[512 * j + 128 * s2:512 * j + 128 * (s2 + 1), :], in_=otok[:]),
              waits=[(vS, vS.n)], inc=st, inc_n=16)
            EM["otok_stored"] = (st, st.n)

    k.wait_at("sync", st, st.n)
    k.emit()
    return nc, dbg_outs



def _pe_table():
    pos = np.arange(S - 1, -1, -1, dtype=np.float32)
    inv_freq = (1.0 / 10000.0 ** (np.arange(0, D, 2, dtype=np.float32) / D)).astype(np.float32)
    sinusoid = pos[:, None] * inv_freq[None, :]
    return np.concatenate([np.sin(sinusoid), np.cos(sinusoid)], axis=-1).astype(np.float32)


_CACHED = {}


def _get_program():
    if "nc" not in _CACHED:
        _CACHED["nc"] = build_program(None)[0]
    return _CACHED["nc"]


def make_inmaps(inputs):
    x = np.ascontiguousarray(inputs["L0_attn_x"])

    pe = _pe_table()
    peT_full = pe.T
    ii = np.arange(128)[:, None]
    cc = np.arange(256)[None, :]
    ident = np.eye(128, dtype=np.float32)
    rmask = (np.arange(128)[:, None] >= np.arange(128)[None, :]).astype(np.float32)

    gq = np.zeros((64, NH, 512), np.float32)
    gk = np.zeros((64, NH, 512), np.float32)
    jj = np.arange(512) % 128
    for h in range(NH):
        g = GAMMAS[h]
        gq[:, h, :] = 0.125 * (g ** jj)[None, :]
        gk[:, h, :] = (g ** (-jj.astype(np.float64))).astype(np.float32)[None, :]

    com = dict(
        peT=None,
        rmask=rmask, ident=ident,
        gq_t=gq, gk_t=gk,
        wq_a=inputs["L0_attn_Wq"], wk_a=inputs["L0_attn_Wk"], wv_a=inputs["L0_attn_Wv"],
        gnw_t=np.ascontiguousarray(inputs["L0_attn_gn_weight"].reshape(NH, 64).T),
        gnb_t=np.ascontiguousarray(inputs["L0_attn_gn_bias"].reshape(NH, 64).T),
        wout_a=inputs["L0_attn_Wout"],
        wqkv=inputs["L0_ffn_Wqkv"], wr=inputs["L0_ffn_Wr"],
        ub_t=np.ascontiguousarray(inputs["L0_ffn_u_bias"].T),
        vb_t=np.ascontiguousarray(inputs["L0_ffn_v_bias"].T),
        wo=inputs["L0_ffn_Wo"],
        ln_g=np.stack([inputs["L0_ffn_ln1_g"].reshape(8, 128).T,
                       inputs["L0_ffn_ln2_g"].reshape(8, 128).T,
                       inputs["L0_ffn_ln3_g"].reshape(8, 128).T], axis=1),
        ln_b=np.stack([inputs["L0_ffn_ln1_b"].reshape(8, 128).T,
                       inputs["L0_ffn_ln2_b"].reshape(8, 128).T,
                       inputs["L0_ffn_ln3_b"].reshape(8, 128).T], axis=1),
        wff1=inputs["L0_ffn_Wff1"],
        bff1_c=np.ascontiguousarray(inputs["L0_ffn_bff1"].reshape(32, 128).T),
        wff2=inputs["L0_ffn_Wff2"],
        bff2_c=np.ascontiguousarray(inputs["L0_ffn_bff2"].reshape(8, 128).T),
        wg=inputs["L0_ffn_Wg"],
        we1=inputs["L0_ffn_We1"],
        be1_c=np.ascontiguousarray(inputs["L0_ffn_be1"].reshape(NE, 32, 128).transpose(0, 2, 1)),
        we2=inputs["L0_ffn_We2"],
        be2_r=inputs["L0_ffn_be2"],
        wout_f=inputs["L0_ffn_Wout"],
        eoffm=np.broadcast_to(2048.0 * np.arange(NE, dtype=np.float32)[None, None, :],
                              (128, 16, NE)),
        offm=np.broadcast_to(
            np.repeat(np.array([0.0, 512.0, 1024.0, 1e9], np.float32), 32)[None, :],
            (NE, 128)),
        iom=np.broadcast_to(np.arange(512, dtype=np.float32)[None, :], (128, 512)),
        iosc=np.arange(128, dtype=np.float32)[:, None],
        tiebm=np.broadcast_to(
            ((7 - np.arange(NE, dtype=np.float32)) * 1e-6)[None, None, :],
            (128, 16, NE)),
    )
    com = {kk: (np.ascontiguousarray(v, dtype=np.float32) if v is not None else None)
           for kk, v in com.items()}
    com["trio"] = np.triu(np.ones((128, 128)), 1).astype(ml_dtypes.bfloat16)
    com["aon1"] = np.ones((128, 128), dtype=ml_dtypes.bfloat16)
    com["spillq"] = np.where(np.arange(128)[None, :] > np.arange(128)[:, None],
                             NEG, 0.0).astype(ml_dtypes.bfloat16)
    com["negt"] = np.full((128, 128), NEG, dtype=ml_dtypes.bfloat16)

    in_maps = []
    for c in range(8):
        b, p = c // 2, c % 2
        rows = np.concatenate([np.arange(256 * t + 128 * p, 256 * t + 128 * p + 128)
                               for t in range(NTI)])
        xqc = np.ascontiguousarray(x[b][rows])
        peTc = np.zeros((D, S + 256), np.float32)
        peTc[:, 128 * p:128 * p + S] = peT_full
        peTc_in = np.ascontiguousarray(
            peTc.reshape(8, 128, S + 256).transpose(1, 0, 2)).astype(ml_dtypes.bfloat16)
        am = np.where(cc <= ii + 128 * p, 0.0, NEG).astype(np.float32)
        sel = np.zeros((64, 2), np.float32)
        sel[:, p] = 1.0
        m = dict(com)
        m["peT"] = peTc_in
        m["pmq"] = np.full((128, 128), NEG if p == 0 else 0.0,
                           dtype=ml_dtypes.bfloat16)
        m["xb"] = np.ascontiguousarray(x[b])
        m["xq"] = xqc
        m["amask"] = am
        m["selc"] = sel
        in_maps.append(m)
    return in_maps


def kernel(**inputs):
    in_maps = make_inmaps(inputs)
    nc = _get_program()
    res = run_bass_kernel_spmd(nc, in_maps, core_ids=list(range(8)))

    outx = np.zeros((4, S, D), np.float32)
    for c in range(8):
        b, p = c // 2, c % 2
        oc = np.asarray(res.results[c]["out"], dtype=np.float32)
        for t in range(NTI):
            outx[b, 256 * t + 128 * p:256 * t + 128 * p + 128] = oc[128 * t:128 * (t + 1)]
    return outx


if __name__ == "__main__":
    build_program()


# version 19
# speedup vs baseline: 1.0087x; 1.0087x over previous
import numpy as np
import ml_dtypes
import concourse.bass as bass
import concourse.mybir as mybir
from concourse.bass_utils import run_bass_kernel_spmd
from concourse.ap import AP
from contextlib import ExitStack

F32 = mybir.dt.float32
BF16 = mybir.dt.bfloat16
AF = mybir.ActivationFunctionType
ALU = mybir.AluOpType
AX = mybir.AxisListType

S, D, NH, DH, FFD, NE = 4096, 1024, 16, 64, 4096, 8
TOK = 2048
NTI = 16
EPS = 1e-5
NEG = -30000.0
GAMMAS = [1.0 - 2.0 ** (-5 - (h // 2)) for h in range(NH)]

ENGS = ("sync", "tensor", "vector", "scalar", "gpsimd")


class Sem:
    __slots__ = ("h", "n", "name")

    def __init__(self, h, name):
        self.h = h
        self.n = 0
        self.name = name


COND_ENGS = ("tensor", "vector", "sync")


class KB:
    def __init__(self, nc, es):
        self.nc = nc
        self.es = es
        self.ops = {e: [] for e in ENGS}
        self.bar = None
        self.region = None
        self.regs = None

    def sem(self, name):
        return Sem(self.es.enter_context(self.nc.semaphore(name)), name)

    def add(self, eng, build, waits=(), inc=None, inc_n=1):
        wl = [(s.h, v) for (s, v) in waits]

        def f(e, wl=wl, build=build, inc=inc, inc_n=inc_n):
            for h, v in wl:
                e.wait_ge(h, v)
            ins = build(e)
            if inc is not None:
                ins.then_inc(inc.h, inc_n)
        self.ops[eng].append([f, build, wl, inc, inc_n])
        if inc is not None:
            inc.n += inc_n
            if self.region is not None and eng in self.region:
                t = self.region[eng]
                t[inc] = t.get(inc, 0) + inc_n

    def begin_region(self, flag_ap_fn, ready):
        assert self.region is None
        self.region = {eng: {} for eng in COND_ENGS}
        for eng in COND_ENGS:
            self.ops[eng].append(["IF", flag_ap_fn, (ready[0].h, ready[1])])

    def end_region(self):
        for eng in COND_ENGS:
            comp = self.region[eng]
            self.ops[eng].append(["ELSE"])
            for sem, n in comp.items():
                self.ops[eng].append(["COMP", sem.h, n])
            self.ops[eng].append(["END"])
        self.region = None

    def w(self, *sems):
        return [(s, s.n) for s in sems]

    def wait_at(self, eng, sem, v):
        self.ops[eng].append([lambda e, h=sem.h, vv=v: e.wait_ge(h, vv), None, [], None, 0])

    def barrier(self, drain=()):
        if self.bar is None:
            self.bar = self.sem("bar")
        for eng in ENGS:
            q = self.ops[eng]
            if eng in ("sync", "gpsimd"):
                for s in drain:
                    self.wait_at(eng, s, s.n)
            q.append([lambda e: e.sem_inc(self.bar.h, 1), None, [], None, 0])
            self.bar.n += 1
        target = self.bar.n
        for eng in ENGS:
            self.wait_at(eng, self.bar, target)

    def emit(self):
        nreg = {eng: [getattr(self.nc, eng).alloc_register(f"fl_{eng}{i}")
                      for i in range(4)] for eng in COND_ENGS}
        with self.nc.Block() as blk:
            for eng in ENGS:
                items = self.ops[eng]
                regs = nreg.get(eng)

                def run(e, items=items, regs=regs):
                    guard = None
                    ri = 0
                    for it in items:
                        tag = it[0]
                        if tag == "IF":
                            _, apf, rd = it
                            e.wait_ge(rd[0], rd[1])
                            reg = regs[ri % 4]
                            ri += 1
                            e.reg_load(reg, apf())
                            guard = e.If_ne(reg, 0)
                            guard.__enter__()
                        elif tag == "ELSE":
                            guard.__exit__(None, None, None)
                            guard = e.Else()
                            guard.__enter__()
                        elif tag == "COMP":
                            e.sem_inc(it[1], it[2])
                        elif tag == "END":
                            guard.__exit__(None, None, None)
                            guard = None
                        else:
                            it[0](e)
                getattr(blk, eng)(run)


def cdiv(a, b):
    return (a + b - 1) // b



def build_program(debug_phase=None):
    nc = bass.Bass("TRN2", target_bir_lowering=False, debug=False)
    es = ExitStack()
    k = KB(nc, es)
    A = k.add
    W = k.w

    def inp(name, shape, dt=F32):
        return nc.dram_tensor(name, list(shape), dt, kind="ExternalInput").ap()

    xb = inp("xb", [S, D])
    xq = inp("xq", [TOK, D])
    peT = inp("peT", [128, 8, S + 256], BF16)
    amask = inp("amask", [128, 256])
    rmask = inp("rmask", [128, 128])
    ident_i = inp("ident", [128, 128])
    selc = inp("selc", [64, 2])
    gq_t = inp("gq_t", [64, NH, 512])
    gk_t = inp("gk_t", [64, NH, 512])
    wq_a = inp("wq_a", [NH, D, DH])
    wk_a = inp("wk_a", [NH, D, DH])
    wv_a = inp("wv_a", [NH, D, DH])
    gnw_t = inp("gnw_t", [64, NH])
    gnb_t = inp("gnb_t", [64, NH])
    wout_a = inp("wout_a", [D, D])
    wqkv = inp("wqkv", [D, 3 * D])
    wr = inp("wr", [D, D])
    ub_t = inp("ub_t", [DH, NH])
    vb_t = inp("vb_t", [DH, NH])
    wo = inp("wo", [D, D])
    ln_g = inp("ln_g", [128, 3, 8])
    ln_b = inp("ln_b", [128, 3, 8])
    wff1 = inp("wff1", [D, FFD])
    bff1_c = inp("bff1_c", [128, 32])
    wff2 = inp("wff2", [FFD, D])
    bff2_c = inp("bff2_c", [128, 8])
    wg = inp("wg", [D, NE])
    we1 = inp("we1", [NE, D, FFD])
    be1_c = inp("be1_c", [NE, 128, 32])
    we2 = inp("we2", [NE, FFD, D])
    be2_r = inp("be2_r", [NE, D])
    wout_f = inp("wout_f", [D, D])
    trio = inp("trio", [128, 128], BF16)
    aon1 = inp("aon1", [128, 128], BF16)
    eoffm = inp("eoffm", [128, 16, NE])
    offm = inp("offm", [NE, 128])
    iom = inp("iom", [128, 512])
    iosc = inp("iosc", [128, 1])
    tiebm = inp("tiebm", [128, 16, NE])
    spillq = inp("spillq", [128, 128], BF16)
    pmq = inp("pmq", [128, 128], BF16)
    negt = inp("negt", [128, 128], BF16)

    out = nc.dram_tensor("out", [TOK, D], F32, kind="ExternalOutput").ap()

    def scr(name, shape, dt=F32):
        return nc.dram_tensor(name, list(shape), dt, kind="Internal").ap()

    pskew = [scr("pskew0", [128 * (S + 256)], BF16), scr("pskew1", [128 * (S + 256)], BF16)]
    pskew_dbg = scr("pskew_dbg", [128 * 256], BF16)
    ctx_park = scr("ctx_park", [128, 8, TOK], BF16)
    cn_park = scr("cn_park", [128, 8, TOK], BF16)
    h1_park = scr("h1_park", [128, 8, TOK], F32)
    h2b_park = scr("h2b_park", [128, 8, TOK], BF16)
    h3_park = scr("h3_park", [128, 8, TOK], F32)

    dbg_outs = {}

    def dbg(name, shape, dt=F32):
        ap = nc.dram_tensor("dbg_" + name, list(shape), dt, kind="ExternalOutput").ap()
        dbg_outs[name] = ap
        return ap

    SBp = lambda n, s, d=F32: es.enter_context(nc.sbuf_tensor(n, list(s), d))
    sb_ident = SBp("c_ident", [128, 128])
    sb_ident16 = SBp("c_ident16", [128, 128], BF16)
    sb_rmask = SBp("c_rmask", [128, 128])
    sb_ones_c = SBp("c_ones_c", [128, 1])
    sb_ones_r = SBp("c_ones_r", [1, 128])
    sb_sel = SBp("c_sel", [64, 2])
    sb_scrv = SBp("c_scrv", [128, 16])
    sb_scrs = SBp("c_scrs", [128, 16])
    sb_lng = SBp("c_lng", [128, 3, 8])
    sb_lnb = SBp("c_lnb", [128, 3, 8])

    ld = k.sem("ld")
    gld = k.sem("gld")
    st = k.sem("st")
    tS = k.sem("tS")
    vS = k.sem("vS")
    sS = k.sem("sS")
    pS = k.sem("pS")
    bS = k.sem("bS")
    aS = k.sem("aS")

    A("sync", lambda e: e.dma_start(out=sb_ident[:], in_=ident_i[:]), inc=ld, inc_n=16)
    A("sync", lambda e: e.dma_start(out=sb_rmask[:], in_=rmask[:]), inc=ld, inc_n=16)
    A("sync", lambda e: e.dma_start(out=sb_sel[:], in_=selc[:]), inc=ld, inc_n=16)
    A("sync", lambda e: e.dma_start(out=sb_lng[:], in_=ln_g[:]), inc=ld, inc_n=16)
    A("sync", lambda e: e.dma_start(out=sb_lnb[:], in_=ln_b[:]), inc=ld, inc_n=16)
    A("vector", lambda e: e.memset(sb_ones_c[:], 1.0), inc=vS)
    A("vector", lambda e: e.memset(sb_ones_r[:], 1.0), inc=vS)
    A("vector", lambda e: e.tensor_copy(sb_ident16[:], sb_ident[:]),
      waits=[(ld, 16)], inc=vS)

    sAC = ExitStack()
    xnT = sAC.enter_context(nc.sbuf_tensor("xnT", [128, 8, S], BF16))
    xnqT16 = sAC.enter_context(nc.sbuf_tensor("xnqT16", [128, 8, TOK], BF16))

    sA = ExitStack()
    SA = lambda n, s, d=F32: sA.enter_context(nc.sbuf_tensor(n, list(s), d))
    PA = lambda n, s, d=F32: sA.enter_context(nc.psum_tensor(n, list(s), d))
    a_xt = [SA(f"a_xt{i}", [128, D]) for i in range(2)]
    a_xn = [SA(f"a_xn{i}", [128, D]) for i in range(2)]
    a_sq = SA("a_sqs", [128, D])
    a_stat = SA("a_stat", [128, 8 * 32])
    ps_tr = [PA(f"a_ptr{i}", [128, 128]) for i in range(2)]

    def fv():
        A("vector", lambda e: e.tensor_copy(sb_scrv[:], sb_ident[:, 0:16]), inc=vS)

    def fs():
        A("scalar", lambda e: e.copy(sb_scrs[:], sb_ident[:, 0:16]), inc=sS)

    def emit_ln_pass(src_fn, nt, stat, xtb, xnb, sqb, trb, dst16, dst32):
        n = nt
        for j in range(nt):
            bi = j % 2
            A("sync", lambda e, j=j, bi=bi: e.dma_start(out=xtb[bi][:], in_=src_fn(j)),
              waits=W(vS, sS), inc=ld, inc_n=16)
            A("vector", lambda e, j=j, bi=bi: e.reduce_sum(
                stat[:, j:j + 1], xtb[bi][:], axis=AX.X), waits=W(ld), inc=vS)
            A("scalar", lambda e, j=j, bi=bi: e.activation(
                sqb[:], xtb[bi][:], AF.Square, accum_out=stat[:, n + j:n + j + 1]),
              waits=W(ld), inc=sS)
        fs()
        fv()
        A("vector", lambda e: e.tensor_scalar_mul(stat[:, 2 * n:3 * n], stat[:, 0:n],
                                                  1.0 / D), waits=W(sS), inc=vS)
        A("vector", lambda e: e.tensor_scalar(stat[:, 7 * n:8 * n], stat[:, n:2 * n],
                                              1.0 / D, EPS, ALU.mult, ALU.add), inc=vS)
        A("vector", lambda e: e.tensor_tensor(stat[:, 6 * n:7 * n], stat[:, 2 * n:3 * n],
                                              stat[:, 2 * n:3 * n], ALU.mult), inc=vS)
        fv()
        A("vector", lambda e: e.tensor_sub(stat[:, 3 * n:4 * n], stat[:, 7 * n:8 * n],
                                           stat[:, 6 * n:7 * n]), inc=vS)
        fv()
        A("scalar", lambda e: e.activation(stat[:, 4 * n:5 * n], stat[:, 3 * n:4 * n],
                                           AF.Sqrt), waits=W(vS), inc=sS)
        fs()
        A("vector", lambda e: e.reciprocal(stat[:, 4 * n:5 * n], stat[:, 4 * n:5 * n]),
          waits=W(sS), inc=vS)
        fv()
        A("vector", lambda e: e.scalar_tensor_tensor(
            stat[:, 5 * n:6 * n], stat[:, 2 * n:3 * n], -1.0, stat[:, 4 * n:5 * n],
            ALU.mult, ALU.mult), inc=vS)
        fv()
        fv()
        fmark = vS.n
        for j in range(nt):
            bi = j % 2
            A("sync", lambda e, j=j, bi=bi: e.dma_start(out=xtb[bi][:], in_=src_fn(j)),
              waits=W(vS, sS), inc=ld, inc_n=16)
            A("scalar", lambda e, j=j, bi=bi: e.activation(
                xnb[bi][:], xtb[bi][:], AF.Identity, bias=stat[:, 5 * n + j:5 * n + j + 1],
                scale=stat[:, 4 * n + j:4 * n + j + 1]),
              waits=[(vS, fmark), (ld, ld.n), (tS, tS.n)], inc=sS)
            for cch in range(8):
                pb = cch % 2
                A("tensor", lambda e, cch=cch, pb=pb, bi=bi: e.transpose(
                    trb[pb], xnb[bi][:, 128 * cch:128 * (cch + 1)], sb_ident[:]),
                  waits=W(sS, vS), inc=tS)
                first = True
                for dd in (dst16, dst32):
                    if dd is None:
                        continue
                    A("vector", lambda e, cch=cch, pb=pb, j=j, dd=dd: e.tensor_copy(
                        dd[:, cch, 128 * j:128 * (j + 1)], trb[pb]),
                      waits=(W(tS) if first else []), inc=vS)
                    first = False

    emit_ln_pass(lambda j: xb[128 * j:128 * (j + 1), :], 32, a_stat, a_xt, a_xn,
                 a_sq, [p[:] for p in ps_tr], xnT, None)
    emit_ln_pass(lambda j: xq[128 * j:128 * (j + 1), :], 16, a_stat, a_xt, a_xn,
                 a_sq, [p[:] for p in ps_tr], xnqT16, None)

    if debug_phase == "A":
        d1 = dbg("xnT", [128, 8, S], BF16)
        d2 = dbg("xnqT", [128, 8, TOK], BF16)
        k.wait_at("sync", vS, vS.n)
        A("sync", lambda e: e.dma_start(out=d1[:], in_=xnT[:]), inc=st, inc_n=16)
        A("sync", lambda e: e.dma_start(out=d2[:], in_=xnqT16[:]), inc=st, inc_n=16)
        k.wait_at("sync", st, st.n)
        k.emit()
        return nc, dbg_outs

    k.barrier(drain=[ld])
    sA.close()
    ln_pass = emit_ln_pass

    sB = ExitStack()
    SB_ = lambda n, s, d=F32: sB.enter_context(nc.sbuf_tensor(n, list(s), d))
    PB = lambda n, s, d=F32: sB.enter_context(nc.psum_tensor(n, list(s), d))

    wqh = SB_("b_wqh", [128, 8, DH], BF16)
    wkh = SB_("b_wkh", [128, 8, DH], BF16)
    wvh = SB_("b_wvh", [128, 8, DH], BF16)
    wrh = SB_("b_wrh", [128, 8, DH], BF16)
    ubv = SB_("b_ubv", [DH, 2 * NH])
    qu = SB_("b_qu", [DH, TOK], BF16)
    qv = SB_("b_qv", [DH, TOK], BF16)
    kvT = SB_("b_kvT", [128, S], BF16)
    vtok = SB_("b_vtok", [128, S // 128, 72], BF16)
    rcT = SB_("b_rcT", [DH, S + 256], BF16)
    pech = [SB_(f"b_pech{i}", [128, 8, 256], BF16) for i in range(2)]
    pbuf = [SB_(f"b_pbuf{i}", [128, S + 256], BF16) for i in range(2)]
    bdb = [SB_(f"b_bdb{i}", [128, S + 256], BF16) for i in range(2)]
    asbT = SB_("b_asbT", [128, 8, 256], BF16)
    sb_negt = SB_("b_negt", [128, 128], BF16)
    rrall = SB_("b_rrall", [1, TOK])
    bcball = SB_("b_bcball", [DH, TOK])
    ctxu65 = SB_("b_ctxu65", [DH + 1, TOK])
    ctxh = SB_("b_ctxh", [DH, TOK], BF16)
    sb_spq = SB_("b_spq", [128, 128], BF16)
    sb_pmq = SB_("b_pmq", [128, 128], BF16)

    bigS = [PB(f"b_bigs{i}", [128, 1024]) for i in range(2)]
    bps = [PB("b_ps0", [128, 512])]
    msc = [bigS[1][:, 0:512], bigS[1][:, 512:1024]]
    ps_t = [PB(f"b_pst{i}", [128, 128], BF16) for i in range(2)]
    ps_ctx = PB("b_psctx", [DH + 1, 256])

    A("sync", lambda e: e.dma_start(out=ubv[:, 0:NH], in_=ub_t[:]), inc=ld, inc_n=16)
    A("sync", lambda e: e.dma_start(out=ubv[:, NH:2 * NH], in_=vb_t[:]), inc=ld, inc_n=16)
    A("sync", lambda e: e.dma_start(out=sb_spq[:], in_=spillq[:]), inc=ld, inc_n=16)
    A("sync", lambda e: e.dma_start(out=sb_pmq[:], in_=pmq[:]), inc=ld, inc_n=16)
    A("sync", lambda e: e.dma_start(out=sb_negt[:], in_=negt[:]), inc=ld, inc_n=16)
    for cch in range(S // 128):
        A("vector", lambda e, cch=cch: e.memset(vtok[:, cch, 64:65], 1.0), inc=vS)

    wqkv_r = wqkv.rearrange("(c p) e -> p c e", p=128)
    wr_r = wr.rearrange("(c p) e -> p c e", p=128)

    M = {}
    BM = {}

    def bw(key):
        m = BM.get(key)
        return [m] if m else []

    def mw(key):
        m = M.get(key)
        return [m] if m else []

    nh_b = 0 if debug_phase in ("C0", "C0a", "C0b") else (1 if debug_phase == "B0" else NH)
    for h in range(nh_b):
        A("gpsimd", lambda e, h=h: e.dma_start(
            out=wqh[:], in_=wqkv_r[:, :, DH * h:DH * (h + 1)]),
          waits=mw("proj_done"), inc=gld, inc_n=16)
        A("gpsimd", lambda e, h=h: e.dma_start(
            out=wkh[:], in_=wqkv_r[:, :, D + DH * h:D + DH * (h + 1)]), inc=gld, inc_n=16)
        A("gpsimd", lambda e, h=h: e.dma_start(
            out=wvh[:], in_=wqkv_r[:, :, 2 * D + DH * h:2 * D + DH * (h + 1)]),
          inc=gld, inc_n=16)
        A("gpsimd", lambda e, h=h: e.dma_start(
            out=wrh[:], in_=wr_r[:, :, DH * h:DH * (h + 1)]), inc=gld, inc_n=16)
        w_loaded = (gld, gld.n)

        def bproj(w_sb, src, ntok, evac, free_mark):
            for j in range(ntok // 512):
                pj = j % 2
                for cch in range(8):
                    A("tensor", lambda e, j=j, cch=cch, pj=pj, w_sb=w_sb, src=src:
                      e.matmul(msc[pj][0:64, :], w_sb[:, cch, :],
                               src[:, cch, 512 * j:512 * (j + 1)],
                               start=(cch == 0), stop=(cch == 7)),
                      waits=(bw(("b", pj)) + [w_loaded] + (mw(free_mark) if j < 2 else [])
                             if cch == 0 else []), inc=tS)
                mmk = (tS, tS.n)
                evac(j, msc[pj][0:64, :], mmk)

        def tpos(t):
            return 2 * (15 - t) if t >= 8 else 2 * t + 1

        def evac_q(j, ps, mmk, h=h):
            for c4 in range(4):
                t4 = 4 * j + c4
                A("scalar", lambda e, c4=c4, t4=t4, ps=ps: e.activation(
                    qu[:, 128 * tpos(t4):128 * tpos(t4) + 128],
                    ps[0:64, 128 * c4:128 * (c4 + 1)], AF.Identity,
                    bias=ubv[:, h:h + 1], scale=1.0),
                  waits=([mmk, (ld, 32)] + (mw("qu_free") if j == 0 else [])
                         if c4 == 0 else []), inc=sS)
            A("scalar", lambda e, j=j, ps=ps: e.activation(
                qv[:, 512 * j:512 * (j + 1)], ps, AF.Identity,
                bias=ubv[:, NH + h:NH + h + 1], scale=1.0), inc=sS)
            BM[("b", j % 2)] = (sS, sS.n)
        bproj(wqh, xnqT16, TOK, evac_q, "qu_free")

        def evac_k(j, ps, mmk):
            if j % 2 == 0:
                A("vector", lambda e, j=j, ps=ps: e.tensor_copy(
                    kvT[0:64, 512 * j:512 * (j + 1)], ps),
                  waits=[mmk] + (mw("kvT_free") if j == 0 else []), inc=vS)
                BM[("b", j % 2)] = (vS, vS.n)
            else:
                A("scalar", lambda e, j=j, ps=ps: e.activation(
                    kvT[0:64, 512 * j:512 * (j + 1)], ps, AF.Identity),
                  waits=[mmk], inc=sS)
                BM[("b", j % 2)] = (sS, sS.n)
        bproj(wkh, xnT, S, evac_k, "kvT_free")

        def evac_v(j, ps, mmk):
            if j % 2 == 0:
                A("vector", lambda e, j=j, ps=ps: e.tensor_copy(
                    kvT[64:128, 512 * j:512 * (j + 1)], ps), waits=[mmk], inc=vS)
                BM[("b", j % 2)] = (vS, vS.n)
            else:
                A("scalar", lambda e, j=j, ps=ps: e.activation(
                    kvT[64:128, 512 * j:512 * (j + 1)], ps, AF.Identity),
                  waits=[mmk], inc=sS)
                BM[("b", j % 2)] = (sS, sS.n)
        bproj(wvh, xnT, S, evac_v, None)
        kv_ready = (vS, vS.n)

        for cch in range(S // 128):
            pb = cch % 2
            A("tensor", lambda e, cch=cch, pb=pb: e.transpose(
                ps_t[pb][:], kvT[:, 128 * cch:128 * (cch + 1)], sb_ident16[:]),
              waits=[kv_ready] + bw(("t", pb)) + (mw("vtok_free") if cch == 0 else []),
              inc=tS)
            A("vector", lambda e, cch=cch, pb=pb: e.tensor_copy(
                vtok[:, cch, 0:64], ps_t[pb][:, 64:128]),
              waits=[(tS, tS.n)], inc=vS)
            BM[("t", pb)] = (vS, vS.n)

        for wch in range(cdiv(S + 256, 256)):
            cw = min(256, S + 256 - 256 * wch)
            pe_b = wch % 2
            A("sync", lambda e, wch=wch, cw=cw, pe_b=pe_b: e.dma_start(
                out=pech[pe_b][:, :, 0:cw], in_=peT[:, :, 256 * wch:256 * wch + cw]),
              waits=bw(("pe", pe_b)), inc=ld, inc_n=16)
            pk_ld = (ld, ld.n)
            pj = wch % 2
            for cch in range(8):
                A("tensor", lambda e, wch=wch, cw=cw, cch=cch, pj=pj, pe_b=pe_b: e.matmul(
                    msc[pj][0:64, 0:cw], wrh[:, cch, :], pech[pe_b][:, cch, 0:cw],
                    start=(cch == 0), stop=(cch == 7)),
                  waits=([pk_ld] + bw(("b", pj)) + (mw("rcT_free") if wch == 0 else [])
                         if cch == 0 else []), inc=tS)
            BM[("pe", pe_b)] = (tS, tS.n)
            if wch % 2 == 0:
                A("vector", lambda e, wch=wch, cw=cw, pj=pj: e.tensor_copy(
                    rcT[:, 256 * wch:256 * wch + cw], msc[pj][0:64, 0:cw]),
                  waits=[(tS, tS.n)], inc=vS)
                BM[("b", pj)] = (vS, vS.n)
            else:
                A("scalar", lambda e, wch=wch, cw=cw, pj=pj: e.activation(
                    rcT[:, 256 * wch:256 * wch + cw], msc[pj][0:64, 0:cw],
                    AF.Identity),
                  waits=[(tS, tS.n)], inc=sS)
                BM[("b", pj)] = (sS, sS.n)
        rc_ready = (vS, vS.n)
        M["proj_done"] = (tS, tS.n)

        def p_stage(t, pp, off, pkey):
            E = 256 * (t + 1)
            W0 = S - 128 - 256 * t
            nw = cdiv(E, 512)
            for w in range(nw):
                cw = min(512, E - 512 * w)
                pj = w % 2
                A("tensor", lambda e, w=w, cw=cw, t=t, W0=W0, pj=pj: e.matmul(
                    msc[pj][:, 0:cw], qv[:, 128 * t:128 * (t + 1)],
                    rcT[:, W0 + 512 * w:W0 + 512 * w + cw], start=True, stop=True),
                  waits=[rc_ready] + bw(("b", pj)), inc=tS)
                mmk = (tS, tS.n)
                if w % 2 == 0:
                    A("vector", lambda e, w=w, cw=cw, pp=pp, off=off, pj=pj:
                      e.tensor_copy(
                        pbuf[pp][:, off + 512 * w:off + 512 * w + cw],
                        msc[pj][:, 0:cw]),
                      waits=[mmk] + (mw(("pbuf_free", pp)) if w == 0 else []), inc=vS)
                    BM[("b", pj)] = (vS, vS.n)
                else:
                    A("scalar", lambda e, w=w, cw=cw, pp=pp, off=off, pj=pj:
                      e.activation(
                        pbuf[pp][:, off + 512 * w:off + 512 * w + cw],
                        msc[pj][:, 0:cw], AF.Identity),
                      waits=[mmk], inc=sS)
                    BM[("b", pj)] = (sS, sS.n)
            A("vector", lambda e, E=E, pp=pp, off=off: e.tensor_tensor(
                pbuf[pp][:, off + E - 128:off + E],
                pbuf[pp][:, off + E - 128:off + E], sb_pmq[:],
                ALU.add), waits=W(ld), inc=vS)
            pev = (vS, vS.n)
            A("sync", lambda e, E=E, pp=pp, off=off: e.dma_start(
                out=AP(pskew[pp].tensor, 128 * off, [[E, 128], [1, E]]),
                in_=pbuf[pp][:, off:off + E]),
              waits=[pev] + mw(("skew_done", pp)), inc=pS, inc_n=16)
            pok = (pS, pS.n)
            A("gpsimd", lambda e, E=E, pp=pp, off=off: e.dma_start(
                out=bdb[pp][:, off:off + E],
                in_=AP(pskew[pp].tensor, 128 * off + 127, [[E - 1, 128], [1, E]])),
              waits=[pok] + mw(("bd_used", pp)), inc=bS, inc_n=16)
            A("vector", lambda e, E=E, pp=pp, off=off: e.tensor_tensor(
                bdb[pp][:, off + E - 128:off + E],
                bdb[pp][:, off + E - 128:off + E], sb_spq[:],
                ALU.add), waits=[(bS, bS.n)], inc=vS)
            M[("bd_ready", pkey)] = (vS, vS.n)
            M[("skew_done", pp)] = (bS, bS.n)
            if off == 0:
                M[("pbuf_free", pp)] = pok

        def s_pair(u, pos):
            pp = pos % 2
            tl, ts = 15 - u, u
            El, Es = 256 * (tl + 1), 256 * (ts + 1)
            ncc, ncs = El // 128, Es // 128
            psS = (bigS[0][:, 0:512], bigS[0][:, 512:1024], bps[0][:, 0:512])
            ng = cdiv(ncc, 2)

            def avs(g):
                ncg = min(2, ncc - 2 * g)
                for c2 in range(ncg):
                    kc = 2 * g + c2
                    A("tensor", lambda e, kc=kc, ncc=ncc: e.matmul(
                        ps_ctx[:], vtok[:, kc, 0:65], asbT[:, kc % 8, :],
                        start=(kc == 0), stop=(kc == ncc - 1)),
                      waits=([M[("exp", u, g)]] if c2 == 0 else [])
                            + (mw("ctx_free") if kc == 0 else []), inc=tS)
                    M[("asbs", kc % 8)] = (tS, tS.n)

            prevg = None
            for g in range(ng):
                ncg = min(2, ncc - 2 * g)
                bank = psS[g % 3]
                for c2 in range(ncg):
                    kc = 2 * g + c2
                    A("tensor", lambda e, kc=kc, c2=c2, bank=bank, u=u: e.matmul(
                        bank[:, 256 * c2:256 * c2 + 256],
                        kvT[0:64, 128 * kc:128 * (kc + 1)],
                        qu[:, 256 * u:256 * u + 256],
                        start=True, stop=False),
                      waits=((bw(("G", g % 3))
                              + ([M[("bd_ready", (u, 1))]] if g == 0 else []))
                             if c2 == 0 else []), inc=tS)
                    A("tensor", lambda e, kc=kc, c2=c2, bank=bank, pp=pp: e.matmul(
                        bank[:, 256 * c2:256 * c2 + 128],
                        bdb[pp][:, 128 * kc:128 * (kc + 1)], sb_ident16[:],
                        start=False, stop=False), inc=tS)
                    if kc < ncs:
                        A("tensor", lambda e, kc=kc, c2=c2, bank=bank, pp=pp, El=El:
                          e.matmul(
                            bank[:, 256 * c2 + 128:256 * c2 + 256],
                            bdb[pp][:, El + 128 * kc:El + 128 * (kc + 1)],
                            sb_ident16[:], start=False, stop=True), inc=tS)
                    else:
                        A("tensor", lambda e, kc=kc, c2=c2, bank=bank: e.matmul(
                            bank[:, 256 * c2 + 128:256 * c2 + 256],
                            sb_negt[:], sb_ident16[:],
                            start=False, stop=True), inc=tS)
                ek = (tS, tS.n)
                sl = (2 * g) % 8
                A("scalar", lambda e, g=g, ncg=ncg, sl=sl, bank=bank: e.activation(
                    asbT[:, sl:sl + ncg, :], bank[:, 0:256 * ncg], AF.Exp,
                    scale=0.125),
                  waits=[ek] + mw(("asbs", sl)) + mw(("asbs", sl + 1)), inc=sS)
                M[("exp", u, g)] = (sS, sS.n)
                BM[("G", g % 3)] = (sS, sS.n)
                if prevg is not None:
                    avs(prevg)
                prevg = g
            avs(prevg)
            avk = (tS, tS.n)
            M[("bd_used", pp)] = avk
            A("vector", lambda e, tl=tl: e.tensor_copy(
                ctxu65[:, 128 * tl:128 * (tl + 1)], ps_ctx[:, 0:128]),
              waits=[avk], inc=vS)
            A("vector", lambda e, ts=ts: e.tensor_copy(
                ctxu65[:, 128 * ts:128 * (ts + 1)], ps_ctx[:, 128:256]), inc=vS)
            M["ctx_free"] = (vS, vS.n)

        p_stage(15, 0, 0, (0, 0))
        p_stage(0, 0, 256 * 16, (0, 1))
        for u in range(1, 8):
            p_stage(15 - u, u % 2, 0, (u, 0))
            p_stage(u, u % 2, 256 * (16 - u), (u, 1))
            s_pair(u - 1, u - 1)
        s_pair(7, 7)

        fv()
        A("vector", lambda e: e.tensor_copy(rrall[:], ctxu65[64:65, :]), inc=vS)
        fv()
        fv()
        A("vector", lambda e: e.reciprocal(rrall[:], rrall[:]), inc=vS)
        fv()
        fv()
        rr_ready = (vS, vS.n)
        for w in range(4):
            pj = w % 2
            A("tensor", lambda e, w=w, pj=pj: e.matmul(
                msc[pj][0:64, 0:512], sb_ones_r[0:1, 0:64],
                rrall[0:1, 512 * w:512 * (w + 1)], start=True, stop=True),
              waits=[rr_ready] + bw(("b", pj)), inc=tS)
            A("vector", lambda e, w=w, pj=pj: e.tensor_copy(
                bcball[:, 512 * w:512 * (w + 1)], msc[pj][0:64, 0:512]),
              waits=[(tS, tS.n)], inc=vS)
            BM[("b", pj)] = (vS, vS.n)
        A("vector", lambda e: e.tensor_tensor(ctxh[:], ctxu65[0:64, :], bcball[:],
                                              ALU.mult), inc=vS)
        A("sync", lambda e, h=h: e.dma_start(
            out=ctx_park[64 * (h % 2):64 * (h % 2) + 64, h // 2, :], in_=ctxh[:]),
          waits=[(vS, vS.n)], inc=ld, inc_n=16)
        M["ctx_free"] = (ld, ld.n)
        M["qu_free"] = (tS, tS.n)
        M["kvT_free"] = (tS, tS.n)
        M["vtok_free"] = (tS, tS.n)
        M["rcT_free"] = (tS, tS.n)

    if debug_phase == "B0":
        d1 = dbg("ctxu", [DH + 1, TOK])
        d2 = dbg("rs", [1, TOK])
        k.wait_at("sync", vS, vS.n)
        k.wait_at("sync", bS, bS.n)
        A("sync", lambda e: e.dma_start(out=d1[:], in_=ctxu65[:]), inc=st, inc_n=16)
        A("sync", lambda e: e.dma_start(out=d2[:], in_=rrall[:]), inc=st, inc_n=16)
        k.wait_at("sync", st, st.n)
        k.emit()
        return nc, dbg_outs

    if debug_phase == "B":
        d1 = dbg("ctx", [128, 8, TOK], BF16)
        k.wait_at("sync", ld, ld.n)
        A("sync", lambda e: e.dma_start(out=d1[:], in_=ctx_park[:]), inc=st, inc_n=16)
        k.wait_at("sync", st, st.n)
        k.emit()
        return nc, dbg_outs


    k.barrier(drain=[ld, pS, bS, gld])
    sB.close()

    sC = ExitStack()
    SC = lambda n, s, d=F32: sC.enter_context(nc.sbuf_tensor(n, list(s), d))
    PC = lambda n, s, d=F32: sC.enter_context(nc.psum_tensor(n, list(s), d))

    rwq = SC("c_rwq", [128, 8, DH], BF16)
    rwk = SC("c_rwk", [128, 8, DH], BF16)
    rwv = SC("c_rwv", [128, 8, DH], BF16)
    gqh = SC("c_gqh", [64, 512])
    gkh = SC("c_gkh", [64, 512])
    qgT = SC("c_qgT", [DH, S], BF16)
    rkvT = SC("c_rkvT", [128, S], BF16)
    kvtok = SC("c_kvtok", [128, S // 128, 128], BF16)
    stT = SC("c_stT", [64, 64])
    stT16 = SC("c_stT16", [64, 64], BF16)
    sc16 = [SC(f"c_sc16{i}", [128, 128], BF16) for i in range(2)]
    atr = [SC(f"c_atr{i}", [128, 128], BF16) for i in range(2)]
    ob = [SC(f"c_ob{i}", [64, TOK]) for i in range(2)]
    stats = SC("c_stats", [64, 64])
    sqsc = SC("c_sqsc", [64, 128])
    mrow = SC("c_mrow", [1, 8])
    rncol = SC("c_rncol", [64, 2])
    scbi = SC("c_scbi", [64, 2])
    gw = SC("c_gw", [64, 2 * NH])
    cnh = SC("c_cnh", [64, TOK], BF16)

    cps = [PC(f"c_ps{i}", [128, 512]) for i in range(2)]
    ctk = [PC(f"c_ptk{i}", [128, 128], BF16) for i in range(2)]
    cpo = [PC(f"c_po{i}", [64, 128]) for i in range(2)]
    cpu = [PC(f"c_pu{i}", [64, 128]) for i in range(2)]
    ps_u = cpu[0]

    A("sync", lambda e: e.dma_start(out=gw[:, 0:NH], in_=gnw_t[:]), inc=ld, inc_n=16)
    A("sync", lambda e: e.dma_start(out=gw[:, NH:2 * NH], in_=gnb_t[:]), inc=ld, inc_n=16)

    CM = {}

    def cmw(key):
        m = CM.get(key)
        return [m] if m else []

    def projc(w_sb, src, ntok, evac, wk_ld):
        for j in range(ntok // 512):
            pj = j % 2
            for cch in range(8):
                A("tensor", lambda e, j=j, cch=cch, pj=pj: e.matmul(
                    cps[pj][0:64, :], w_sb[:, cch, :], src[:, cch, 512 * j:512 * (j + 1)],
                    start=(cch == 0), stop=(cch == 7)),
                  waits=([wk_ld] + cmw(("cb", pj)) if cch == 0 else []), inc=tS)
            evac(j, cps[pj][0:64, :], (tS, tS.n))
            CM[("cb", pj)] = (vS, vS.n)

    c_dbg = debug_phase in ("C0", "C0a", "C0b")
    for h in range(1 if c_dbg else NH):
        g = GAMMAS[h]
        gC = g ** 128
        A("gpsimd", lambda e, h=h: e.dma_start(
            out=rwq[:], in_=wq_a[h].rearrange("(c p) e -> p c e", p=128)),
          waits=cmw("w_free"), inc=gld, inc_n=16)
        A("gpsimd", lambda e, h=h: e.dma_start(
            out=rwk[:], in_=wk_a[h].rearrange("(c p) e -> p c e", p=128)), inc=gld, inc_n=16)
        A("gpsimd", lambda e, h=h: e.dma_start(
            out=rwv[:], in_=wv_a[h].rearrange("(c p) e -> p c e", p=128)), inc=gld, inc_n=16)
        cw_ld = (gld, gld.n)
        A("sync", lambda e, h=h: e.dma_start(out=gqh[:], in_=gq_t[:, h, :]),
          waits=cmw("g_free"), inc=ld, inc_n=16)
        A("sync", lambda e, h=h: e.dma_start(out=gkh[:], in_=gk_t[:, h, :]), inc=ld, inc_n=16)
        g_ld = (ld, ld.n)

        def evac_qg(j, ps, mmk):
            A("vector", lambda e, j=j, ps=ps: e.tensor_tensor(
                qgT[:, 512 * j:512 * (j + 1)], ps, gqh[:], ALU.mult),
              waits=[mmk, g_ld] + (cmw("qg_free") if j == 0 else []), inc=vS)
        projc(rwq, xnT, S, evac_qg, cw_ld)

        def evac_kg(j, ps, mmk):
            A("vector", lambda e, j=j, ps=ps: e.tensor_tensor(
                rkvT[0:64, 512 * j:512 * (j + 1)], ps, gkh[:], ALU.mult),
              waits=[mmk] + (cmw("kv_free") if j == 0 else []), inc=vS)
        projc(rwk, xnT, S, evac_kg, cw_ld)

        def evac_rv(j, ps, mmk):
            A("vector", lambda e, j=j, ps=ps: e.tensor_copy(
                rkvT[64:128, 512 * j:512 * (j + 1)], ps), waits=[mmk], inc=vS)
        projc(rwv, xnT, S, evac_rv, cw_ld)
        qgkv_ready = (vS, vS.n)
        CM["w_free"] = (tS, tS.n)

        A("vector", lambda e: e.memset(stT[:], 0.0), waits=cmw("st_free"), inc=vS)
        A("vector", lambda e: e.memset(stT16[:], 0.0), inc=vS)
        st_ready = (vS, vS.n)

        for j in range(0 if debug_phase == "C0a" else S // 128):
            jb = j % 2
            A("tensor", lambda e, j=j, jb=jb: e.transpose(
                ctk[jb][:], rkvT[:, 128 * j:128 * (j + 1)], sb_ident16[:]),
              waits=[qgkv_ready] + cmw(("tk", jb)), inc=tS)
            A("vector", lambda e, j=j, jb=jb: e.tensor_copy(kvtok[:, j, :], ctk[jb][:]),
              waits=[(tS, tS.n)], inc=vS)
            A("vector", lambda e, j=j, gC=gC: e.tensor_scalar_mul(
                kvtok[:, j, 0:64], kvtok[:, j, 0:64], gC), inc=vS)
            kvtok_k = (vS, vS.n)
            A("tensor", lambda e, j=j, jb=jb: e.matmul(
                cps[jb][0:128, 0:128], qgT[:, 128 * j:128 * (j + 1)],
                rkvT[0:64, 128 * j:128 * (j + 1)],
                start=True, stop=True), waits=[qgkv_ready] + cmw(("cb", jb)), inc=tS)
            A("vector", lambda e, jb=jb: e.tensor_tensor(
                sc16[jb][:], cps[jb][0:128, 0:128], sb_rmask[:], ALU.mult),
              waits=[(tS, tS.n)] + cmw(("sc16u", jb)), inc=vS)
            CM[("cb", jb)] = (vS, vS.n)
            A("tensor", lambda e, jb=jb: e.transpose(ctk[jb][:], sc16[jb][:],
                                                     sb_ident16[:]),
              waits=[(vS, vS.n)], inc=tS)
            CM[("sc16u", jb)] = (tS, tS.n)
            A("vector", lambda e, jb=jb: e.tensor_copy(atr[jb][:], ctk[jb][:]),
              waits=[(tS, tS.n)] + cmw(("atru", jb)), inc=vS)
            CM[("tk", jb)] = (vS, vS.n)
            atr_k = (vS, vS.n)
            A("tensor", lambda e, j=j, jb=jb: e.matmul(
                cpo[jb][:, 0:128], kvtok[:, j, 64:128], atr[jb][:],
                start=True, stop=False),
              waits=[atr_k, kvtok_k if j == 0 else st_ready] + cmw(("ob", jb)), inc=tS)
            A("tensor", lambda e, j=j, jb=jb: e.matmul(
                cpo[jb][:, 0:128], stT16[:], qgT[:, 128 * j:128 * (j + 1)],
                start=False, stop=True), inc=tS)
            CM[("atru", jb)] = (tS, tS.n)
            ok_ = (tS, tS.n)
            A("vector", lambda e, j=j, jb=jb: e.tensor_copy(
                ob[jb][:, 128 * (j // 2):128 * (j // 2 + 1)], cpo[jb][:, 0:128]),
              waits=[ok_], inc=vS)
            CM[("ob", jb)] = (vS, vS.n)
            A("vector", lambda e, j=j, jb=jb: e.reduce_sum(
                stats[:, j:j + 1], ob[jb][:, 128 * (j // 2):128 * (j // 2 + 1)],
                axis=AX.X), inc=vS)
            A("scalar", lambda e, j=j, jb=jb: e.activation(
                sqsc[:], ob[jb][:, 128 * (j // 2):128 * (j // 2 + 1)], AF.Square,
                accum_out=stats[:, 32 + j:33 + j]), waits=[(vS, vS.n)], inc=sS)
            A("tensor", lambda e, j=j, jb=jb: e.matmul(
                cpu[jb][:, 0:64], kvtok[:, j, 0:64], kvtok[:, j, 64:128],
                start=True, stop=True),
              waits=[kvtok_k] + cmw(("ub", jb)), inc=tS)
            upd_k = (tS, tS.n)
            A("vector", lambda e, gC=gC, jb=jb: e.scalar_tensor_tensor(
                stT16[:], stT[:], gC, cpu[jb][:, 0:64], ALU.mult, ALU.add),
              waits=[upd_k, st_ready], inc=vS)
            A("vector", lambda e, gC=gC, jb=jb: e.scalar_tensor_tensor(
                stT[:], stT[:], gC, cpu[jb][:, 0:64], ALU.mult, ALU.add), inc=vS)
            CM[("ub", jb)] = (vS, vS.n)
            st_ready = (vS, vS.n)

        if debug_phase in ("C0a", "C0b"):
            continue
        fs()
        A("vector", lambda e: e.reduce_sum(rncol[:, 0:1], stats[:, 0:32], axis=AX.X),
          waits=W(sS), inc=vS)
        A("vector", lambda e: e.reduce_sum(rncol[:, 1:2], stats[:, 32:64], axis=AX.X),
          inc=vS)
        fv()
        A("tensor", lambda e: e.matmul(ps_u[0:1, 64:66], sb_ones_c[0:64, 0:1],
                                       rncol[:, 0:2], start=True, stop=True),
          waits=W(vS), inc=tS)
        A("vector", lambda e: e.tensor_copy(mrow[0:1, 0:2], ps_u[0:1, 64:66]),
          waits=W(tS), inc=vS)
        fv()
        A("vector", lambda e: e.tensor_scalar_mul(mrow[0:1, 2:4], mrow[0:1, 0:2],
                                                  1.0 / (64.0 * S)), inc=vS)
        fv()
        A("vector", lambda e: e.tensor_tensor(mrow[0:1, 4:5], mrow[0:1, 2:3],
                                              mrow[0:1, 2:3], ALU.mult), inc=vS)
        fv()
        A("vector", lambda e: e.tensor_sub(mrow[0:1, 5:6], mrow[0:1, 3:4],
                                           mrow[0:1, 4:5]), inc=vS)
        fv()
        A("vector", lambda e: e.tensor_scalar_add(mrow[0:1, 5:6], mrow[0:1, 5:6], EPS),
          inc=vS)
        fv()
        A("scalar", lambda e: e.activation(mrow[0:1, 6:7], mrow[0:1, 5:6], AF.Sqrt),
          waits=W(vS), inc=sS)
        fs()
        A("vector", lambda e: e.reciprocal(mrow[0:1, 7:8], mrow[0:1, 6:7]),
          waits=W(sS), inc=vS)
        fv()
        A("vector", lambda e: e.tensor_scalar_mul(mrow[0:1, 2:3], mrow[0:1, 2:3], -1.0),
          inc=vS)
        A("vector", lambda e: e.tensor_copy(mrow[0:1, 3:4], mrow[0:1, 7:8]), inc=vS)
        fv()
        A("tensor", lambda e: e.matmul(ps_u[0:64, 66:68], sb_ones_r[0:1, 0:64],
                                       mrow[0:1, 2:4], start=True, stop=True),
          waits=W(vS), inc=tS)
        A("vector", lambda e: e.tensor_copy(rncol[:], ps_u[0:64, 66:68]),
          waits=W(tS), inc=vS)
        fv()
        A("vector", lambda e, h=h: e.tensor_tensor(scbi[:, 0:1], gw[:, h:h + 1],
                                                   rncol[:, 1:2], ALU.mult), inc=vS)
        fv()
        A("vector", lambda e, h=h: e.scalar_tensor_tensor(
            scbi[:, 1:2], scbi[:, 0:1], rncol[:, 0:1], gw[:, NH + h:NH + h + 1],
            ALU.mult, ALU.add), inc=vS)
        A("vector", lambda e: e.tensor_scalar(ob[0][:], ob[0][:], sb_sel[:, 0:1], None,
                                              ALU.mult), inc=vS)
        A("vector", lambda e: e.scalar_tensor_tensor(ob[0][:], ob[1][:], sb_sel[:, 1:2],
                                                     ob[0][:], ALU.mult, ALU.add), inc=vS)
        fv()
        A("scalar", lambda e: e.activation(cnh[:], ob[0][:], AF.Identity,
                                           bias=scbi[:, 1:2], scale=scbi[:, 0:1]),
          waits=W(vS, ld), inc=sS)
        A("sync", lambda e, h=h: e.dma_start(
            out=cn_park[64 * (h % 2):64 * (h % 2) + 64, h // 2, :], in_=cnh[:]),
          waits=W(sS), inc=ld, inc_n=16)
        CM["g_free"] = (vS, vS.n)
        CM["qg_free"] = (tS, tS.n)
        CM["kv_free"] = (tS, tS.n)
        CM["st_free"] = (vS, vS.n)

    if debug_phase in ("C0a", "C0b"):
        d1 = dbg("qg", [DH, S], BF16)
        d2 = dbg("rkv", [128, S], BF16)
        k.wait_at("sync", vS, vS.n)
        A("sync", lambda e: e.dma_start(out=d1[:], in_=qgT[:]), inc=st, inc_n=16)
        A("sync", lambda e: e.dma_start(out=d2[:], in_=rkvT[:]), inc=st, inc_n=16)
        if debug_phase == "C0b":
            d3 = dbg("ob0", [DH, TOK])
            d4 = dbg("ob1", [DH, TOK])
            d5 = dbg("stats", [DH, 64])
            A("sync", lambda e: e.dma_start(out=d3[:], in_=ob[0][:]), inc=st, inc_n=16)
            A("sync", lambda e: e.dma_start(out=d4[:], in_=ob[1][:]), inc=st, inc_n=16)
            A("sync", lambda e: e.dma_start(out=d5[:], in_=stats[:]), inc=st, inc_n=16)
        k.wait_at("sync", st, st.n)
        k.emit()
        return nc, dbg_outs

    if debug_phase in ("C", "C0"):
        dm = dbg("mrow", [1, 8])
        dsc = dbg("scbi", [DH, 2])
        k.wait_at("sync", vS, vS.n)
        A("sync", lambda e: e.dma_start(out=dm[:], in_=mrow[:]), inc=st, inc_n=16)
        A("sync", lambda e: e.dma_start(out=dsc[:], in_=scbi[:]), inc=st, inc_n=16)
        d1 = dbg("cn", [128, 8, TOK], BF16)
        k.wait_at("sync", ld, ld.n)
        A("sync", lambda e: e.dma_start(out=d1[:], in_=cn_park[:]), inc=st, inc_n=16)
        k.wait_at("sync", st, st.n)
        k.emit()
        return nc, dbg_outs

    k.barrier(drain=[ld, gld])
    sC.close()
    sAC.close()

    sEF = ExitStack()
    PSB = [sEF.enter_context(nc.psum_tensor(f"e_psb{i}", [128, 512], F32))
           for i in range(8)]

    def mk_lnbufs(stack, tag):
        SE_ = lambda n, s, d=F32: stack.enter_context(nc.sbuf_tensor(n, list(s), d))
        return dict(
            f32t=SE_(f"{tag}_f32t", [128, 8, 512]),
            sqch=[SE_(f"{tag}_sqch{i}", [128, 512]) for i in range(2)],
            lrow=SE_(f"{tag}_lrow", [1, 8, 512]),
            bcn=SE_(f"{tag}_bcn", [128, 512]),
            lntmp=SE_(f"{tag}_lntmp", [128, 512]),
        )

    def ln_feature(lb, src_fn, j, ln_idx, out16_fn, park32_ap, park16_ap, park16_src):
        f32t, sqch, lrow, bcn, lntmp = (lb["f32t"], lb["sqch"], lb["lrow"],
                                        lb["bcn"], lb["lntmp"])
        for cch in range(8):
            A("tensor", lambda e, cch=cch: e.matmul(
                PSB[0][0:1, :], sb_ones_c[:], src_fn(cch),
                start=(cch == 0), stop=(cch == 7)),
              waits=(W(vS, sS, ld) if cch == 0 else []), inc=tS)
        for cch in range(8):
            pb = cch % 2
            A("scalar", lambda e, cch=cch, pb=pb: e.activation(
                sqch[pb][:], src_fn(cch), AF.Square), waits=W(vS, tS, ld), inc=sS)
            A("tensor", lambda e, cch=cch, pb=pb: e.matmul(
                PSB[1][0:1, :], sb_ones_c[:], sqch[pb][:],
                start=(cch == 0), stop=(cch == 7)), waits=W(sS), inc=tS)
        A("vector", lambda e: e.tensor_scalar_mul(lrow[0:1, 0, :], PSB[0][0:1, :], 1.0 / D),
          waits=W(tS), inc=vS)
        A("vector", lambda e: e.tensor_scalar(lrow[0:1, 1, :], PSB[1][0:1, :], 1.0 / D,
                                              EPS, ALU.mult, ALU.add), inc=vS)
        A("vector", lambda e: e.tensor_tensor(lrow[0:1, 2, :], lrow[0:1, 0, :],
                                              lrow[0:1, 0, :], ALU.mult), inc=vS)
        fv()
        fv()
        A("vector", lambda e: e.tensor_sub(lrow[0:1, 5, :], lrow[0:1, 1, :],
                                           lrow[0:1, 2, :]), inc=vS)
        fv()
        fv()
        A("scalar", lambda e: e.activation(lrow[0:1, 3, :], lrow[0:1, 5, :], AF.Sqrt),
          waits=W(vS), inc=sS)
        fs()
        fs()
        A("vector", lambda e: e.reciprocal(lrow[0:1, 3, :], lrow[0:1, 3, :]),
          waits=W(sS), inc=vS)
        fv()
        fv()
        A("vector", lambda e: e.scalar_tensor_tensor(
            lrow[0:1, 4, :], lrow[0:1, 0, :], -1.0, lrow[0:1, 3, :],
            ALU.mult, ALU.mult), inc=vS)
        fv()
        fv()
        A("tensor", lambda e: e.matmul(PSB[0][:], sb_ones_r[:], lrow[0:1, 3, :],
                                       start=True, stop=True), waits=W(vS), inc=tS)
        A("tensor", lambda e: e.matmul(PSB[1][:], sb_ones_r[:], lrow[0:1, 4, :],
                                       start=True, stop=True), inc=tS)
        A("vector", lambda e: e.tensor_copy(bcn[:], PSB[1][:]), waits=W(tS), inc=vS)
        for cch in range(8):
            A("vector", lambda e, cch=cch: e.tensor_tensor(
                lntmp[:], src_fn(cch), PSB[0][:], ALU.mult), waits=W(sS, ld), inc=vS)
            A("vector", lambda e: e.tensor_add(lntmp[:], lntmp[:], bcn[:]), inc=vS)
            A("scalar", lambda e, cch=cch: e.activation(
                f32t[:, cch, :], lntmp[:], AF.Identity,
                bias=sb_lnb[:, ln_idx, cch:cch + 1], scale=sb_lng[:, ln_idx, cch:cch + 1]),
              waits=W(vS, ld, tS), inc=sS)
            if out16_fn is not None:
                A("vector", lambda e, cch=cch: e.tensor_copy(
                    out16_fn(cch), f32t[:, cch, :]), waits=W(sS, ld), inc=vS)
        if park32_ap is not None:
            A("sync", lambda e: e.dma_start(
                out=park32_ap[:, :, 512 * j:512 * (j + 1)], in_=f32t[:]),
              waits=W(sS, vS), inc=ld, inc_n=16)
        if park16_ap is not None:
            A("sync", lambda e: e.dma_start(
                out=park16_ap[:, :, 512 * j:512 * (j + 1)], in_=park16_src),
              waits=W(vS), inc=ld, inc_n=16)

    we1b = scr("we1b", [NE, 128, 8, FFD], BF16)
    we2b = scr("we2b", [NE, 128, 32, D], BF16)
    for ee in range(NE):
        A("gpsimd", lambda e, ee=ee: e.dma_start(
            out=we1b[ee], in_=we1[ee].rearrange("(c p) f -> p c f", p=128)),
          inc=gld, inc_n=16)
        A("gpsimd", lambda e, ee=ee: e.dma_start(
            out=we2b[ee], in_=we2[ee].rearrange("(c p) d2 -> p c d2", p=128)),
          inc=gld, inc_n=16)
    wpark_k = (gld, gld.n)

    sE25 = ExitStack()
    resf = sE25.enter_context(nc.sbuf_tensor("e_resf", [128, 8, TOK], F32))
    sE12 = ExitStack()
    SE1 = lambda n, s, d=F32: sE12.enter_context(nc.sbuf_tensor(n, list(s), d))
    xnqT32 = SE1("e_xnq32", [128, 8, TOK])
    ext = [SE1(f"e_xt{i}", [128, D]) for i in range(2)]
    exc = [SE1(f"e_xc{i}", [128, D]) for i in range(2)]
    esq = SE1("e_sqs", [128, D])
    e_stat = SE1("e_stat", [128, 8 * 16])

    ln_pass(lambda j: xq[128 * j:128 * (j + 1), :], 16, e_stat, ext, exc, esq,
            [PSB[0][:, 0:128], PSB[1][:, 0:128]], None, xnqT32)

    wo_sb = SE1("e_wo", [128, 8, D], BF16)
    ctxch = [SE1(f"e_ctxch{i}", [128, 8, 512], BF16) for i in range(2)]
    wo_r = wo.rearrange("(c p) e -> p c e", p=128)
    A("gpsimd", lambda e: e.dma_start(out=wo_sb[:], in_=wo_r[:]), inc=gld, inc_n=16)

    EM = {}

    def emw(key):
        m = EM.get(key)
        return [m] if m else []

    wo_ld = (gld, gld.n)
    for j in range(TOK // 512):
        A("sync", lambda e, j=j: e.dma_start(
            out=ctxch[j % 2][:], in_=ctx_park[:, :, 512 * j:512 * (j + 1)]),
          waits=emw(("ctxch", j % 2)), inc=ld, inc_n=16)
        ctx_ld = (ld, ld.n)
        for m in range(8):
            pm = 2 + (m % 2)
            for cch in range(8):
                A("tensor", lambda e, j=j, m=m, cch=cch, pm=pm: e.matmul(
                    PSB[pm][:], wo_sb[:, cch, 128 * m:128 * (m + 1)],
                    ctxch[j % 2][:, cch, :], start=(cch == 0), stop=(cch == 7)),
                  waits=([ctx_ld, wo_ld, (vS, vS.n)] if (j + m == 0) else
                         ([ctx_ld] + emw(("eb", pm)) if cch == 0 else [])), inc=tS)
            mmk = (tS, tS.n)
            if m == 7:
                EM[("ctxch", j % 2)] = mmk
            A("vector", lambda e, j=j, m=m, pm=pm: e.tensor_tensor(
                resf[:, m, 512 * j:512 * (j + 1)], PSB[pm][:],
                xnqT32[:, m, 512 * j:512 * (j + 1)], ALU.add), waits=[mmk], inc=vS)
            EM[("eb", pm)] = (vS, vS.n)

    k.barrier(drain=[ld, gld])
    sE12.close()

    sE35 = ExitStack()
    SE3 = lambda n, s, d=F32: sE35.enter_context(nc.sbuf_tensor(n, list(s), d))
    h1_16 = SE3("e_h116", [128, 8, TOK], BF16)
    lnA = mk_lnbufs(sE35, "lnA")
    for j in range(TOK // 512):
        ln_feature(lnA, lambda cch, j=j: resf[:, cch, 512 * j:512 * (j + 1)], j, 0,
                   lambda cch, j=j: h1_16[:, cch, 512 * j:512 * (j + 1)],
                   h1_park, None, None)

    if debug_phase == "E3":
        d1 = dbg("h1", [128, 8, TOK], BF16)
        k.wait_at("sync", vS, vS.n)
        A("sync", lambda e: e.dma_start(out=d1[:], in_=h1_16[:]), inc=st, inc_n=16)
        k.wait_at("sync", st, st.n)
        k.emit()
        return nc, dbg_outs

    sE4 = ExitStack()
    SE4 = lambda n, s, d=F32: sE4.enter_context(nc.sbuf_tensor(n, list(s), d))
    hid = SE4("e_hid", [128, 32, 512], BF16)
    wsl = [SE4(f"e_wsl{i}", [128, 8, 128], BF16) for i in range(2)]
    w2sl = [SE4(f"e_w2sl{i}", [128, D], BF16) for i in range(2)]
    h1c = SE4("e_h1c", [128, 8, 512])
    bffs = SE4("e_bffs", [128, 40])
    A("sync", lambda e: e.dma_start(out=bffs[:, 0:32], in_=bff1_c[:]), inc=ld, inc_n=16)
    A("sync", lambda e: e.dma_start(out=bffs[:, 32:40], in_=bff2_c[:]), inc=ld, inc_n=16)
    wff1_r = wff1.rearrange("(c p) f -> p c f", p=128)

    bff_ld = None
    for j in range(TOK // 512):
        A("sync", lambda e, j=j: e.dma_start(
            out=h1c[:], in_=h1_park[:, :, 512 * j:512 * (j + 1)]),
          waits=emw("h1c_used"), inc=ld, inc_n=16)
        h1c_ld = (ld, ld.n)
        if bff_ld is None:
            bff_ld = h1c_ld
        for fc in range(32):
            wb = fc % 2
            A("gpsimd", lambda e, fc=fc, wb=wb: e.dma_start(
                out=wsl[wb][:], in_=wff1_r[:, :, 128 * fc:128 * (fc + 1)]),
              waits=emw(("wsl", wb)), inc=gld, inc_n=16)
            sl_ld = (gld, gld.n)
            pf = fc % 4
            for cch in range(8):
                A("tensor", lambda e, fc=fc, cch=cch, pf=pf, wb=wb, j=j: e.matmul(
                    PSB[pf][:], wsl[wb][:, cch, :], h1_16[:, cch, 512 * j:512 * (j + 1)],
                    start=(cch == 0), stop=(cch == 7)),
                  waits=(([sl_ld] + emw(("fb", pf)) +
                          (emw(("res", pf)) if fc < 4 else []) +
                          ([(vS, vS.n), (sS, sS.n)] if (j + fc == 0) else []))
                         if cch == 0 else []), inc=tS)
            EM[("wsl", wb)] = (tS, tS.n)
            A("scalar", lambda e, fc=fc, pf=pf: e.activation(
                hid[:, fc, :], PSB[pf][:], AF.Relu, bias=bffs[:, fc:fc + 1], scale=1.0),
              waits=[(tS, tS.n), bff_ld] + emw(("hid_used", fc)), inc=sS)
            EM[("fb", pf)] = (sS, sS.n)
        for fc in range(32):
            wb = fc % 2
            A("gpsimd", lambda e, fc=fc, wb=wb: e.dma_start(
                out=w2sl[wb][:], in_=wff2[128 * fc:128 * (fc + 1), :]),
              waits=emw(("w2sl", wb)), inc=gld, inc_n=16)
            sl_ld = (gld, gld.n)
            relu_k = EM[("fb", fc % 2)] if fc >= 30 else None
            for m in range(8):
                A("tensor", lambda e, fc=fc, m=m, wb=wb: e.matmul(
                    PSB[m][:], w2sl[wb][:, 128 * m:128 * (m + 1)], hid[:, fc, :],
                    start=(fc == 0), stop=(fc == 31)),
                  waits=([sl_ld] + ([(sS, sS.n)] if fc == 0 else []) +
                         (emw(("res", m)) if fc == 0 else [])
                         if (m == 0 or fc == 0) else []), inc=tS)
            EM[("w2sl", wb)] = (tS, tS.n)
            EM[("hid_used", fc)] = (tS, tS.n)
        ff2k = (tS, tS.n)
        for m in range(8):
            A("vector", lambda e, j=j, m=m: e.scalar_tensor_tensor(
                resf[:, m, 512 * j:512 * (j + 1)], PSB[m][:], bffs[:, 32 + m:33 + m],
                h1c[:, m, :], ALU.add, ALU.add),
              waits=([ff2k] if m == 0 else []), inc=vS)
            EM[("res", m)] = (vS, vS.n)
        EM["h1c_used"] = (vS, vS.n)

    k.barrier(drain=[ld, gld])
    sE4.close()

    if debug_phase == "E4":
        d1 = dbg("h2raw", [128, 8, TOK])
        k.wait_at("sync", vS, vS.n)
        A("sync", lambda e: e.dma_start(out=d1[:], in_=resf[:]), inc=st, inc_n=16)
        k.wait_at("sync", st, st.n)
        k.emit()
        return nc, dbg_outs

    mgm16_park = scr("mgm16_park", [128, 16, NE], BF16)
    mgm32_park = scr("mgm32_park", [128, 16, NE], F32)
    gatec_park = scr("gatec_park", [128, 16], F32)
    h2tb = SE3("e_h2tb", [128, 8, 512], BF16)
    wg_sb = SE3("e_wg", [128, 8, NE])
    lgs = SE3("e_lgs", [8, 512])
    lgtok = SE3("e_lgtok", [128, 16, NE])
    gmx = SE3("e_gmx", [128, 16, 3])
    getm = SE3("e_getm", [128, NE])
    mgm = SE3("e_mgm", [128, 16, NE])
    mgm16 = SE3("e_mgm16", [128, 16, NE], BF16)
    gatecs = SE3("e_gatecs", [128, 16])
    sb_tiebm = SE3("e_tiebm", [128, 16, NE])
    wg_r = wg.rearrange("(c p) e -> p c e", p=128)
    A("sync", lambda e: e.dma_start(out=wg_sb[:], in_=wg_r[:]), inc=ld, inc_n=16)
    A("sync", lambda e: e.dma_start(out=sb_tiebm[:], in_=tiebm[:]), inc=ld, inc_n=16)

    for j in range(TOK // 512):
        ln_feature(lnA, lambda cch, j=j: resf[:, cch, 512 * j:512 * (j + 1)], j, 1,
                   lambda cch: h2tb[:, cch, :], h3_park, h2b_park, h2tb[:])
        for cch in range(8):
            A("tensor", lambda e, cch=cch: e.matmul(
                PSB[4][0:8, :], wg_sb[:, cch, :], lnA["f32t"][:, cch, :],
                start=(cch == 0), stop=(cch == 7)),
              waits=(W(vS, sS, ld) if cch == 0 else []), inc=tS)
        A("vector", lambda e: e.tensor_copy(lgs[:], PSB[4][0:8, :]), waits=W(tS), inc=vS)
        for c2 in range(4):
            A("tensor", lambda e, c2=c2: e.transpose(
                PSB[5][0:128, 0:8], lgs[:, 128 * c2:128 * (c2 + 1)], sb_ident[0:8, 0:8]),
              waits=W(vS), inc=tS)
            A("vector", lambda e, j=j, c2=c2: e.tensor_copy(
                lgtok[:, 4 * j + c2, :], PSB[5][0:128, 0:8]), waits=W(tS), inc=vS)

    A("vector", lambda e: e.tensor_tensor(
        lgtok[:], lgtok[:], sb_tiebm[:], ALU.add), waits=W(ld), inc=vS)
    for ch in range(16):
        A("vector", lambda e, ch=ch: e.reduce_max(gmx[:, ch, 0:1], lgtok[:, ch, :],
                                                  axis=AX.X), inc=vS)
        A("vector", lambda e, ch=ch: e.reduce_max(gmx[:, ch, 1:2], lgtok[:, ch, :],
                                                  axis=AX.X, negate=True), inc=vS)
        fv()
        A("scalar", lambda e, ch=ch: e.activation(getm[:], lgtok[:, ch, :], AF.Exp,
                                                  bias=gmx[:, ch, 1:2],
                                                  accum_out=gmx[:, ch, 2:3]),
          waits=W(vS), inc=sS)
        fs()
        fs()
        A("vector", lambda e, ch=ch: e.reciprocal(gmx[:, ch, 2:3], gmx[:, ch, 2:3]),
          waits=W(sS), inc=vS)
        for ee in range(NE):
            A("vector", lambda e, ch=ch, ee=ee: e.tensor_tensor(
                mgm[:, ch, ee:ee + 1], lgtok[:, ch, ee:ee + 1], gmx[:, ch, 0:1],
                ALU.is_ge), inc=vS)
    A("vector", lambda e: e.tensor_copy(mgm16[:], mgm[:]), inc=vS)
    A("vector", lambda e: e.tensor_copy(gatecs[:], gmx[:, :, 2:3]), inc=vS)
    fv()
    fv()
    A("sync", lambda e: e.dma_start(out=mgm16_park[:], in_=mgm16[:]),
      waits=W(vS), inc=ld, inc_n=16)
    A("sync", lambda e: e.dma_start(out=mgm32_park[:], in_=mgm[:]),
      inc=ld, inc_n=16)
    A("sync", lambda e: e.dma_start(out=gatec_park[:], in_=gatecs[:]),
      inc=ld, inc_n=16)

    if debug_phase == "E5":
        d1 = dbg("h2", [128, 8, TOK], BF16)
        d2 = dbg("mgm", [128, 16, NE], BF16)
        d3 = dbg("gatec", [128, 16])
        k.wait_at("sync", vS, vS.n)
        k.wait_at("sync", ld, ld.n)
        A("sync", lambda e: e.dma_start(out=d1[:], in_=h2b_park[:]), inc=st, inc_n=16)
        A("sync", lambda e: e.dma_start(out=d2[:], in_=mgm16_park[:]), inc=st, inc_n=16)
        A("sync", lambda e: e.dma_start(out=d3[:], in_=gatec_park[:]), inc=st, inc_n=16)
        k.wait_at("sync", st, st.n)
        k.emit()
        return nc, dbg_outs

    k.barrier(drain=[ld, gld])
    sE35.close()
    sE25.close()

    flgd = scr("flgd", [NE, 128], mybir.dt.int32)
    sMO = ExitStack()
    moe_acc = sMO.enter_context(nc.sbuf_tensor("e_moeacc", [128, 8, TOK], BF16))
    sE6 = ExitStack()
    SE6 = lambda n, s, d=F32: sE6.enter_context(nc.sbuf_tensor(n, list(s), d))
    hT = SE6("e_hT", [128, NTI, D], BF16)
    grank_bc = SE6("e_grankbc", [128, TOK])
    gate_bc = SE6("e_gatebc", [128, TOK], BF16)
    we1h = [SE6(f"e_we1h{i}", [128, 8, 512], BF16) for i in range(2)]
    we2h = [SE6(f"e_we2h{i}", [128, 8, 512], BF16) for i in range(2)]
    mhid = SE6("e_mhid", [128, 32, 512], BF16)
    ghc = SE6("e_ghc", [128, 8, 512], BF16)
    eout = SE6("e_eout", [128, 4, D], BF16)
    psm = SE6("e_psm", [128, 4, TOK], BF16)
    ptb = [SE6(f"e_ptb{i}", [128, 512], BF16) for i in range(2)]
    hstrf = [SE6(f"e_hstrf{i}", [128, 8, 128]) for i in range(2)]
    mgm16s = SE6("e_mgm16s", [128, 16, NE], BF16)
    mgm32s = SE6("e_mgm32s", [128, 16, NE])
    gatecb = SE6("e_gatecb", [128, 16])
    rankT = SE6("e_rankT", [128, 16, NE])
    grankT = SE6("e_grankT", [128, 16])
    growrow = SE6("e_growrow", [1, TOK])
    gaterow = SE6("e_gaterow", [1, TOK], BF16)
    cntc = SE6("e_cntc", [8, 1])
    e_flg = SE6("e_flg", [8, 128])
    e_flgi = sE6.enter_context(nc.sbuf_tensor("e_flgi", [8, 128], mybir.dt.int32))
    sb_trio = SE6("e_trio", [128, 128], BF16)
    sb_aon1 = SE6("e_aon1", [128, 128], BF16)
    sb_eoffm = SE6("e_eoffm", [128, 16, NE])
    sb_offm = SE6("e_offm", [NE, 128])
    sb_iom = SE6("e_iom", [128, 512])
    sb_iosc = SE6("e_iosc", [128, 1])
    bex = [SE6(f"e_bex{i}", [128, 32]) for i in range(2)]
    be2rw = [SE6("e_be2rw0", [1, D])] * 2

    A("sync", lambda e: e.dma_start(out=sb_trio[:], in_=trio[:]), inc=ld, inc_n=16)
    A("sync", lambda e: e.dma_start(out=sb_aon1[:], in_=aon1[:]), inc=ld, inc_n=16)
    A("sync", lambda e: e.dma_start(out=sb_eoffm[:], in_=eoffm[:]), inc=ld, inc_n=16)
    A("sync", lambda e: e.dma_start(out=sb_offm[:], in_=offm[:]), inc=ld, inc_n=16)
    A("sync", lambda e: e.dma_start(out=sb_iom[:], in_=iom[:]), inc=ld, inc_n=16)
    A("sync", lambda e: e.dma_start(out=sb_iosc[:], in_=iosc[:]), inc=ld, inc_n=16)
    A("sync", lambda e: e.dma_start(out=mgm16s[:], in_=mgm16_park[:]), inc=ld, inc_n=16)
    A("sync", lambda e: e.dma_start(out=mgm32s[:], in_=mgm32_park[:]),
      inc=ld, inc_n=16)
    A("sync", lambda e: e.dma_start(out=gatecb[:], in_=gatec_park[:]), inc=ld, inc_n=16)
    pre_ld = (ld, ld.n)
    pre_gld = (gld, gld.n)
    for m in range(8):
        A("vector", lambda e, m=m: e.tensor_tensor(
            moe_acc[:, m, :], moe_acc[:, m, :], moe_acc[:, m, :], ALU.is_lt), inc=vS)
    for tc in range(NTI):
        hb = tc % 2
        A("sync", lambda e, tc=tc, hb=hb: e.dma_start(
            out=hstrf[hb][:], in_=h3_park[:, :, 128 * tc:128 * (tc + 1)]),
          waits=emw(("hstr_used", hb)), inc=ld, inc_n=16)
        hs_ld = (ld, ld.n)
        for cch in range(8):
            pb = cch % 2
            A("tensor", lambda e, tc=tc, cch=cch, pb=pb, hb=hb: e.transpose(
                PSB[6 + pb][:, 0:128], hstrf[hb][:, cch, :], sb_ident[:]),
              waits=([hs_ld] if cch == 0 else []) + emw(("htr", pb)), inc=tS)
            A("vector", lambda e, tc=tc, cch=cch, pb=pb: e.tensor_copy(
                hT[:, tc, 128 * cch:128 * (cch + 1)], PSB[6 + pb][:, 0:128]),
              waits=[(tS, tS.n)], inc=vS)
            EM[("htr", pb)] = (vS, vS.n)
        EM[("hstr_used", hb)] = (tS, tS.n)
    for ch in range(16):
        pb = ch % 2
        for ch2 in range(ch + 1):
            A("tensor", lambda e, ch=ch, ch2=ch2, pb=pb: e.matmul(
                PSB[pb][:, 0:8],
                sb_trio[:] if ch2 == ch else sb_aon1[:],
                mgm16s[:, ch2, :], start=(ch2 == 0), stop=(ch2 == ch)),
              waits=([pre_ld] + emw(("rkb", pb)) if ch2 == 0 else []), inc=tS)
        A("vector", lambda e, ch=ch, pb=pb: e.tensor_copy(
            rankT[:, ch, :], PSB[pb][:, 0:8]), waits=[(tS, tS.n)], inc=vS)
        EM[("rkb", pb)] = (vS, vS.n)
    for ch in range(16):
        A("tensor", lambda e, ch=ch: e.matmul(
            PSB[2][0:8, 0:1], mgm16s[:, ch, :], sb_aon1[:, 0:1],
            start=(ch == 0), stop=(ch == 15)),
          waits=([pre_ld] if ch == 0 else []), inc=tS)
    A("vector", lambda e: e.tensor_copy(cntc[:], PSB[2][0:8, 0:1]),
      waits=[(tS, tS.n)], inc=vS)
    fv()
    A("vector", lambda e: e.tensor_single_scalar(
        e_flg[:], sb_offm[:], cntc[0:8, 0:1], ALU.is_lt), waits=W(ld), inc=vS)
    A("vector", lambda e: e.tensor_copy(e_flgi[:], e_flg[:]), inc=vS)
    fv()
    fv()
    A("sync", lambda e: e.dma_start(out=flgd[:], in_=e_flgi[:]), waits=W(vS),
      inc=ld, inc_n=16)
    flags_ready = (ld, ld.n)
    A("vector", lambda e: e.tensor_tensor(
        rankT[:], rankT[:], sb_eoffm[:], ALU.add), waits=W(ld, gld), inc=vS)
    A("vector", lambda e: e.tensor_tensor(
        rankT[:], rankT[:], mgm32s[:], ALU.mult), inc=vS)
    for ch in range(16):
        A("vector", lambda e, ch=ch: e.reduce_sum(
            grankT[:, ch:ch + 1], rankT[:, ch, :], axis=AX.X), inc=vS)
    fv()
    for ch in range(16):
        pb = ch % 2
        A("tensor", lambda e, ch=ch, pb=pb: e.transpose(
            PSB[4 + pb][0:1, 0:128], grankT[:, ch:ch + 1], sb_ident[:]),
          waits=W(vS) + emw(("grb", pb)), inc=tS)
        A("vector", lambda e, ch=ch, pb=pb: e.tensor_copy(
            growrow[0:1, 128 * ch:128 * (ch + 1)], PSB[4 + pb][0:1, 0:128]),
          waits=[(tS, tS.n)], inc=vS)
        EM[("grb", pb)] = (vS, vS.n)
        A("tensor", lambda e, ch=ch, pb=pb: e.transpose(
            PSB[4 + pb][0:1, 0:128], gatecb[:, ch:ch + 1], sb_ident[:]),
          waits=W(vS) + emw(("gtb", pb)), inc=tS)
        A("vector", lambda e, ch=ch, pb=pb: e.tensor_copy(
            gaterow[0:1, 128 * ch:128 * (ch + 1)], PSB[4 + pb][0:1, 0:128]),
          waits=[(tS, tS.n)], inc=vS)
        EM[("gtb", pb)] = (vS, vS.n)
    fv()
    fv()
    for w in range(4):
        A("tensor", lambda e, w=w: e.matmul(
            PSB[4][:, 0:512], sb_ones_r[:], growrow[0:1, 512 * w:512 * (w + 1)],
            start=True, stop=True), waits=W(vS) + emw("bcb"), inc=tS)
        A("vector", lambda e, w=w: e.tensor_copy(
            grank_bc[:, 512 * w:512 * (w + 1)], PSB[4][:, 0:512]),
          waits=[(tS, tS.n)], inc=vS)
        EM["bcb"] = (vS, vS.n)
        A("tensor", lambda e, w=w: e.matmul(
            PSB[4][:, 0:512], sb_aon1[0:1, 0:128], gaterow[0:1, 512 * w:512 * (w + 1)],
            start=True, stop=True), waits=W(vS) + emw("bcb"), inc=tS)
        A("vector", lambda e, w=w: e.tensor_copy(
            gate_bc[:, 512 * w:512 * (w + 1)], PSB[4][:, 0:512]),
          waits=[(tS, tS.n)], inc=vS)
        EM["bcb"] = (vS, vS.n)
    fv()
    routing_done_v = (vS, vS.n)

    if debug_phase == "E6r":
        d1 = dbg("grankT", [128, 16])
        d2 = dbg("cnt", [8, 1])
        d6 = dbg("mgm32", [128, 16, NE])
        d7 = dbg("rankTf", [128, 16, NE])
        d8 = dbg("h3p", [128, 8, TOK])
        A("sync", lambda e: e.dma_start(out=d8[:], in_=h3_park[:]), inc=st, inc_n=16)
        A("sync", lambda e: e.dma_start(out=d6[:], in_=mgm32s[:]), inc=st, inc_n=16)
        A("sync", lambda e: e.dma_start(out=d7[:], in_=rankT[:]), inc=st, inc_n=16)
        d3 = dbg("grankbc", [128, TOK])
        d4 = dbg("gatebc", [128, TOK], BF16)
        d5 = dbg("hTd", [128, NTI, D], BF16)
        k.wait_at("sync", vS, vS.n)
        k.wait_at("sync", ld, ld.n)
        A("sync", lambda e: e.dma_start(out=d1[:], in_=grankT[:]), inc=st, inc_n=16)
        A("sync", lambda e: e.dma_start(out=d2[:], in_=cntc[:]), inc=st, inc_n=16)
        A("sync", lambda e: e.dma_start(out=d3[:], in_=grank_bc[:]), inc=st, inc_n=16)
        A("sync", lambda e: e.dma_start(out=d4[:], in_=gate_bc[:]), inc=st, inc_n=16)
        A("sync", lambda e: e.dma_start(out=d5[:], in_=hT[:]), inc=st, inc_n=16)
        k.wait_at("sync", st, st.n)
        k.emit()
        return nc, dbg_outs

    JMAP = [0, 1, 2, 2]
    we1r = [we1b[ee] for ee in range(NE)]
    we2r = [we2b[ee] for ee in range(NE)]

    for ee in range(NE):
        eb = ee % 2
        A("sync", lambda e, ee=ee, eb=eb: e.dma_start(
            out=bex[eb][:], in_=be1_c[ee]), waits=emw(("bex_used", eb)),
          inc=ld, inc_n=16)
        A("sync", lambda e, ee=ee, eb=eb: e.dma_start(
            out=be2rw[0][:], in_=be2_r[ee:ee + 1, :]),
          waits=emw("be2u"), inc=ld, inc_n=16)
        bex_ld = (ld, ld.n)
        for si in range(4):
            base_g = 2048 * ee + 512 * si
            fap = (lambda ee=ee, jf=JMAP[si]:
                   flgd[ee:ee + 1, 32 * jf:32 * jf + 1])

            k.begin_region(fap, flags_ready)
            for ph in range(2):
                A("sync", lambda e, ee=ee, ph=ph: e.dma_start(
                    out=we1h[ph][:], in_=we1r[ee][:, :, 512 * ph:512 * (ph + 1)]),
                  waits=[wpark_k] + emw(("w1u", ph)), inc=ld, inc_n=16)
                EM[("w1l", ph)] = (ld, ld.n)
            for tc in range(NTI):
                pb = tc % 2
                A("vector", lambda e, tc=tc, pb=pb, base_g=base_g: e.tensor_scalar(
                    ptb[pb][:], sb_iom[:], grankT[:, tc:tc + 1], float(-base_g),
                    ALU.subtract, ALU.is_equal),
                  waits=emw(("ptu", pb)), inc=vS)
                ptk = (vS, vS.n)
                for m in range(8):
                    A("tensor", lambda e, tc=tc, m=m, pb=pb: e.matmul(
                        PSB[m][:], hT[:, tc, 128 * m:128 * (m + 1)], ptb[pb][:],
                        start=(tc == 0), stop=(tc == NTI - 1)),
                      waits=(([ptk] if m == 0 else [])
                             + (emw(("sct", m)) if tc == 0 else [])), inc=tS)
                EM[("ptu", pb)] = (tS, tS.n)
            gk = (tS, tS.n)
            for m in range(8):
                A("vector", lambda e, m=m: e.tensor_copy(ghc[:, m, :], PSB[m][:]),
                  waits=[gk] + (emw("ghu") if m == 0 else []), inc=vS)
            ghk = (vS, vS.n)
            for fc in range(32):
                pf = fc % 4
                ph = fc // 4
                for cch in range(8):
                    A("tensor", lambda e, fc=fc, cch=cch, pf=pf, ph=ph: e.matmul(
                        PSB[pf][:], we1h[ph % 2][:, cch, 128 * (fc % 4):128 * (fc % 4 + 1)],
                        ghc[:, cch, :], start=(cch == 0), stop=(cch == 7)),
                      waits=(([EM[("w1l", ph % 2)], ghk, bex_ld] + emw(("mb", pf)))
                             if cch == 0 else []), inc=tS)
                ff1k = (tS, tS.n)
                if fc % 4 == 3:
                    EM[("w1u", ph % 2)] = ff1k
                    if ph + 2 < 8:
                        A("sync", lambda e, ee=ee, ph=ph: e.dma_start(
                            out=we1h[ph % 2][:],
                            in_=we1r[ee][:, :, 512 * (ph + 2):512 * (ph + 3)]),
                          waits=emw(("w1u", ph % 2)), inc=ld, inc_n=16)
                        EM[("w1l", ph % 2)] = (ld, ld.n)
                A("vector", lambda e, fc=fc, pf=pf, eb=eb: e.tensor_scalar(
                    mhid[:, fc, :], PSB[pf][:], bex[eb][:, fc:fc + 1], 0.0,
                    ALU.add, ALU.max),
                  waits=[ff1k, bex_ld] + emw(("mhu", fc)), inc=vS)
                EM[("mb", pf)] = (vS, vS.n)
            relu_k = (vS, vS.n)
            EM["ghu"] = (tS, tS.n)
            k.end_region()

            k.begin_region(fap, flags_ready)
            for sc in range(4):
                A("vector", lambda e, sc=sc, base_g=base_g: e.tensor_scalar(
                    psm[:, sc, :], grank_bc[:], sb_iosc[:, 0:1],
                    float(base_g + 128 * sc), ALU.subtract, ALU.is_equal),
                  waits=emw("psmu") if sc == 0 else [], inc=vS)
                A("vector", lambda e, sc=sc: e.tensor_tensor(
                    psm[:, sc, :], psm[:, sc, :], gate_bc[:], ALU.mult), inc=vS)
            psm_k = (vS, vS.n)
            for half in range(2):
                for g4 in range(4):
                    pw = (half * 4 + g4) % 2
                    A("sync", lambda e, ee=ee, g4=g4, half=half, pw=pw: e.dma_start(
                        out=we2h[pw][:],
                        in_=we2r[ee][:, 8 * g4:8 * (g4 + 1), 512 * half:512 * (half + 1)]),
                      waits=emw(("w2u", pw)), inc=ld, inc_n=16)
                    EM[("w2l", pw)] = (ld, ld.n)
                    for fc8 in range(8):
                        fc = 8 * g4 + fc8
                        for sc in range(4):
                            bank = (4 + sc) if half == 0 else sc
                            A("tensor", lambda e, fc=fc, sc=sc, bank=bank, pw=pw,
                              half=half: e.matmul(
                                PSB[bank][:], mhid[:, fc, 128 * sc:128 * (sc + 1)],
                                we2h[pw][:, fc % 8, :],
                                start=(fc == 0), stop=False),
                              waits=(([relu_k, EM[("w2l", pw)]]
                                      + (emw(("scv", 7)) if (fc == 0 and half == 0)
                                         else []))
                                     if (fc8 == 0 and sc == 0) else []), inc=tS)
                        if fc8 == 7:
                            EM[("w2u", pw)] = (tS, tS.n)
                        if half == 1:
                            EM[("mhu", fc)] = (tS, tS.n)
                for sc in range(4):
                    bank = (4 + sc) if half == 0 else sc
                    A("tensor", lambda e, sc=sc, bank=bank, half=half, eb=eb: e.matmul(
                        PSB[bank][:], sb_ones_r[:],
                        be2rw[eb][0:1, 512 * half:512 * (half + 1)],
                        start=False, stop=True), inc=tS)
                ff2k = (tS, tS.n)
                if half == 1:
                    EM["be2u"] = ff2k
                for sc in range(4):
                    bank = (4 + sc) if half == 0 else sc
                    A("vector", lambda e, sc=sc, bank=bank, half=half: e.tensor_copy(
                        eout[:, sc, 512 * half:512 * (half + 1)], PSB[bank][:]),
                      waits=[ff2k] + (emw("eou") if half + sc == 0 else []), inc=vS)
            eo_k = (vS, vS.n)
            for tp in range(4):
                for m in range(8):
                    for sc in range(4):
                        A("tensor", lambda e, tp=tp, m=m, sc=sc: e.matmul(
                            PSB[m][:], eout[:, sc, 128 * m:128 * (m + 1)],
                            psm[:, sc, 512 * tp:512 * (tp + 1)],
                            start=(sc == 0), stop=(sc == 3)),
                          waits=(([eo_k, psm_k] + emw(("scv", m)))
                                 if sc == 0 else []), inc=tS)
                    sck = (tS, tS.n)
                    A("vector", lambda e, tp=tp, m=m: e.tensor_tensor(
                        moe_acc[:, m, 512 * tp:512 * (tp + 1)],
                        moe_acc[:, m, 512 * tp:512 * (tp + 1)], PSB[m][:], ALU.add),
                      waits=[sck], inc=vS)
                    EM[("scv", m)] = (vS, vS.n)
                    if tp == 3:
                        EM[("sct", m)] = (vS, vS.n)
            EM["psmu"] = (tS, tS.n)
            EM["eou"] = (tS, tS.n)
            k.end_region()
        EM[("bex_used", eb)] = (vS, vS.n)

    k.barrier(drain=[ld, gld, aS])

    if debug_phase == "E6":
        d1 = dbg("moe", [128, 8, TOK], BF16)
        d2 = dbg("cnt", [8, 1])
        d6 = dbg("mgm32", [128, 16, NE])
        A("sync", lambda e: e.dma_start(out=d1[:], in_=moe_acc[:]), inc=st, inc_n=16)
        A("sync", lambda e: e.dma_start(out=d2[:], in_=cntc[:]), inc=st, inc_n=16)
        A("sync", lambda e: e.dma_start(out=d6[:], in_=mgm32s[:]), inc=st, inc_n=16)
        k.wait_at("sync", st, st.n)
        k.emit()
        return nc, dbg_outs

    sE6.close()


    sE7F = ExitStack()
    gT = sE7F.enter_context(nc.sbuf_tensor("f_gT", [128, 8, TOK], F32))
    sE7 = ExitStack()
    SE7 = lambda n, s, d=F32: sE7.enter_context(nc.sbuf_tensor(n, list(s), d))
    h316c = [SE7(f"e_h316c{i}", [128, 8, 512], BF16) for i in range(2)]
    h3c = SE7("e_h3c", [128, 8, 512])
    wof_sb = SE7("e_wof", [128, 8, D], BF16)
    lnB = mk_lnbufs(sE7, "lnB")
    A("gpsimd", lambda e: e.dma_start(
        out=wof_sb[:], in_=wout_f.rearrange("(c p) e -> p c e", p=128)),
      inc=gld, inc_n=16)
    wof_ld = (gld, gld.n)

    for j in range(TOK // 512):
        A("sync", lambda e, j=j: e.dma_start(
            out=h3c[:], in_=h3_park[:, :, 512 * j:512 * (j + 1)]),
          waits=W(tS, sS, vS), inc=ld, inc_n=16)
        A("vector", lambda e, j=j: e.tensor_tensor(
            h3c[:], h3c[:], moe_acc[:, :, 512 * j:512 * (j + 1)],
            ALU.add), waits=W(ld), inc=vS)
        ln_feature(lnB, lambda cch: h3c[:, cch, :], j, 2,
                   lambda cch, j=j: h316c[j % 2][:, cch, :],
                   None, None, None)
        ln_k = (vS, vS.n)
        for m in range(8):
            pm = m % 2
            for cch in range(8):
                A("tensor", lambda e, j=j, m=m, cch=cch, pm=pm: e.matmul(
                    PSB[2 + pm][:], wof_sb[:, cch, 128 * m:128 * (m + 1)],
                    h316c[j % 2][:, cch, :],
                    start=(cch == 0), stop=(cch == 7)),
                  waits=(([wof_ld, ln_k] + emw(("gb", pm)) +
                          ([(sS, sS.n)] if (j + m == 0) else []))
                         if cch == 0 else []), inc=tS)
            A("scalar", lambda e, j=j, m=m, pm=pm: e.activation(
                gT[:, m, 512 * j:512 * (j + 1)], PSB[2 + pm][:], AF.Silu),
              waits=[(tS, tS.n)], inc=sS)
            EM[("gb", pm)] = (sS, sS.n)

    k.barrier(drain=[ld, gld])
    sE7.close()

    sF = ExitStack()
    SF = lambda n, s, d=F32: sF.enter_context(nc.sbuf_tensor(n, list(s), d))
    cnc = [SF(f"f_cnc{i}", [128, 8, 512], BF16) for i in range(2)]
    woa_sb = SF("f_woa", [128, 8, D], BF16)
    prod = SF("f_prod", [128, 8, 512])
    otok = SF("f_otok", [128, D])
    xqt = SF("f_xqt", [128, D])
    A("gpsimd", lambda e: e.dma_start(
        out=woa_sb[:], in_=wout_a.rearrange("(c p) e -> p c e", p=128)),
      inc=gld, inc_n=16)

    woa_ld = (gld, gld.n)
    for j in range(TOK // 512):
        A("sync", lambda e, j=j: e.dma_start(
            out=cnc[j % 2][:], in_=cn_park[:, :, 512 * j:512 * (j + 1)]),
          waits=emw(("cnc_used", j % 2)), inc=ld, inc_n=16)
        cn_ld = (ld, ld.n)
        for m in range(8):
            pm = m % 2
            for cch in range(8):
                A("tensor", lambda e, j=j, m=m, cch=cch, pm=pm: e.matmul(
                    PSB[pm][:], woa_sb[:, cch, 128 * m:128 * (m + 1)],
                    cnc[j % 2][:, cch, :], start=(cch == 0), stop=(cch == 7)),
                  waits=(([cn_ld, woa_ld] + emw(("sb", pm)) +
                          ([(vS, vS.n), (sS, sS.n)] if (j + m == 0) else []))
                         if cch == 0 else []), inc=tS)
            if m == 7:
                EM[("cnc_used", j % 2)] = (tS, tS.n)
            A("vector", lambda e, j=j, m=m, pm=pm: e.tensor_tensor(
                prod[:, m, :], PSB[pm][:], gT[:, m, 512 * j:512 * (j + 1)], ALU.mult),
              waits=[(tS, tS.n)] + emw(("prod_used", m)), inc=vS)
            EM[("sb", pm)] = (vS, vS.n)
        for s2 in range(4):
            A("sync", lambda e, j=j, s2=s2: e.dma_start(
                out=xqt[:], in_=xq[512 * j + 128 * s2:512 * j + 128 * (s2 + 1), :]),
              waits=emw("xqt_used"), inc=ld, inc_n=16)
            xq_ld = (ld, ld.n)
            for m in range(8):
                pm = 2 + (m % 2)
                A("tensor", lambda e, s2=s2, m=m, pm=pm, j=j: e.transpose(
                    PSB[pm][:, 0:128], prod[:, m, 128 * s2:128 * (s2 + 1)], sb_ident[:]),
                  waits=(emw(("sb", 0)) + emw(("sb", 1)) if (s2 == 0 and m == 0) else [])
                        + emw(("fb", pm)), inc=tS)
                if s2 == 3:
                    EM[("prod_used", m)] = (tS, tS.n)
                A("vector", lambda e, m=m, pm=pm: e.tensor_copy(
                    otok[:, 128 * m:128 * (m + 1)], PSB[pm][:, 0:128]),
                  waits=[(tS, tS.n)] + (emw("otok_stored") if m == 0 else []), inc=vS)
                EM[("fb", pm)] = (vS, vS.n)
            A("vector", lambda e: e.tensor_add(otok[:], otok[:], xqt[:]),
              waits=[xq_ld], inc=vS)
            EM["xqt_used"] = (vS, vS.n)
            A("sync", lambda e, j=j, s2=s2: e.dma_start(
                out=out[512 * j + 128 * s2:512 * j + 128 * (s2 + 1), :], in_=otok[:]),
              waits=[(vS, vS.n)], inc=st, inc_n=16)
            EM["otok_stored"] = (st, st.n)

    k.wait_at("sync", st, st.n)
    k.emit()
    return nc, dbg_outs



def _pe_table():
    pos = np.arange(S - 1, -1, -1, dtype=np.float32)
    inv_freq = (1.0 / 10000.0 ** (np.arange(0, D, 2, dtype=np.float32) / D)).astype(np.float32)
    sinusoid = pos[:, None] * inv_freq[None, :]
    return np.concatenate([np.sin(sinusoid), np.cos(sinusoid)], axis=-1).astype(np.float32)


_CACHED = {}


def _get_program():
    if "nc" not in _CACHED:
        _CACHED["nc"] = build_program(None)[0]
    return _CACHED["nc"]


def make_inmaps(inputs):
    x = np.ascontiguousarray(inputs["L0_attn_x"])

    pe = _pe_table()
    peT_full = pe.T
    ii = np.arange(128)[:, None]
    cc = np.arange(256)[None, :]
    ident = np.eye(128, dtype=np.float32)
    rmask = (np.arange(128)[:, None] >= np.arange(128)[None, :]).astype(np.float32)

    gq = np.zeros((64, NH, 512), np.float32)
    gk = np.zeros((64, NH, 512), np.float32)
    jj = np.arange(512) % 128
    for h in range(NH):
        g = GAMMAS[h]
        gq[:, h, :] = 0.125 * (g ** jj)[None, :]
        gk[:, h, :] = (g ** (-jj.astype(np.float64))).astype(np.float32)[None, :]

    com = dict(
        peT=None,
        rmask=rmask, ident=ident,
        gq_t=gq, gk_t=gk,
        wq_a=inputs["L0_attn_Wq"], wk_a=inputs["L0_attn_Wk"], wv_a=inputs["L0_attn_Wv"],
        gnw_t=np.ascontiguousarray(inputs["L0_attn_gn_weight"].reshape(NH, 64).T),
        gnb_t=np.ascontiguousarray(inputs["L0_attn_gn_bias"].reshape(NH, 64).T),
        wout_a=inputs["L0_attn_Wout"],
        wqkv=inputs["L0_ffn_Wqkv"], wr=inputs["L0_ffn_Wr"],
        ub_t=np.ascontiguousarray(inputs["L0_ffn_u_bias"].T),
        vb_t=np.ascontiguousarray(inputs["L0_ffn_v_bias"].T),
        wo=inputs["L0_ffn_Wo"],
        ln_g=np.stack([inputs["L0_ffn_ln1_g"].reshape(8, 128).T,
                       inputs["L0_ffn_ln2_g"].reshape(8, 128).T,
                       inputs["L0_ffn_ln3_g"].reshape(8, 128).T], axis=1),
        ln_b=np.stack([inputs["L0_ffn_ln1_b"].reshape(8, 128).T,
                       inputs["L0_ffn_ln2_b"].reshape(8, 128).T,
                       inputs["L0_ffn_ln3_b"].reshape(8, 128).T], axis=1),
        wff1=inputs["L0_ffn_Wff1"],
        bff1_c=np.ascontiguousarray(inputs["L0_ffn_bff1"].reshape(32, 128).T),
        wff2=inputs["L0_ffn_Wff2"],
        bff2_c=np.ascontiguousarray(inputs["L0_ffn_bff2"].reshape(8, 128).T),
        wg=inputs["L0_ffn_Wg"],
        we1=inputs["L0_ffn_We1"],
        be1_c=np.ascontiguousarray(inputs["L0_ffn_be1"].reshape(NE, 32, 128).transpose(0, 2, 1)),
        we2=inputs["L0_ffn_We2"],
        be2_r=inputs["L0_ffn_be2"],
        wout_f=inputs["L0_ffn_Wout"],
        eoffm=np.broadcast_to(2048.0 * np.arange(NE, dtype=np.float32)[None, None, :],
                              (128, 16, NE)),
        offm=np.broadcast_to(
            np.repeat(np.array([0.0, 512.0, 1024.0, 1e9], np.float32), 32)[None, :],
            (NE, 128)),
        iom=np.broadcast_to(np.arange(512, dtype=np.float32)[None, :], (128, 512)),
        iosc=np.arange(128, dtype=np.float32)[:, None],
        tiebm=np.broadcast_to(
            ((7 - np.arange(NE, dtype=np.float32)) * 1e-6)[None, None, :],
            (128, 16, NE)),
    )
    com = {kk: (np.ascontiguousarray(v, dtype=np.float32) if v is not None else None)
           for kk, v in com.items()}
    com["trio"] = np.triu(np.ones((128, 128)), 1).astype(ml_dtypes.bfloat16)
    com["aon1"] = np.ones((128, 128), dtype=ml_dtypes.bfloat16)
    com["spillq"] = np.where(np.arange(128)[None, :] > np.arange(128)[:, None],
                             NEG, 0.0).astype(ml_dtypes.bfloat16)
    com["negt"] = np.full((128, 128), NEG, dtype=ml_dtypes.bfloat16)

    in_maps = []
    for c in range(8):
        b, p = c // 2, c % 2
        rows = np.concatenate([np.arange(256 * t + 128 * p, 256 * t + 128 * p + 128)
                               for t in range(NTI)])
        xqc = np.ascontiguousarray(x[b][rows])
        peTc = np.zeros((D, S + 256), np.float32)
        peTc[:, 128 * p:128 * p + S] = peT_full
        peTc_in = np.ascontiguousarray(
            peTc.reshape(8, 128, S + 256).transpose(1, 0, 2)).astype(ml_dtypes.bfloat16)
        am = np.where(cc <= ii + 128 * p, 0.0, NEG).astype(np.float32)
        sel = np.zeros((64, 2), np.float32)
        sel[:, p] = 1.0
        m = dict(com)
        m["peT"] = peTc_in
        m["pmq"] = np.full((128, 128), NEG if p == 0 else 0.0,
                           dtype=ml_dtypes.bfloat16)
        m["xb"] = np.ascontiguousarray(x[b])
        m["xq"] = xqc
        m["amask"] = am
        m["selc"] = sel
        in_maps.append(m)
    return in_maps


def kernel(**inputs):
    in_maps = make_inmaps(inputs)
    nc = _get_program()
    res = run_bass_kernel_spmd(nc, in_maps, core_ids=list(range(8)))

    outx = np.zeros((4, S, D), np.float32)
    for c in range(8):
        b, p = c // 2, c % 2
        oc = np.asarray(res.results[c]["out"], dtype=np.float32)
        for t in range(NTI):
            outx[b, 256 * t + 128 * p:256 * t + 128 * p + 128] = oc[128 * t:128 * (t + 1)]
    return outx


if __name__ == "__main__":
    build_program()


# version 24
# speedup vs baseline: 1.0245x; 1.0157x over previous
import numpy as np
import ml_dtypes
import concourse.bass as bass
import concourse.mybir as mybir
from concourse.bass_utils import run_bass_kernel_spmd
from concourse.ap import AP
from contextlib import ExitStack

F32 = mybir.dt.float32
BF16 = mybir.dt.bfloat16
AF = mybir.ActivationFunctionType
ALU = mybir.AluOpType
AX = mybir.AxisListType

S, D, NH, DH, FFD, NE = 4096, 1024, 16, 64, 4096, 8
TOK = 2048
NTI = 16
EPS = 1e-5
NEG = -30000.0
GAMMAS = [1.0 - 2.0 ** (-5 - (h // 2)) for h in range(NH)]

ENGS = ("sync", "tensor", "vector", "scalar", "gpsimd")


class Sem:
    __slots__ = ("h", "n", "name")

    def __init__(self, h, name):
        self.h = h
        self.n = 0
        self.name = name


COND_ENGS = ("tensor", "vector", "sync")


class KB:
    def __init__(self, nc, es):
        self.nc = nc
        self.es = es
        self.ops = {e: [] for e in ENGS}
        self.bar = None
        self.region = None
        self.regs = None

    def sem(self, name):
        return Sem(self.es.enter_context(self.nc.semaphore(name)), name)

    def add(self, eng, build, waits=(), inc=None, inc_n=1):
        wl = [(s.h, v) for (s, v) in waits]

        def f(e, wl=wl, build=build, inc=inc, inc_n=inc_n):
            for h, v in wl:
                e.wait_ge(h, v)
            ins = build(e)
            if inc is not None:
                ins.then_inc(inc.h, inc_n)
        self.ops[eng].append([f, build, wl, inc, inc_n])
        if inc is not None:
            inc.n += inc_n
            if self.region is not None and eng in self.region:
                t = self.region[eng]
                t[inc] = t.get(inc, 0) + inc_n

    def begin_region(self, flag_ap_fn, ready):
        assert self.region is None
        self.region = {eng: {} for eng in COND_ENGS}
        for eng in COND_ENGS:
            self.ops[eng].append(["IF", flag_ap_fn, (ready[0].h, ready[1])])

    def end_region(self):
        for eng in COND_ENGS:
            comp = self.region[eng]
            self.ops[eng].append(["ELSE"])
            for sem, n in comp.items():
                self.ops[eng].append(["COMP", sem.h, n])
            self.ops[eng].append(["END"])
        self.region = None

    def w(self, *sems):
        return [(s, s.n) for s in sems]

    def wait_at(self, eng, sem, v):
        self.ops[eng].append([lambda e, h=sem.h, vv=v: e.wait_ge(h, vv), None, [], None, 0])

    def barrier(self, drain=()):
        if self.bar is None:
            self.bar = self.sem("bar")
        for eng in ENGS:
            q = self.ops[eng]
            if eng in ("sync", "gpsimd"):
                for s in drain:
                    self.wait_at(eng, s, s.n)
            q.append([lambda e: e.sem_inc(self.bar.h, 1), None, [], None, 0])
            self.bar.n += 1
        target = self.bar.n
        for eng in ENGS:
            self.wait_at(eng, self.bar, target)

    def emit(self):
        nreg = {eng: [getattr(self.nc, eng).alloc_register(f"fl_{eng}{i}")
                      for i in range(4)] for eng in COND_ENGS}
        with self.nc.Block() as blk:
            for eng in ENGS:
                items = self.ops[eng]
                regs = nreg.get(eng)

                def run(e, items=items, regs=regs):
                    guard = None
                    ri = 0
                    for it in items:
                        tag = it[0]
                        if tag == "IF":
                            _, apf, rd = it
                            e.wait_ge(rd[0], rd[1])
                            reg = regs[ri % 4]
                            ri += 1
                            e.reg_load(reg, apf())
                            guard = e.If_ne(reg, 0)
                            guard.__enter__()
                        elif tag == "ELSE":
                            guard.__exit__(None, None, None)
                            guard = e.Else()
                            guard.__enter__()
                        elif tag == "COMP":
                            e.sem_inc(it[1], it[2])
                        elif tag == "END":
                            guard.__exit__(None, None, None)
                            guard = None
                        else:
                            it[0](e)
                getattr(blk, eng)(run)


def cdiv(a, b):
    return (a + b - 1) // b



def build_program(debug_phase=None):
    nc = bass.Bass("TRN2", target_bir_lowering=False, debug=False)
    es = ExitStack()
    k = KB(nc, es)
    A = k.add
    W = k.w

    def inp(name, shape, dt=F32):
        return nc.dram_tensor(name, list(shape), dt, kind="ExternalInput").ap()

    xb = inp("xb", [S, D])
    xq = inp("xq", [TOK, D])
    peT = inp("peT", [128, 8, S + 256], BF16)
    amask = inp("amask", [128, 256])
    rmask = inp("rmask", [128, 128])
    ident_i = inp("ident", [128, 128])
    selc = inp("selc", [64, 2])
    gq_t = inp("gq_t", [64, NH, 512])
    gk_t = inp("gk_t", [64, NH, 512])
    wq_a = inp("wq_a", [NH, D, DH])
    wk_a = inp("wk_a", [NH, D, DH])
    wv_a = inp("wv_a", [NH, D, DH])
    gnw_t = inp("gnw_t", [64, NH])
    gnb_t = inp("gnb_t", [64, NH])
    wout_a = inp("wout_a", [D, D])
    wqkv = inp("wqkv", [D, 3 * D])
    wr = inp("wr", [D, D])
    ub_t = inp("ub_t", [DH, NH])
    vb_t = inp("vb_t", [DH, NH])
    wo = inp("wo", [D, D])
    ln_g = inp("ln_g", [128, 3, 8])
    ln_b = inp("ln_b", [128, 3, 8])
    wff1 = inp("wff1", [D, FFD])
    bff1_c = inp("bff1_c", [128, 32])
    wff2 = inp("wff2", [FFD, D])
    bff2_c = inp("bff2_c", [128, 8])
    wg = inp("wg", [D, NE])
    we1 = inp("we1", [NE, D, FFD])
    be1_c = inp("be1_c", [NE, 128, 32])
    we2 = inp("we2", [NE, FFD, D])
    be2_r = inp("be2_r", [NE, D])
    wout_f = inp("wout_f", [D, D])
    trio = inp("trio", [128, 128], BF16)
    aon1 = inp("aon1", [128, 128], BF16)
    eoffm = inp("eoffm", [128, 16, NE])
    offm = inp("offm", [NE, 128])
    iom = inp("iom", [128, 512])
    iosc = inp("iosc", [128, 1])
    tiebm = inp("tiebm", [128, 16, NE])
    spillq = inp("spillq", [128, 128], BF16)
    pmq = inp("pmq", [128, 128], BF16)
    negt = inp("negt", [128, 128], BF16)

    out = nc.dram_tensor("out", [TOK, D], F32, kind="ExternalOutput").ap()

    def scr(name, shape, dt=F32):
        return nc.dram_tensor(name, list(shape), dt, kind="Internal").ap()

    pskew = [scr("pskew0", [128 * (S + 256)], BF16), scr("pskew1", [128 * (S + 256)], BF16)]
    pskew_dbg = scr("pskew_dbg", [128 * 256], BF16)
    ctx_park = scr("ctx_park", [128, 8, TOK], BF16)
    cn_park = scr("cn_park", [128, 8, TOK], BF16)
    h1_park = scr("h1_park", [128, 8, TOK], F32)
    h2b_park = scr("h2b_park", [128, 8, TOK], BF16)
    h3_park = scr("h3_park", [128, 8, TOK], F32)

    dbg_outs = {}

    def dbg(name, shape, dt=F32):
        ap = nc.dram_tensor("dbg_" + name, list(shape), dt, kind="ExternalOutput").ap()
        dbg_outs[name] = ap
        return ap

    SBp = lambda n, s, d=F32: es.enter_context(nc.sbuf_tensor(n, list(s), d))
    sb_ident = SBp("c_ident", [128, 128])
    sb_ident16 = SBp("c_ident16", [128, 128], BF16)
    sb_rmask = SBp("c_rmask", [128, 128])
    sb_ones_c = SBp("c_ones_c", [128, 1])
    sb_ones_r = SBp("c_ones_r", [1, 128])
    sb_sel = SBp("c_sel", [64, 2])
    sb_scrv = SBp("c_scrv", [128, 16])
    sb_scrs = SBp("c_scrs", [128, 16])
    sb_lng = SBp("c_lng", [128, 3, 8])
    sb_lnb = SBp("c_lnb", [128, 3, 8])

    ld = k.sem("ld")
    gld = k.sem("gld")
    st = k.sem("st")
    tS = k.sem("tS")
    vS = k.sem("vS")
    sS = k.sem("sS")
    pS = k.sem("pS")
    bS = k.sem("bS")
    aS = k.sem("aS")

    A("sync", lambda e: e.dma_start(out=sb_ident[:], in_=ident_i[:]), inc=ld, inc_n=16)
    A("sync", lambda e: e.dma_start(out=sb_rmask[:], in_=rmask[:]), inc=ld, inc_n=16)
    A("sync", lambda e: e.dma_start(out=sb_sel[:], in_=selc[:]), inc=ld, inc_n=16)
    A("sync", lambda e: e.dma_start(out=sb_lng[:], in_=ln_g[:]), inc=ld, inc_n=16)
    A("sync", lambda e: e.dma_start(out=sb_lnb[:], in_=ln_b[:]), inc=ld, inc_n=16)
    A("vector", lambda e: e.memset(sb_ones_c[:], 1.0), inc=vS)
    A("vector", lambda e: e.memset(sb_ones_r[:], 1.0), inc=vS)
    A("vector", lambda e: e.tensor_copy(sb_ident16[:], sb_ident[:]),
      waits=[(ld, 16)], inc=vS)

    sAC = ExitStack()
    xnT = sAC.enter_context(nc.sbuf_tensor("xnT", [128, 8, S], BF16))
    xnqT16 = sAC.enter_context(nc.sbuf_tensor("xnqT16", [128, 8, TOK], BF16))

    sA = ExitStack()
    SA = lambda n, s, d=F32: sA.enter_context(nc.sbuf_tensor(n, list(s), d))
    PA = lambda n, s, d=F32: sA.enter_context(nc.psum_tensor(n, list(s), d))
    a_xt = [SA(f"a_xt{i}", [128, D]) for i in range(2)]
    a_xn = [SA(f"a_xn{i}", [128, D]) for i in range(2)]
    a_sq = SA("a_sqs", [128, D])
    a_stat = SA("a_stat", [128, 8 * 32])
    ps_tr = [PA(f"a_ptr{i}", [128, 128]) for i in range(2)]

    def fv():
        A("vector", lambda e: e.tensor_copy(sb_scrv[:], sb_ident[:, 0:16]), inc=vS)

    def fs():
        A("scalar", lambda e: e.copy(sb_scrs[:], sb_ident[:, 0:16]), inc=sS)

    def emit_ln_pass(src_fn, nt, stat, xtb, xnb, sqb, trb, dst16, dst32):
        n = nt
        for j in range(nt):
            bi = j % 2
            A("sync", lambda e, j=j, bi=bi: e.dma_start(out=xtb[bi][:], in_=src_fn(j)),
              waits=W(vS, sS), inc=ld, inc_n=16)
            A("vector", lambda e, j=j, bi=bi: e.reduce_sum(
                stat[:, j:j + 1], xtb[bi][:], axis=AX.X), waits=W(ld), inc=vS)
            A("scalar", lambda e, j=j, bi=bi: e.activation(
                sqb[:], xtb[bi][:], AF.Square, accum_out=stat[:, n + j:n + j + 1]),
              waits=W(ld), inc=sS)
        fs()
        fv()
        A("vector", lambda e: e.tensor_scalar_mul(stat[:, 2 * n:3 * n], stat[:, 0:n],
                                                  1.0 / D), waits=W(sS), inc=vS)
        A("vector", lambda e: e.tensor_scalar(stat[:, 7 * n:8 * n], stat[:, n:2 * n],
                                              1.0 / D, EPS, ALU.mult, ALU.add), inc=vS)
        A("vector", lambda e: e.tensor_tensor(stat[:, 6 * n:7 * n], stat[:, 2 * n:3 * n],
                                              stat[:, 2 * n:3 * n], ALU.mult), inc=vS)
        fv()
        A("vector", lambda e: e.tensor_sub(stat[:, 3 * n:4 * n], stat[:, 7 * n:8 * n],
                                           stat[:, 6 * n:7 * n]), inc=vS)
        fv()
        A("scalar", lambda e: e.activation(stat[:, 4 * n:5 * n], stat[:, 3 * n:4 * n],
                                           AF.Sqrt), waits=W(vS), inc=sS)
        fs()
        A("vector", lambda e: e.reciprocal(stat[:, 4 * n:5 * n], stat[:, 4 * n:5 * n]),
          waits=W(sS), inc=vS)
        fv()
        A("vector", lambda e: e.scalar_tensor_tensor(
            stat[:, 5 * n:6 * n], stat[:, 2 * n:3 * n], -1.0, stat[:, 4 * n:5 * n],
            ALU.mult, ALU.mult), inc=vS)
        fv()
        fv()
        fmark = vS.n
        for j in range(nt):
            bi = j % 2
            A("sync", lambda e, j=j, bi=bi: e.dma_start(out=xtb[bi][:], in_=src_fn(j)),
              waits=W(vS, sS), inc=ld, inc_n=16)
            A("scalar", lambda e, j=j, bi=bi: e.activation(
                xnb[bi][:], xtb[bi][:], AF.Identity, bias=stat[:, 5 * n + j:5 * n + j + 1],
                scale=stat[:, 4 * n + j:4 * n + j + 1]),
              waits=[(vS, fmark), (ld, ld.n), (tS, tS.n)], inc=sS)
            for cch in range(8):
                pb = cch % 2
                A("tensor", lambda e, cch=cch, pb=pb, bi=bi: e.transpose(
                    trb[pb], xnb[bi][:, 128 * cch:128 * (cch + 1)], sb_ident[:]),
                  waits=W(sS, vS), inc=tS)
                first = True
                for dd in (dst16, dst32):
                    if dd is None:
                        continue
                    A("vector", lambda e, cch=cch, pb=pb, j=j, dd=dd: e.tensor_copy(
                        dd[:, cch, 128 * j:128 * (j + 1)], trb[pb]),
                      waits=(W(tS) if first else []), inc=vS)
                    first = False

    emit_ln_pass(lambda j: xb[128 * j:128 * (j + 1), :], 32, a_stat, a_xt, a_xn,
                 a_sq, [p[:] for p in ps_tr], xnT, None)
    emit_ln_pass(lambda j: xq[128 * j:128 * (j + 1), :], 16, a_stat, a_xt, a_xn,
                 a_sq, [p[:] for p in ps_tr], xnqT16, None)

    if debug_phase == "A":
        d1 = dbg("xnT", [128, 8, S], BF16)
        d2 = dbg("xnqT", [128, 8, TOK], BF16)
        k.wait_at("sync", vS, vS.n)
        A("sync", lambda e: e.dma_start(out=d1[:], in_=xnT[:]), inc=st, inc_n=16)
        A("sync", lambda e: e.dma_start(out=d2[:], in_=xnqT16[:]), inc=st, inc_n=16)
        k.wait_at("sync", st, st.n)
        k.emit()
        return nc, dbg_outs

    k.barrier(drain=[ld])
    sA.close()
    ln_pass = emit_ln_pass

    sB = ExitStack()
    SB_ = lambda n, s, d=F32: sB.enter_context(nc.sbuf_tensor(n, list(s), d))
    PB = lambda n, s, d=F32: sB.enter_context(nc.psum_tensor(n, list(s), d))

    wqh = SB_("b_wqh", [128, 8, DH], BF16)
    wkh = SB_("b_wkh", [128, 8, DH], BF16)
    wvh = SB_("b_wvh", [128, 8, DH], BF16)
    wrh = SB_("b_wrh", [128, 8, DH], BF16)
    ubv = SB_("b_ubv", [DH, 2 * NH])
    qu = SB_("b_qu", [DH, TOK], BF16)
    qv = SB_("b_qv", [DH, TOK], BF16)
    kvT = SB_("b_kvT", [128, S], BF16)
    vtok = SB_("b_vtok", [128, S // 128, 72], BF16)
    rcT = SB_("b_rcT", [DH, S + 256], BF16)
    pech = [SB_(f"b_pech{i}", [128, 8, 256], BF16) for i in range(2)]
    pbuf = [SB_(f"b_pbuf{i}", [128, S + 256], BF16) for i in range(2)]
    bdb = [SB_(f"b_bdb{i}", [128, S + 256], BF16) for i in range(2)]
    asbT = SB_("b_asbT", [128, 8, 256], BF16)
    sb_negt = SB_("b_negt", [128, 128], BF16)
    rrall = SB_("b_rrall", [1, TOK])
    bcball = SB_("b_bcball", [DH, TOK])
    ctxu65 = SB_("b_ctxu65", [DH + 1, TOK])
    ctxh = SB_("b_ctxh", [DH, TOK], BF16)
    sb_spq = SB_("b_spq", [128, 128], BF16)
    sb_pmq = SB_("b_pmq", [128, 128], BF16)

    bigS = [PB(f"b_bigs{i}", [128, 1024]) for i in range(2)]
    bps = [PB("b_ps0", [128, 512])]
    msc = [bigS[1][:, 0:512], bigS[1][:, 512:1024]]
    ps_t = [PB(f"b_pst{i}", [128, 128], BF16) for i in range(2)]
    ps_ctx = PB("b_psctx", [DH + 1, 256])

    A("sync", lambda e: e.dma_start(out=ubv[:, 0:NH], in_=ub_t[:]), inc=ld, inc_n=16)
    A("sync", lambda e: e.dma_start(out=ubv[:, NH:2 * NH], in_=vb_t[:]), inc=ld, inc_n=16)
    A("sync", lambda e: e.dma_start(out=sb_spq[:], in_=spillq[:]), inc=ld, inc_n=16)
    A("sync", lambda e: e.dma_start(out=sb_pmq[:], in_=pmq[:]), inc=ld, inc_n=16)
    A("sync", lambda e: e.dma_start(out=sb_negt[:], in_=negt[:]), inc=ld, inc_n=16)
    for cch in range(S // 128):
        A("vector", lambda e, cch=cch: e.memset(vtok[:, cch, 64:65], 1.0), inc=vS)

    wqkv_r = wqkv.rearrange("(c p) e -> p c e", p=128)
    wr_r = wr.rearrange("(c p) e -> p c e", p=128)

    M = {}
    BM = {}

    def bw(key):
        m = BM.get(key)
        return [m] if m else []

    def mw(key):
        m = M.get(key)
        return [m] if m else []

    nh_b = 0 if debug_phase in ("C0", "C0a", "C0b") else (1 if debug_phase == "B0" else NH)
    for h in range(nh_b):
        A("gpsimd", lambda e, h=h: e.dma_start(
            out=wqh[:], in_=wqkv_r[:, :, DH * h:DH * (h + 1)]),
          waits=mw("proj_done"), inc=gld, inc_n=16)
        A("gpsimd", lambda e, h=h: e.dma_start(
            out=wkh[:], in_=wqkv_r[:, :, D + DH * h:D + DH * (h + 1)]), inc=gld, inc_n=16)
        A("gpsimd", lambda e, h=h: e.dma_start(
            out=wvh[:], in_=wqkv_r[:, :, 2 * D + DH * h:2 * D + DH * (h + 1)]),
          inc=gld, inc_n=16)
        A("gpsimd", lambda e, h=h: e.dma_start(
            out=wrh[:], in_=wr_r[:, :, DH * h:DH * (h + 1)]), inc=gld, inc_n=16)
        w_loaded = (gld, gld.n)

        def bproj(w_sb, src, ntok, evac, free_mark):
            for j in range(ntok // 512):
                pj = j % 2
                for cch in range(8):
                    A("tensor", lambda e, j=j, cch=cch, pj=pj, w_sb=w_sb, src=src:
                      e.matmul(msc[pj][0:64, :], w_sb[:, cch, :],
                               src[:, cch, 512 * j:512 * (j + 1)],
                               start=(cch == 0), stop=(cch == 7)),
                      waits=(bw(("b", pj)) + [w_loaded] + (mw(free_mark) if j < 2 else [])
                             if cch == 0 else []), inc=tS)
                mmk = (tS, tS.n)
                evac(j, msc[pj][0:64, :], mmk)

        def tpos(t):
            return 2 * (15 - t) if t >= 8 else 2 * t + 1

        def evac_q(j, ps, mmk, h=h):
            for c4 in range(4):
                t4 = 4 * j + c4
                A("scalar", lambda e, c4=c4, t4=t4, ps=ps: e.activation(
                    qu[:, 128 * tpos(t4):128 * tpos(t4) + 128],
                    ps[0:64, 128 * c4:128 * (c4 + 1)], AF.Identity,
                    bias=ubv[:, h:h + 1], scale=1.0),
                  waits=([mmk, (ld, 32)] + (mw("qu_free") if j == 0 else [])
                         if c4 == 0 else []), inc=sS)
            A("scalar", lambda e, j=j, ps=ps: e.activation(
                qv[:, 512 * j:512 * (j + 1)], ps, AF.Identity,
                bias=ubv[:, NH + h:NH + h + 1], scale=1.0), inc=sS)
            BM[("b", j % 2)] = (sS, sS.n)
        bproj(wqh, xnqT16, TOK, evac_q, "qu_free")

        def evac_k(j, ps, mmk):
            if j % 2 == 0:
                A("vector", lambda e, j=j, ps=ps: e.tensor_copy(
                    kvT[0:64, 512 * j:512 * (j + 1)], ps),
                  waits=[mmk] + (mw("kvT_free") if j == 0 else []), inc=vS)
                BM[("b", j % 2)] = (vS, vS.n)
            else:
                A("scalar", lambda e, j=j, ps=ps: e.activation(
                    kvT[0:64, 512 * j:512 * (j + 1)], ps, AF.Identity),
                  waits=[mmk], inc=sS)
                BM[("b", j % 2)] = (sS, sS.n)
        bproj(wkh, xnT, S, evac_k, "kvT_free")

        def evac_v(j, ps, mmk):
            if j % 2 == 0:
                A("vector", lambda e, j=j, ps=ps: e.tensor_copy(
                    kvT[64:128, 512 * j:512 * (j + 1)], ps), waits=[mmk], inc=vS)
                BM[("b", j % 2)] = (vS, vS.n)
            else:
                A("scalar", lambda e, j=j, ps=ps: e.activation(
                    kvT[64:128, 512 * j:512 * (j + 1)], ps, AF.Identity),
                  waits=[mmk], inc=sS)
                BM[("b", j % 2)] = (sS, sS.n)
        bproj(wvh, xnT, S, evac_v, None)
        kv_ready = (vS, vS.n)

        for cch in range(S // 128):
            pb = cch % 2
            A("tensor", lambda e, cch=cch, pb=pb: e.transpose(
                ps_t[pb][:], kvT[:, 128 * cch:128 * (cch + 1)], sb_ident16[:]),
              waits=[kv_ready] + bw(("t", pb)) + (mw("vtok_free") if cch == 0 else []),
              inc=tS)
            A("vector", lambda e, cch=cch, pb=pb: e.tensor_copy(
                vtok[:, cch, 0:64], ps_t[pb][:, 64:128]),
              waits=[(tS, tS.n)], inc=vS)
            BM[("t", pb)] = (vS, vS.n)

        for wch in range(cdiv(S + 256, 256)):
            cw = min(256, S + 256 - 256 * wch)
            pe_b = wch % 2
            A("sync", lambda e, wch=wch, cw=cw, pe_b=pe_b: e.dma_start(
                out=pech[pe_b][:, :, 0:cw], in_=peT[:, :, 256 * wch:256 * wch + cw]),
              waits=bw(("pe", pe_b)), inc=ld, inc_n=16)
            pk_ld = (ld, ld.n)
            pj = wch % 2
            for cch in range(8):
                A("tensor", lambda e, wch=wch, cw=cw, cch=cch, pj=pj, pe_b=pe_b: e.matmul(
                    msc[pj][0:64, 0:cw], wrh[:, cch, :], pech[pe_b][:, cch, 0:cw],
                    start=(cch == 0), stop=(cch == 7)),
                  waits=([pk_ld] + bw(("b", pj)) + (mw("rcT_free") if wch == 0 else [])
                         if cch == 0 else []), inc=tS)
            BM[("pe", pe_b)] = (tS, tS.n)
            if wch % 2 == 0:
                A("vector", lambda e, wch=wch, cw=cw, pj=pj: e.tensor_copy(
                    rcT[:, 256 * wch:256 * wch + cw], msc[pj][0:64, 0:cw]),
                  waits=[(tS, tS.n)], inc=vS)
                BM[("b", pj)] = (vS, vS.n)
            else:
                A("scalar", lambda e, wch=wch, cw=cw, pj=pj: e.activation(
                    rcT[:, 256 * wch:256 * wch + cw], msc[pj][0:64, 0:cw],
                    AF.Identity),
                  waits=[(tS, tS.n)], inc=sS)
                BM[("b", pj)] = (sS, sS.n)
        rc_ready = (vS, vS.n)
        M["proj_done"] = (tS, tS.n)

        def p_stage(t, pp, off, pkey):
            E = 256 * (t + 1)
            W0 = S - 128 - 256 * t
            nw = cdiv(E, 512)
            for w in range(nw):
                cw = min(512, E - 512 * w)
                pj = w % 2
                A("tensor", lambda e, w=w, cw=cw, t=t, W0=W0, pj=pj: e.matmul(
                    msc[pj][:, 0:cw], qv[:, 128 * t:128 * (t + 1)],
                    rcT[:, W0 + 512 * w:W0 + 512 * w + cw], start=True, stop=True),
                  waits=[rc_ready] + bw(("b", pj)), inc=tS)
                mmk = (tS, tS.n)
                if w % 2 == 0:
                    A("vector", lambda e, w=w, cw=cw, pp=pp, off=off, pj=pj:
                      e.tensor_copy(
                        pbuf[pp][:, off + 512 * w:off + 512 * w + cw],
                        msc[pj][:, 0:cw]),
                      waits=[mmk] + (mw(("pbuf_free", pp)) if w == 0 else []), inc=vS)
                    BM[("b", pj)] = (vS, vS.n)
                else:
                    A("scalar", lambda e, w=w, cw=cw, pp=pp, off=off, pj=pj:
                      e.activation(
                        pbuf[pp][:, off + 512 * w:off + 512 * w + cw],
                        msc[pj][:, 0:cw], AF.Identity),
                      waits=[mmk], inc=sS)
                    BM[("b", pj)] = (sS, sS.n)
            A("vector", lambda e, E=E, pp=pp, off=off: e.tensor_tensor(
                pbuf[pp][:, off + E - 128:off + E],
                pbuf[pp][:, off + E - 128:off + E], sb_pmq[:],
                ALU.add), waits=W(ld), inc=vS)
            pev = (vS, vS.n)
            A("sync", lambda e, E=E, pp=pp, off=off: e.dma_start(
                out=AP(pskew[pp].tensor, 128 * off, [[E, 128], [1, E]]),
                in_=pbuf[pp][:, off:off + E]),
              waits=[pev] + mw(("skew_done", pp)), inc=pS, inc_n=16)
            pok = (pS, pS.n)
            A("gpsimd", lambda e, E=E, pp=pp, off=off: e.dma_start(
                out=bdb[pp][:, off:off + E],
                in_=AP(pskew[pp].tensor, 128 * off + 127, [[E - 1, 128], [1, E]])),
              waits=[pok] + mw(("bd_used", pp)), inc=bS, inc_n=16)
            A("vector", lambda e, E=E, pp=pp, off=off: e.tensor_tensor(
                bdb[pp][:, off + E - 128:off + E],
                bdb[pp][:, off + E - 128:off + E], sb_spq[:],
                ALU.add), waits=[(bS, bS.n)], inc=vS)
            M[("bd_ready", pkey)] = (vS, vS.n)
            M[("skew_done", pp)] = (bS, bS.n)
            if off == 0:
                M[("pbuf_free", pp)] = pok

        def s_pair(u, pos):
            pp = pos % 2
            tl, ts = 15 - u, u
            El, Es = 256 * (tl + 1), 256 * (ts + 1)
            ncc, ncs = El // 128, Es // 128
            psS = (bigS[0][:, 0:512], bigS[0][:, 512:1024], bps[0][:, 0:512])
            ng = cdiv(ncc, 2)

            def avs(g):
                ncg = min(2, ncc - 2 * g)
                for c2 in range(ncg):
                    kc = 2 * g + c2
                    A("tensor", lambda e, kc=kc, ncc=ncc: e.matmul(
                        ps_ctx[:], vtok[:, kc, 0:65], asbT[:, kc % 8, :],
                        start=(kc == 0), stop=(kc == ncc - 1)),
                      waits=([M[("exp", u, g)]] if c2 == 0 else [])
                            + (mw("ctx_free") if kc == 0 else []), inc=tS)
                    M[("asbs", kc % 8)] = (tS, tS.n)

            prevg = None
            for g in range(ng):
                ncg = min(2, ncc - 2 * g)
                bank = psS[g % 3]
                for c2 in range(ncg):
                    kc = 2 * g + c2
                    A("tensor", lambda e, kc=kc, c2=c2, bank=bank, u=u: e.matmul(
                        bank[:, 256 * c2:256 * c2 + 256],
                        kvT[0:64, 128 * kc:128 * (kc + 1)],
                        qu[:, 256 * u:256 * u + 256],
                        start=True, stop=False),
                      waits=((bw(("G", g % 3))
                              + ([M[("bd_ready", (u, 1))]] if g == 0 else []))
                             if c2 == 0 else []), inc=tS)
                    A("tensor", lambda e, kc=kc, c2=c2, bank=bank, pp=pp: e.matmul(
                        bank[:, 256 * c2:256 * c2 + 128],
                        bdb[pp][:, 128 * kc:128 * (kc + 1)], sb_ident16[:],
                        start=False, stop=False), inc=tS)
                    if kc < ncs:
                        A("tensor", lambda e, kc=kc, c2=c2, bank=bank, pp=pp, El=El:
                          e.matmul(
                            bank[:, 256 * c2 + 128:256 * c2 + 256],
                            bdb[pp][:, El + 128 * kc:El + 128 * (kc + 1)],
                            sb_ident16[:], start=False, stop=True), inc=tS)
                    else:
                        A("tensor", lambda e, kc=kc, c2=c2, bank=bank: e.matmul(
                            bank[:, 256 * c2 + 128:256 * c2 + 256],
                            sb_negt[:], sb_ident16[:],
                            start=False, stop=True), inc=tS)
                ek = (tS, tS.n)
                sl = (2 * g) % 8
                A("scalar", lambda e, g=g, ncg=ncg, sl=sl, bank=bank: e.activation(
                    asbT[:, sl:sl + ncg, :], bank[:, 0:256 * ncg], AF.Exp,
                    scale=0.125),
                  waits=[ek] + mw(("asbs", sl)) + mw(("asbs", sl + 1)), inc=sS)
                M[("exp", u, g)] = (sS, sS.n)
                BM[("G", g % 3)] = (sS, sS.n)
                if prevg is not None:
                    avs(prevg)
                prevg = g
            avs(prevg)
            avk = (tS, tS.n)
            M[("bd_used", pp)] = avk
            A("vector", lambda e, tl=tl: e.tensor_copy(
                ctxu65[:, 128 * tl:128 * (tl + 1)], ps_ctx[:, 0:128]),
              waits=[avk], inc=vS)
            A("vector", lambda e, ts=ts: e.tensor_copy(
                ctxu65[:, 128 * ts:128 * (ts + 1)], ps_ctx[:, 128:256]), inc=vS)
            M["ctx_free"] = (vS, vS.n)

        p_stage(15, 0, 0, (0, 0))
        p_stage(0, 0, 256 * 16, (0, 1))
        for u in range(1, 8):
            p_stage(15 - u, u % 2, 0, (u, 0))
            p_stage(u, u % 2, 256 * (16 - u), (u, 1))
            s_pair(u - 1, u - 1)
        s_pair(7, 7)

        fv()
        A("vector", lambda e: e.tensor_copy(rrall[:], ctxu65[64:65, :]), inc=vS)
        fv()
        fv()
        A("vector", lambda e: e.reciprocal(rrall[:], rrall[:]), inc=vS)
        fv()
        fv()
        rr_ready = (vS, vS.n)
        for w in range(4):
            pj = w % 2
            A("tensor", lambda e, w=w, pj=pj: e.matmul(
                msc[pj][0:64, 0:512], sb_ones_r[0:1, 0:64],
                rrall[0:1, 512 * w:512 * (w + 1)], start=True, stop=True),
              waits=[rr_ready] + bw(("b", pj)), inc=tS)
            A("vector", lambda e, w=w, pj=pj: e.tensor_copy(
                bcball[:, 512 * w:512 * (w + 1)], msc[pj][0:64, 0:512]),
              waits=[(tS, tS.n)], inc=vS)
            BM[("b", pj)] = (vS, vS.n)
        A("vector", lambda e: e.tensor_tensor(ctxh[:], ctxu65[0:64, :], bcball[:],
                                              ALU.mult), inc=vS)
        A("sync", lambda e, h=h: e.dma_start(
            out=ctx_park[64 * (h % 2):64 * (h % 2) + 64, h // 2, :], in_=ctxh[:]),
          waits=[(vS, vS.n)], inc=ld, inc_n=16)
        M["ctx_free"] = (ld, ld.n)
        M["qu_free"] = (tS, tS.n)
        M["kvT_free"] = (tS, tS.n)
        M["vtok_free"] = (tS, tS.n)
        M["rcT_free"] = (tS, tS.n)

    if debug_phase == "B0":
        d1 = dbg("ctxu", [DH + 1, TOK])
        d2 = dbg("rs", [1, TOK])
        k.wait_at("sync", vS, vS.n)
        k.wait_at("sync", bS, bS.n)
        A("sync", lambda e: e.dma_start(out=d1[:], in_=ctxu65[:]), inc=st, inc_n=16)
        A("sync", lambda e: e.dma_start(out=d2[:], in_=rrall[:]), inc=st, inc_n=16)
        k.wait_at("sync", st, st.n)
        k.emit()
        return nc, dbg_outs

    if debug_phase == "B":
        d1 = dbg("ctx", [128, 8, TOK], BF16)
        k.wait_at("sync", ld, ld.n)
        A("sync", lambda e: e.dma_start(out=d1[:], in_=ctx_park[:]), inc=st, inc_n=16)
        k.wait_at("sync", st, st.n)
        k.emit()
        return nc, dbg_outs


    k.barrier(drain=[ld, pS, bS, gld])
    sB.close()

    sC = ExitStack()
    SC = lambda n, s, d=F32: sC.enter_context(nc.sbuf_tensor(n, list(s), d))
    PC = lambda n, s, d=F32: sC.enter_context(nc.psum_tensor(n, list(s), d))

    rwq = SC("c_rwq", [128, 8, DH], BF16)
    rwk = SC("c_rwk", [128, 8, DH], BF16)
    rwv = SC("c_rwv", [128, 8, DH], BF16)
    gqh = SC("c_gqh", [64, 512])
    gkh = SC("c_gkh", [64, 512])
    qgT = SC("c_qgT", [DH, S], BF16)
    rkvT = SC("c_rkvT", [128, S], BF16)
    kvtok = SC("c_kvtok", [128, S // 128, 128], BF16)
    stT = SC("c_stT", [64, 64])
    stT16 = SC("c_stT16", [64, 64], BF16)
    sc16 = [SC(f"c_sc16{i}", [128, 128], BF16) for i in range(2)]
    atr = [SC(f"c_atr{i}", [128, 128], BF16) for i in range(2)]
    ob = [SC(f"c_ob{i}", [64, TOK]) for i in range(2)]
    stats = SC("c_stats", [64, 64])
    sqsc = SC("c_sqsc", [64, 128])
    mrow = SC("c_mrow", [1, 8])
    rncol = SC("c_rncol", [64, 2])
    scbi = SC("c_scbi", [64, 2])
    gw = SC("c_gw", [64, 2 * NH])
    cnh = SC("c_cnh", [64, TOK], BF16)

    cps = [PC(f"c_ps{i}", [128, 512]) for i in range(2)]
    ctk = [PC(f"c_ptk{i}", [128, 128], BF16) for i in range(2)]
    cpo = [PC(f"c_po{i}", [64, 128]) for i in range(2)]
    cpu = [PC(f"c_pu{i}", [64, 128]) for i in range(2)]
    ps_u = cpu[0]

    A("sync", lambda e: e.dma_start(out=gw[:, 0:NH], in_=gnw_t[:]), inc=ld, inc_n=16)
    A("sync", lambda e: e.dma_start(out=gw[:, NH:2 * NH], in_=gnb_t[:]), inc=ld, inc_n=16)

    CM = {}

    def cmw(key):
        m = CM.get(key)
        return [m] if m else []

    def projc(w_sb, src, ntok, evac, wk_ld):
        for j in range(ntok // 512):
            pj = j % 2
            for cch in range(8):
                A("tensor", lambda e, j=j, cch=cch, pj=pj: e.matmul(
                    cps[pj][0:64, :], w_sb[:, cch, :], src[:, cch, 512 * j:512 * (j + 1)],
                    start=(cch == 0), stop=(cch == 7)),
                  waits=([wk_ld] + cmw(("cb", pj)) if cch == 0 else []), inc=tS)
            evac(j, cps[pj][0:64, :], (tS, tS.n))
            CM[("cb", pj)] = (vS, vS.n)

    c_dbg = debug_phase in ("C0", "C0a", "C0b")
    for h in range(1 if c_dbg else NH):
        g = GAMMAS[h]
        gC = g ** 128
        A("gpsimd", lambda e, h=h: e.dma_start(
            out=rwq[:], in_=wq_a[h].rearrange("(c p) e -> p c e", p=128)),
          waits=cmw("w_free"), inc=gld, inc_n=16)
        A("gpsimd", lambda e, h=h: e.dma_start(
            out=rwk[:], in_=wk_a[h].rearrange("(c p) e -> p c e", p=128)), inc=gld, inc_n=16)
        A("gpsimd", lambda e, h=h: e.dma_start(
            out=rwv[:], in_=wv_a[h].rearrange("(c p) e -> p c e", p=128)), inc=gld, inc_n=16)
        cw_ld = (gld, gld.n)
        A("sync", lambda e, h=h: e.dma_start(out=gqh[:], in_=gq_t[:, h, :]),
          waits=cmw("g_free"), inc=ld, inc_n=16)
        A("sync", lambda e, h=h: e.dma_start(out=gkh[:], in_=gk_t[:, h, :]), inc=ld, inc_n=16)
        g_ld = (ld, ld.n)

        def evac_qg(j, ps, mmk):
            A("vector", lambda e, j=j, ps=ps: e.tensor_tensor(
                qgT[:, 512 * j:512 * (j + 1)], ps, gqh[:], ALU.mult),
              waits=[mmk, g_ld] + (cmw("qg_free") if j == 0 else []), inc=vS)
        projc(rwq, xnT, S, evac_qg, cw_ld)

        def evac_kg(j, ps, mmk):
            A("vector", lambda e, j=j, ps=ps: e.tensor_tensor(
                rkvT[0:64, 512 * j:512 * (j + 1)], ps, gkh[:], ALU.mult),
              waits=[mmk] + (cmw("kv_free") if j == 0 else []), inc=vS)
        projc(rwk, xnT, S, evac_kg, cw_ld)

        def evac_rv(j, ps, mmk):
            A("vector", lambda e, j=j, ps=ps: e.tensor_copy(
                rkvT[64:128, 512 * j:512 * (j + 1)], ps), waits=[mmk], inc=vS)
        projc(rwv, xnT, S, evac_rv, cw_ld)
        qgkv_ready = (vS, vS.n)
        CM["w_free"] = (tS, tS.n)

        A("vector", lambda e: e.memset(stT[:], 0.0), waits=cmw("st_free"), inc=vS)
        A("vector", lambda e: e.memset(stT16[:], 0.0), inc=vS)
        st_ready = (vS, vS.n)

        for j in range(0 if debug_phase == "C0a" else S // 128):
            jb = j % 2
            A("tensor", lambda e, j=j, jb=jb: e.transpose(
                ctk[jb][:], rkvT[:, 128 * j:128 * (j + 1)], sb_ident16[:]),
              waits=[qgkv_ready] + cmw(("tk", jb)), inc=tS)
            A("vector", lambda e, j=j, jb=jb: e.tensor_copy(kvtok[:, j, :], ctk[jb][:]),
              waits=[(tS, tS.n)], inc=vS)
            A("vector", lambda e, j=j, gC=gC: e.tensor_scalar_mul(
                kvtok[:, j, 0:64], kvtok[:, j, 0:64], gC), inc=vS)
            kvtok_k = (vS, vS.n)
            A("tensor", lambda e, j=j, jb=jb: e.matmul(
                cps[jb][0:128, 0:128], qgT[:, 128 * j:128 * (j + 1)],
                rkvT[0:64, 128 * j:128 * (j + 1)],
                start=True, stop=True), waits=[qgkv_ready] + cmw(("cb", jb)), inc=tS)
            A("vector", lambda e, jb=jb: e.tensor_tensor(
                sc16[jb][:], cps[jb][0:128, 0:128], sb_rmask[:], ALU.mult),
              waits=[(tS, tS.n)] + cmw(("sc16u", jb)), inc=vS)
            CM[("cb", jb)] = (vS, vS.n)
            A("tensor", lambda e, jb=jb: e.transpose(ctk[jb][:], sc16[jb][:],
                                                     sb_ident16[:]),
              waits=[(vS, vS.n)], inc=tS)
            CM[("sc16u", jb)] = (tS, tS.n)
            A("vector", lambda e, jb=jb: e.tensor_copy(atr[jb][:], ctk[jb][:]),
              waits=[(tS, tS.n)] + cmw(("atru", jb)), inc=vS)
            CM[("tk", jb)] = (vS, vS.n)
            atr_k = (vS, vS.n)
            A("tensor", lambda e, j=j, jb=jb: e.matmul(
                cpo[jb][:, 0:128], kvtok[:, j, 64:128], atr[jb][:],
                start=True, stop=False),
              waits=[atr_k, kvtok_k if j == 0 else st_ready] + cmw(("ob", jb)), inc=tS)
            A("tensor", lambda e, j=j, jb=jb: e.matmul(
                cpo[jb][:, 0:128], stT16[:], qgT[:, 128 * j:128 * (j + 1)],
                start=False, stop=True), inc=tS)
            CM[("atru", jb)] = (tS, tS.n)
            ok_ = (tS, tS.n)
            A("vector", lambda e, j=j, jb=jb: e.tensor_copy(
                ob[jb][:, 128 * (j // 2):128 * (j // 2 + 1)], cpo[jb][:, 0:128]),
              waits=[ok_], inc=vS)
            CM[("ob", jb)] = (vS, vS.n)
            A("vector", lambda e, j=j, jb=jb: e.reduce_sum(
                stats[:, j:j + 1], ob[jb][:, 128 * (j // 2):128 * (j // 2 + 1)],
                axis=AX.X), inc=vS)
            A("scalar", lambda e, j=j, jb=jb: e.activation(
                sqsc[:], ob[jb][:, 128 * (j // 2):128 * (j // 2 + 1)], AF.Square,
                accum_out=stats[:, 32 + j:33 + j]), waits=[(vS, vS.n)], inc=sS)
            A("tensor", lambda e, j=j, jb=jb: e.matmul(
                cpu[jb][:, 0:64], kvtok[:, j, 0:64], kvtok[:, j, 64:128],
                start=True, stop=True),
              waits=[kvtok_k] + cmw(("ub", jb)), inc=tS)
            upd_k = (tS, tS.n)
            A("vector", lambda e, gC=gC, jb=jb: e.scalar_tensor_tensor(
                stT16[:], stT[:], gC, cpu[jb][:, 0:64], ALU.mult, ALU.add),
              waits=[upd_k, st_ready], inc=vS)
            A("vector", lambda e, gC=gC, jb=jb: e.scalar_tensor_tensor(
                stT[:], stT[:], gC, cpu[jb][:, 0:64], ALU.mult, ALU.add), inc=vS)
            CM[("ub", jb)] = (vS, vS.n)
            st_ready = (vS, vS.n)

        if debug_phase in ("C0a", "C0b"):
            continue
        fs()
        A("vector", lambda e: e.reduce_sum(rncol[:, 0:1], stats[:, 0:32], axis=AX.X),
          waits=W(sS), inc=vS)
        A("vector", lambda e: e.reduce_sum(rncol[:, 1:2], stats[:, 32:64], axis=AX.X),
          inc=vS)
        fv()
        A("tensor", lambda e: e.matmul(ps_u[0:1, 64:66], sb_ones_c[0:64, 0:1],
                                       rncol[:, 0:2], start=True, stop=True),
          waits=W(vS), inc=tS)
        A("vector", lambda e: e.tensor_copy(mrow[0:1, 0:2], ps_u[0:1, 64:66]),
          waits=W(tS), inc=vS)
        fv()
        A("vector", lambda e: e.tensor_scalar_mul(mrow[0:1, 2:4], mrow[0:1, 0:2],
                                                  1.0 / (64.0 * S)), inc=vS)
        fv()
        A("vector", lambda e: e.tensor_tensor(mrow[0:1, 4:5], mrow[0:1, 2:3],
                                              mrow[0:1, 2:3], ALU.mult), inc=vS)
        fv()
        A("vector", lambda e: e.tensor_sub(mrow[0:1, 5:6], mrow[0:1, 3:4],
                                           mrow[0:1, 4:5]), inc=vS)
        fv()
        A("vector", lambda e: e.tensor_scalar_add(mrow[0:1, 5:6], mrow[0:1, 5:6], EPS),
          inc=vS)
        fv()
        A("scalar", lambda e: e.activation(mrow[0:1, 6:7], mrow[0:1, 5:6], AF.Sqrt),
          waits=W(vS), inc=sS)
        fs()
        A("vector", lambda e: e.reciprocal(mrow[0:1, 7:8], mrow[0:1, 6:7]),
          waits=W(sS), inc=vS)
        fv()
        A("vector", lambda e: e.tensor_scalar_mul(mrow[0:1, 2:3], mrow[0:1, 2:3], -1.0),
          inc=vS)
        A("vector", lambda e: e.tensor_copy(mrow[0:1, 3:4], mrow[0:1, 7:8]), inc=vS)
        fv()
        A("tensor", lambda e: e.matmul(ps_u[0:64, 66:68], sb_ones_r[0:1, 0:64],
                                       mrow[0:1, 2:4], start=True, stop=True),
          waits=W(vS), inc=tS)
        A("vector", lambda e: e.tensor_copy(rncol[:], ps_u[0:64, 66:68]),
          waits=W(tS), inc=vS)
        fv()
        A("vector", lambda e, h=h: e.tensor_tensor(scbi[:, 0:1], gw[:, h:h + 1],
                                                   rncol[:, 1:2], ALU.mult), inc=vS)
        fv()
        A("vector", lambda e, h=h: e.scalar_tensor_tensor(
            scbi[:, 1:2], scbi[:, 0:1], rncol[:, 0:1], gw[:, NH + h:NH + h + 1],
            ALU.mult, ALU.add), inc=vS)
        A("vector", lambda e: e.tensor_scalar(ob[0][:], ob[0][:], sb_sel[:, 0:1], None,
                                              ALU.mult), inc=vS)
        A("vector", lambda e: e.scalar_tensor_tensor(ob[0][:], ob[1][:], sb_sel[:, 1:2],
                                                     ob[0][:], ALU.mult, ALU.add), inc=vS)
        fv()
        A("scalar", lambda e: e.activation(cnh[:], ob[0][:], AF.Identity,
                                           bias=scbi[:, 1:2], scale=scbi[:, 0:1]),
          waits=W(vS, ld), inc=sS)
        A("sync", lambda e, h=h: e.dma_start(
            out=cn_park[64 * (h % 2):64 * (h % 2) + 64, h // 2, :], in_=cnh[:]),
          waits=W(sS), inc=ld, inc_n=16)
        CM["g_free"] = (vS, vS.n)
        CM["qg_free"] = (tS, tS.n)
        CM["kv_free"] = (tS, tS.n)
        CM["st_free"] = (vS, vS.n)

    if debug_phase in ("C0a", "C0b"):
        d1 = dbg("qg", [DH, S], BF16)
        d2 = dbg("rkv", [128, S], BF16)
        k.wait_at("sync", vS, vS.n)
        A("sync", lambda e: e.dma_start(out=d1[:], in_=qgT[:]), inc=st, inc_n=16)
        A("sync", lambda e: e.dma_start(out=d2[:], in_=rkvT[:]), inc=st, inc_n=16)
        if debug_phase == "C0b":
            d3 = dbg("ob0", [DH, TOK])
            d4 = dbg("ob1", [DH, TOK])
            d5 = dbg("stats", [DH, 64])
            A("sync", lambda e: e.dma_start(out=d3[:], in_=ob[0][:]), inc=st, inc_n=16)
            A("sync", lambda e: e.dma_start(out=d4[:], in_=ob[1][:]), inc=st, inc_n=16)
            A("sync", lambda e: e.dma_start(out=d5[:], in_=stats[:]), inc=st, inc_n=16)
        k.wait_at("sync", st, st.n)
        k.emit()
        return nc, dbg_outs

    if debug_phase in ("C", "C0"):
        dm = dbg("mrow", [1, 8])
        dsc = dbg("scbi", [DH, 2])
        k.wait_at("sync", vS, vS.n)
        A("sync", lambda e: e.dma_start(out=dm[:], in_=mrow[:]), inc=st, inc_n=16)
        A("sync", lambda e: e.dma_start(out=dsc[:], in_=scbi[:]), inc=st, inc_n=16)
        d1 = dbg("cn", [128, 8, TOK], BF16)
        k.wait_at("sync", ld, ld.n)
        A("sync", lambda e: e.dma_start(out=d1[:], in_=cn_park[:]), inc=st, inc_n=16)
        k.wait_at("sync", st, st.n)
        k.emit()
        return nc, dbg_outs

    k.barrier(drain=[ld, gld])
    sC.close()
    sAC.close()

    sEF = ExitStack()
    PSB = [sEF.enter_context(nc.psum_tensor(f"e_psb{i}", [128, 512], F32))
           for i in range(8)]

    def mk_lnbufs(stack, tag):
        SE_ = lambda n, s, d=F32: stack.enter_context(nc.sbuf_tensor(n, list(s), d))
        return dict(
            f32t=SE_(f"{tag}_f32t", [128, 8, 512]),
            sqch=[SE_(f"{tag}_sqch{i}", [128, 512]) for i in range(2)],
            lrow=SE_(f"{tag}_lrow", [1, 8, 512]),
            bcn=SE_(f"{tag}_bcn", [128, 512]),
            lntmp=SE_(f"{tag}_lntmp", [128, 512]),
        )

    def ln_feature(lb, src_fn, j, ln_idx, out16_fn, park32_ap, park16_ap, park16_src):
        f32t, sqch, lrow, bcn, lntmp = (lb["f32t"], lb["sqch"], lb["lrow"],
                                        lb["bcn"], lb["lntmp"])
        for cch in range(8):
            A("tensor", lambda e, cch=cch: e.matmul(
                PSB[0][0:1, :], sb_ones_c[:], src_fn(cch),
                start=(cch == 0), stop=(cch == 7)),
              waits=(W(vS, ld) if cch == 0 else []), inc=tS)
        for cch in range(8):
            pb = cch % 2
            A("vector", lambda e, cch=cch, pb=pb: e.tensor_tensor(
                sqch[pb][:], src_fn(cch), src_fn(cch), ALU.mult),
              waits=W(tS, ld), inc=vS)
            A("tensor", lambda e, cch=cch, pb=pb: e.matmul(
                PSB[1][0:1, :], sb_ones_c[:], sqch[pb][:],
                start=(cch == 0), stop=(cch == 7)), waits=W(vS), inc=tS)
        A("vector", lambda e: e.tensor_scalar_mul(lrow[0:1, 0, :], PSB[0][0:1, :], 1.0 / D),
          waits=W(tS), inc=vS)
        A("vector", lambda e: e.tensor_scalar(lrow[0:1, 1, :], PSB[1][0:1, :], 1.0 / D,
                                              EPS, ALU.mult, ALU.add), inc=vS)
        A("vector", lambda e: e.tensor_tensor(lrow[0:1, 2, :], lrow[0:1, 0, :],
                                              lrow[0:1, 0, :], ALU.mult), inc=vS)
        fv()
        fv()
        A("vector", lambda e: e.tensor_sub(lrow[0:1, 5, :], lrow[0:1, 1, :],
                                           lrow[0:1, 2, :]), inc=vS)
        fv()
        fv()
        A("scalar", lambda e: e.activation(lrow[0:1, 3, :], lrow[0:1, 5, :], AF.Sqrt),
          waits=W(vS), inc=sS)
        fs()
        fs()
        A("vector", lambda e: e.reciprocal(lrow[0:1, 3, :], lrow[0:1, 3, :]),
          waits=W(sS), inc=vS)
        fv()
        fv()
        A("vector", lambda e: e.scalar_tensor_tensor(
            lrow[0:1, 4, :], lrow[0:1, 0, :], -1.0, lrow[0:1, 3, :],
            ALU.mult, ALU.mult), inc=vS)
        fv()
        fv()
        A("tensor", lambda e: e.matmul(PSB[0][:], sb_ones_r[:], lrow[0:1, 3, :],
                                       start=True, stop=True), waits=W(vS), inc=tS)
        A("tensor", lambda e: e.matmul(PSB[1][:], sb_ones_r[:], lrow[0:1, 4, :],
                                       start=True, stop=True), inc=tS)
        A("vector", lambda e: e.tensor_copy(bcn[:], PSB[1][:]), waits=W(tS), inc=vS)
        for cch in range(8):
            A("vector", lambda e, cch=cch: e.tensor_tensor(
                lntmp[:], src_fn(cch), PSB[0][:], ALU.mult), waits=W(ld), inc=vS)
            A("vector", lambda e: e.tensor_add(lntmp[:], lntmp[:], bcn[:]), inc=vS)
            A("vector", lambda e, cch=cch: e.tensor_scalar(
                f32t[:, cch, :], lntmp[:], sb_lng[:, ln_idx, cch:cch + 1],
                sb_lnb[:, ln_idx, cch:cch + 1], ALU.mult, ALU.add),
              waits=W(ld), inc=vS)
            if out16_fn is not None:
                A("vector", lambda e, cch=cch: e.tensor_copy(
                    out16_fn(cch), f32t[:, cch, :]), waits=W(ld), inc=vS)
        if park32_ap is not None:
            A("sync", lambda e: e.dma_start(
                out=park32_ap[:, :, 512 * j:512 * (j + 1)], in_=f32t[:]),
              waits=W(vS), inc=ld, inc_n=16)
        if park16_ap is not None:
            A("sync", lambda e: e.dma_start(
                out=park16_ap[:, :, 512 * j:512 * (j + 1)], in_=park16_src),
              waits=W(vS), inc=ld, inc_n=16)

    we1b = scr("we1b", [NE, 128, 8, FFD], BF16)
    we2b = scr("we2b", [NE, 128, 32, D], BF16)
    for ee in range(NE):
        A("gpsimd", lambda e, ee=ee: e.dma_start(
            out=we1b[ee], in_=we1[ee].rearrange("(c p) f -> p c f", p=128)),
          inc=gld, inc_n=16)
        A("gpsimd", lambda e, ee=ee: e.dma_start(
            out=we2b[ee], in_=we2[ee].rearrange("(c p) d2 -> p c d2", p=128)),
          inc=gld, inc_n=16)
    wpark_k = (gld, gld.n)

    sE25 = ExitStack()
    resf = sE25.enter_context(nc.sbuf_tensor("e_resf", [128, 8, TOK], F32))
    sE12 = ExitStack()
    SE1 = lambda n, s, d=F32: sE12.enter_context(nc.sbuf_tensor(n, list(s), d))
    xnqT32 = SE1("e_xnq32", [128, 8, TOK])
    ext = [SE1(f"e_xt{i}", [128, D]) for i in range(2)]
    exc = [SE1(f"e_xc{i}", [128, D]) for i in range(2)]
    esq = SE1("e_sqs", [128, D])
    e_stat = SE1("e_stat", [128, 8 * 16])

    ln_pass(lambda j: xq[128 * j:128 * (j + 1), :], 16, e_stat, ext, exc, esq,
            [PSB[0][:, 0:128], PSB[1][:, 0:128]], None, xnqT32)

    wo_sb = SE1("e_wo", [128, 8, D], BF16)
    ctxch = [SE1(f"e_ctxch{i}", [128, 8, 512], BF16) for i in range(2)]
    wo_r = wo.rearrange("(c p) e -> p c e", p=128)
    A("gpsimd", lambda e: e.dma_start(out=wo_sb[:], in_=wo_r[:]), inc=gld, inc_n=16)

    EM = {}

    def emw(key):
        m = EM.get(key)
        return [m] if m else []

    wo_ld = (gld, gld.n)
    for j in range(TOK // 512):
        A("sync", lambda e, j=j: e.dma_start(
            out=ctxch[j % 2][:], in_=ctx_park[:, :, 512 * j:512 * (j + 1)]),
          waits=emw(("ctxch", j % 2)), inc=ld, inc_n=16)
        ctx_ld = (ld, ld.n)
        for m in range(8):
            pm = 2 + (m % 2)
            for cch in range(8):
                A("tensor", lambda e, j=j, m=m, cch=cch, pm=pm: e.matmul(
                    PSB[pm][:], wo_sb[:, cch, 128 * m:128 * (m + 1)],
                    ctxch[j % 2][:, cch, :], start=(cch == 0), stop=(cch == 7)),
                  waits=([ctx_ld, wo_ld, (vS, vS.n)] if (j + m == 0) else
                         ([ctx_ld] + emw(("eb", pm)) if cch == 0 else [])), inc=tS)
            mmk = (tS, tS.n)
            if m == 7:
                EM[("ctxch", j % 2)] = mmk
            A("vector", lambda e, j=j, m=m, pm=pm: e.tensor_tensor(
                resf[:, m, 512 * j:512 * (j + 1)], PSB[pm][:],
                xnqT32[:, m, 512 * j:512 * (j + 1)], ALU.add), waits=[mmk], inc=vS)
            EM[("eb", pm)] = (vS, vS.n)

    k.barrier(drain=[ld, gld])
    sE12.close()

    sE35 = ExitStack()
    SE3 = lambda n, s, d=F32: sE35.enter_context(nc.sbuf_tensor(n, list(s), d))
    h1_16 = SE3("e_h116", [128, 8, TOK], BF16)
    lnA = mk_lnbufs(sE35, "lnA")
    for j in range(TOK // 512):
        ln_feature(lnA, lambda cch, j=j: resf[:, cch, 512 * j:512 * (j + 1)], j, 0,
                   lambda cch, j=j: h1_16[:, cch, 512 * j:512 * (j + 1)],
                   h1_park, None, None)

    if debug_phase == "E3":
        d1 = dbg("h1", [128, 8, TOK], BF16)
        k.wait_at("sync", vS, vS.n)
        A("sync", lambda e: e.dma_start(out=d1[:], in_=h1_16[:]), inc=st, inc_n=16)
        k.wait_at("sync", st, st.n)
        k.emit()
        return nc, dbg_outs

    sE4 = ExitStack()
    SE4 = lambda n, s, d=F32: sE4.enter_context(nc.sbuf_tensor(n, list(s), d))
    hid = SE4("e_hid", [128, 32, 512], BF16)
    wsl = [SE4(f"e_wsl{i}", [128, 8, 128], BF16) for i in range(2)]
    w2sl = [SE4(f"e_w2sl{i}", [128, D], BF16) for i in range(2)]
    h1c = SE4("e_h1c", [128, 8, 512])
    bffs = SE4("e_bffs", [128, 40])
    A("sync", lambda e: e.dma_start(out=bffs[:, 0:32], in_=bff1_c[:]), inc=ld, inc_n=16)
    A("sync", lambda e: e.dma_start(out=bffs[:, 32:40], in_=bff2_c[:]), inc=ld, inc_n=16)
    wff1_r = wff1.rearrange("(c p) f -> p c f", p=128)

    bff_ld = None
    for j in range(TOK // 512):
        A("sync", lambda e, j=j: e.dma_start(
            out=h1c[:], in_=h1_park[:, :, 512 * j:512 * (j + 1)]),
          waits=emw("h1c_used"), inc=ld, inc_n=16)
        h1c_ld = (ld, ld.n)
        if bff_ld is None:
            bff_ld = h1c_ld
        for fc in range(32):
            wb = fc % 2
            A("gpsimd", lambda e, fc=fc, wb=wb: e.dma_start(
                out=wsl[wb][:], in_=wff1_r[:, :, 128 * fc:128 * (fc + 1)]),
              waits=emw(("wsl", wb)), inc=gld, inc_n=16)
            sl_ld = (gld, gld.n)
            pf = fc % 4
            for cch in range(8):
                A("tensor", lambda e, fc=fc, cch=cch, pf=pf, wb=wb, j=j: e.matmul(
                    PSB[pf][:], wsl[wb][:, cch, :], h1_16[:, cch, 512 * j:512 * (j + 1)],
                    start=(cch == 0), stop=(cch == 7)),
                  waits=(([sl_ld] + emw(("fb", pf)) +
                          (emw(("res", pf)) if fc < 4 else []) +
                          ([(vS, vS.n), (sS, sS.n)] if (j + fc == 0) else []))
                         if cch == 0 else []), inc=tS)
            EM[("wsl", wb)] = (tS, tS.n)
            A("scalar", lambda e, fc=fc, pf=pf: e.activation(
                hid[:, fc, :], PSB[pf][:], AF.Relu, bias=bffs[:, fc:fc + 1], scale=1.0),
              waits=[(tS, tS.n), bff_ld] + emw(("hid_used", fc)), inc=sS)
            EM[("fb", pf)] = (sS, sS.n)
        for fc in range(32):
            wb = fc % 2
            A("gpsimd", lambda e, fc=fc, wb=wb: e.dma_start(
                out=w2sl[wb][:], in_=wff2[128 * fc:128 * (fc + 1), :]),
              waits=emw(("w2sl", wb)), inc=gld, inc_n=16)
            sl_ld = (gld, gld.n)
            relu_k = EM[("fb", fc % 2)] if fc >= 30 else None
            for m in range(8):
                A("tensor", lambda e, fc=fc, m=m, wb=wb: e.matmul(
                    PSB[m][:], w2sl[wb][:, 128 * m:128 * (m + 1)], hid[:, fc, :],
                    start=(fc == 0), stop=(fc == 31)),
                  waits=([sl_ld] + ([(sS, sS.n)] if fc == 0 else []) +
                         (emw(("res", m)) if fc == 0 else [])
                         if (m == 0 or fc == 0) else []), inc=tS)
            EM[("w2sl", wb)] = (tS, tS.n)
            EM[("hid_used", fc)] = (tS, tS.n)
        ff2k = (tS, tS.n)
        for m in range(8):
            A("vector", lambda e, j=j, m=m: e.scalar_tensor_tensor(
                resf[:, m, 512 * j:512 * (j + 1)], PSB[m][:], bffs[:, 32 + m:33 + m],
                h1c[:, m, :], ALU.add, ALU.add),
              waits=([ff2k] if m == 0 else []), inc=vS)
            EM[("res", m)] = (vS, vS.n)
        EM["h1c_used"] = (vS, vS.n)

    k.barrier(drain=[ld, gld])
    sE4.close()

    if debug_phase == "E4":
        d1 = dbg("h2raw", [128, 8, TOK])
        k.wait_at("sync", vS, vS.n)
        A("sync", lambda e: e.dma_start(out=d1[:], in_=resf[:]), inc=st, inc_n=16)
        k.wait_at("sync", st, st.n)
        k.emit()
        return nc, dbg_outs

    mgm16_park = scr("mgm16_park", [128, 16, NE], BF16)
    mgm32_park = scr("mgm32_park", [128, 16, NE], F32)
    gatec_park = scr("gatec_park", [128, 16], F32)
    h2tb = SE3("e_h2tb", [128, 8, 512], BF16)
    wg_sb = SE3("e_wg", [128, 8, NE])
    lgs = SE3("e_lgs", [8, 512])
    lgtok = SE3("e_lgtok", [128, 16, NE])
    gmx = SE3("e_gmx", [128, 16, 3])
    getm = SE3("e_getm", [128, NE])
    mgm = SE3("e_mgm", [128, 16, NE])
    mgm16 = SE3("e_mgm16", [128, 16, NE], BF16)
    gatecs = SE3("e_gatecs", [128, 16])
    sb_tiebm = SE3("e_tiebm", [128, 16, NE])
    wg_r = wg.rearrange("(c p) e -> p c e", p=128)
    A("sync", lambda e: e.dma_start(out=wg_sb[:], in_=wg_r[:]), inc=ld, inc_n=16)
    A("sync", lambda e: e.dma_start(out=sb_tiebm[:], in_=tiebm[:]), inc=ld, inc_n=16)

    for j in range(TOK // 512):
        ln_feature(lnA, lambda cch, j=j: resf[:, cch, 512 * j:512 * (j + 1)], j, 1,
                   lambda cch: h2tb[:, cch, :], h3_park, h2b_park, h2tb[:])
        for cch in range(8):
            A("tensor", lambda e, cch=cch: e.matmul(
                PSB[4][0:8, :], wg_sb[:, cch, :], lnA["f32t"][:, cch, :],
                start=(cch == 0), stop=(cch == 7)),
              waits=(W(vS, sS, ld) if cch == 0 else []), inc=tS)
        A("vector", lambda e: e.tensor_copy(lgs[:], PSB[4][0:8, :]), waits=W(tS), inc=vS)
        for c2 in range(4):
            A("tensor", lambda e, c2=c2: e.transpose(
                PSB[5][0:128, 0:8], lgs[:, 128 * c2:128 * (c2 + 1)], sb_ident[0:8, 0:8]),
              waits=W(vS), inc=tS)
            A("vector", lambda e, j=j, c2=c2: e.tensor_copy(
                lgtok[:, 4 * j + c2, :], PSB[5][0:128, 0:8]), waits=W(tS), inc=vS)

    A("vector", lambda e: e.tensor_tensor(
        lgtok[:], lgtok[:], sb_tiebm[:], ALU.add), waits=W(ld), inc=vS)
    for ch in range(16):
        A("vector", lambda e, ch=ch: e.reduce_max(gmx[:, ch, 0:1], lgtok[:, ch, :],
                                                  axis=AX.X), inc=vS)
        A("vector", lambda e, ch=ch: e.reduce_max(gmx[:, ch, 1:2], lgtok[:, ch, :],
                                                  axis=AX.X, negate=True), inc=vS)
        fv()
        A("scalar", lambda e, ch=ch: e.activation(getm[:], lgtok[:, ch, :], AF.Exp,
                                                  bias=gmx[:, ch, 1:2],
                                                  accum_out=gmx[:, ch, 2:3]),
          waits=W(vS), inc=sS)
        fs()
        fs()
        A("vector", lambda e, ch=ch: e.reciprocal(gmx[:, ch, 2:3], gmx[:, ch, 2:3]),
          waits=W(sS), inc=vS)
        for ee in range(NE):
            A("vector", lambda e, ch=ch, ee=ee: e.tensor_tensor(
                mgm[:, ch, ee:ee + 1], lgtok[:, ch, ee:ee + 1], gmx[:, ch, 0:1],
                ALU.is_ge), inc=vS)
    A("vector", lambda e: e.tensor_copy(mgm16[:], mgm[:]), inc=vS)
    A("vector", lambda e: e.tensor_copy(gatecs[:], gmx[:, :, 2:3]), inc=vS)
    fv()
    fv()
    A("sync", lambda e: e.dma_start(out=mgm16_park[:], in_=mgm16[:]),
      waits=W(vS), inc=ld, inc_n=16)
    A("sync", lambda e: e.dma_start(out=mgm32_park[:], in_=mgm[:]),
      inc=ld, inc_n=16)
    A("sync", lambda e: e.dma_start(out=gatec_park[:], in_=gatecs[:]),
      inc=ld, inc_n=16)

    if debug_phase == "E5":
        d1 = dbg("h2", [128, 8, TOK], BF16)
        d2 = dbg("mgm", [128, 16, NE], BF16)
        d3 = dbg("gatec", [128, 16])
        k.wait_at("sync", vS, vS.n)
        k.wait_at("sync", ld, ld.n)
        A("sync", lambda e: e.dma_start(out=d1[:], in_=h2b_park[:]), inc=st, inc_n=16)
        A("sync", lambda e: e.dma_start(out=d2[:], in_=mgm16_park[:]), inc=st, inc_n=16)
        A("sync", lambda e: e.dma_start(out=d3[:], in_=gatec_park[:]), inc=st, inc_n=16)
        k.wait_at("sync", st, st.n)
        k.emit()
        return nc, dbg_outs

    k.barrier(drain=[ld, gld])
    sE35.close()
    sE25.close()

    flgd = scr("flgd", [NE, 128], mybir.dt.int32)
    sMO = ExitStack()
    moe_acc = sMO.enter_context(nc.sbuf_tensor("e_moeacc", [128, 8, TOK], BF16))
    sE6 = ExitStack()
    SE6 = lambda n, s, d=F32: sE6.enter_context(nc.sbuf_tensor(n, list(s), d))
    hT = SE6("e_hT", [128, NTI, D], BF16)
    grank_bc = SE6("e_grankbc", [128, TOK])
    gate_bc = SE6("e_gatebc", [128, TOK], BF16)
    we1h = [SE6(f"e_we1h{i}", [128, 8, 512], BF16) for i in range(2)]
    we2h = [SE6(f"e_we2h{i}", [128, 8, 512], BF16) for i in range(2)]
    mhid = SE6("e_mhid", [128, 32, 512], BF16)
    ghc = SE6("e_ghc", [128, 8, 512], BF16)
    eout = SE6("e_eout", [128, 4, D], BF16)
    psm = SE6("e_psm", [128, 4, TOK], BF16)
    ptb = [SE6(f"e_ptb{i}", [128, 512], BF16) for i in range(2)]
    hstrf = [SE6(f"e_hstrf{i}", [128, 8, 128]) for i in range(2)]
    mgm16s = SE6("e_mgm16s", [128, 16, NE], BF16)
    mgm32s = SE6("e_mgm32s", [128, 16, NE])
    gatecb = SE6("e_gatecb", [128, 16])
    rankT = SE6("e_rankT", [128, 16, NE])
    grankT = SE6("e_grankT", [128, 16])
    growrow = SE6("e_growrow", [1, TOK])
    gaterow = SE6("e_gaterow", [1, TOK], BF16)
    cntc = SE6("e_cntc", [8, 1])
    e_flg = SE6("e_flg", [8, 128])
    e_flgi = sE6.enter_context(nc.sbuf_tensor("e_flgi", [8, 128], mybir.dt.int32))
    sb_trio = SE6("e_trio", [128, 128], BF16)
    sb_aon1 = SE6("e_aon1", [128, 128], BF16)
    sb_eoffm = SE6("e_eoffm", [128, 16, NE])
    sb_offm = SE6("e_offm", [NE, 128])
    sb_iom = SE6("e_iom", [128, 512])
    sb_iosc = SE6("e_iosc", [128, 1])
    bex = [SE6(f"e_bex{i}", [128, 32]) for i in range(2)]
    be2rw = [SE6("e_be2rw0", [1, D])] * 2

    A("sync", lambda e: e.dma_start(out=sb_trio[:], in_=trio[:]), inc=ld, inc_n=16)
    A("sync", lambda e: e.dma_start(out=sb_aon1[:], in_=aon1[:]), inc=ld, inc_n=16)
    A("sync", lambda e: e.dma_start(out=sb_eoffm[:], in_=eoffm[:]), inc=ld, inc_n=16)
    A("sync", lambda e: e.dma_start(out=sb_offm[:], in_=offm[:]), inc=ld, inc_n=16)
    A("sync", lambda e: e.dma_start(out=sb_iom[:], in_=iom[:]), inc=ld, inc_n=16)
    A("sync", lambda e: e.dma_start(out=sb_iosc[:], in_=iosc[:]), inc=ld, inc_n=16)
    A("sync", lambda e: e.dma_start(out=mgm16s[:], in_=mgm16_park[:]), inc=ld, inc_n=16)
    A("sync", lambda e: e.dma_start(out=mgm32s[:], in_=mgm32_park[:]),
      inc=ld, inc_n=16)
    A("sync", lambda e: e.dma_start(out=gatecb[:], in_=gatec_park[:]), inc=ld, inc_n=16)
    pre_ld = (ld, ld.n)
    pre_gld = (gld, gld.n)
    for m in range(8):
        A("vector", lambda e, m=m: e.tensor_tensor(
            moe_acc[:, m, :], moe_acc[:, m, :], moe_acc[:, m, :], ALU.is_lt), inc=vS)
    for tc in range(NTI):
        hb = tc % 2
        A("sync", lambda e, tc=tc, hb=hb: e.dma_start(
            out=hstrf[hb][:], in_=h3_park[:, :, 128 * tc:128 * (tc + 1)]),
          waits=emw(("hstr_used", hb)), inc=ld, inc_n=16)
        hs_ld = (ld, ld.n)
        for cch in range(8):
            pb = cch % 2
            A("tensor", lambda e, tc=tc, cch=cch, pb=pb, hb=hb: e.transpose(
                PSB[6 + pb][:, 0:128], hstrf[hb][:, cch, :], sb_ident[:]),
              waits=([hs_ld] if cch == 0 else []) + emw(("htr", pb)), inc=tS)
            A("vector", lambda e, tc=tc, cch=cch, pb=pb: e.tensor_copy(
                hT[:, tc, 128 * cch:128 * (cch + 1)], PSB[6 + pb][:, 0:128]),
              waits=[(tS, tS.n)], inc=vS)
            EM[("htr", pb)] = (vS, vS.n)
        EM[("hstr_used", hb)] = (tS, tS.n)
    for ch in range(16):
        pb = ch % 2
        for ch2 in range(ch + 1):
            A("tensor", lambda e, ch=ch, ch2=ch2, pb=pb: e.matmul(
                PSB[pb][:, 0:8],
                sb_trio[:] if ch2 == ch else sb_aon1[:],
                mgm16s[:, ch2, :], start=(ch2 == 0), stop=(ch2 == ch)),
              waits=([pre_ld] + emw(("rkb", pb)) if ch2 == 0 else []), inc=tS)
        A("vector", lambda e, ch=ch, pb=pb: e.tensor_copy(
            rankT[:, ch, :], PSB[pb][:, 0:8]), waits=[(tS, tS.n)], inc=vS)
        EM[("rkb", pb)] = (vS, vS.n)
    for ch in range(16):
        A("tensor", lambda e, ch=ch: e.matmul(
            PSB[2][0:8, 0:1], mgm16s[:, ch, :], sb_aon1[:, 0:1],
            start=(ch == 0), stop=(ch == 15)),
          waits=([pre_ld] if ch == 0 else []), inc=tS)
    A("vector", lambda e: e.tensor_copy(cntc[:], PSB[2][0:8, 0:1]),
      waits=[(tS, tS.n)], inc=vS)
    fv()
    A("vector", lambda e: e.tensor_single_scalar(
        e_flg[:], sb_offm[:], cntc[0:8, 0:1], ALU.is_lt), waits=W(ld), inc=vS)
    A("vector", lambda e: e.tensor_copy(e_flgi[:], e_flg[:]), inc=vS)
    fv()
    fv()
    A("sync", lambda e: e.dma_start(out=flgd[:], in_=e_flgi[:]), waits=W(vS),
      inc=ld, inc_n=16)
    flags_ready = (ld, ld.n)
    A("vector", lambda e: e.tensor_tensor(
        rankT[:], rankT[:], sb_eoffm[:], ALU.add), waits=W(ld, gld), inc=vS)
    A("vector", lambda e: e.tensor_tensor(
        rankT[:], rankT[:], mgm32s[:], ALU.mult), inc=vS)
    for ch in range(16):
        A("vector", lambda e, ch=ch: e.reduce_sum(
            grankT[:, ch:ch + 1], rankT[:, ch, :], axis=AX.X), inc=vS)
    fv()
    for ch in range(16):
        pb = ch % 2
        A("tensor", lambda e, ch=ch, pb=pb: e.transpose(
            PSB[4 + pb][0:1, 0:128], grankT[:, ch:ch + 1], sb_ident[:]),
          waits=W(vS) + emw(("grb", pb)), inc=tS)
        A("vector", lambda e, ch=ch, pb=pb: e.tensor_copy(
            growrow[0:1, 128 * ch:128 * (ch + 1)], PSB[4 + pb][0:1, 0:128]),
          waits=[(tS, tS.n)], inc=vS)
        EM[("grb", pb)] = (vS, vS.n)
        A("tensor", lambda e, ch=ch, pb=pb: e.transpose(
            PSB[4 + pb][0:1, 0:128], gatecb[:, ch:ch + 1], sb_ident[:]),
          waits=W(vS) + emw(("gtb", pb)), inc=tS)
        A("vector", lambda e, ch=ch, pb=pb: e.tensor_copy(
            gaterow[0:1, 128 * ch:128 * (ch + 1)], PSB[4 + pb][0:1, 0:128]),
          waits=[(tS, tS.n)], inc=vS)
        EM[("gtb", pb)] = (vS, vS.n)
    fv()
    fv()
    for w in range(4):
        A("tensor", lambda e, w=w: e.matmul(
            PSB[4][:, 0:512], sb_ones_r[:], growrow[0:1, 512 * w:512 * (w + 1)],
            start=True, stop=True), waits=W(vS) + emw("bcb"), inc=tS)
        A("vector", lambda e, w=w: e.tensor_copy(
            grank_bc[:, 512 * w:512 * (w + 1)], PSB[4][:, 0:512]),
          waits=[(tS, tS.n)], inc=vS)
        EM["bcb"] = (vS, vS.n)
        A("tensor", lambda e, w=w: e.matmul(
            PSB[4][:, 0:512], sb_aon1[0:1, 0:128], gaterow[0:1, 512 * w:512 * (w + 1)],
            start=True, stop=True), waits=W(vS) + emw("bcb"), inc=tS)
        A("vector", lambda e, w=w: e.tensor_copy(
            gate_bc[:, 512 * w:512 * (w + 1)], PSB[4][:, 0:512]),
          waits=[(tS, tS.n)], inc=vS)
        EM["bcb"] = (vS, vS.n)
    fv()
    routing_done_v = (vS, vS.n)

    if debug_phase == "E6r":
        d1 = dbg("grankT", [128, 16])
        d2 = dbg("cnt", [8, 1])
        d6 = dbg("mgm32", [128, 16, NE])
        d7 = dbg("rankTf", [128, 16, NE])
        d8 = dbg("h3p", [128, 8, TOK])
        A("sync", lambda e: e.dma_start(out=d8[:], in_=h3_park[:]), inc=st, inc_n=16)
        A("sync", lambda e: e.dma_start(out=d6[:], in_=mgm32s[:]), inc=st, inc_n=16)
        A("sync", lambda e: e.dma_start(out=d7[:], in_=rankT[:]), inc=st, inc_n=16)
        d3 = dbg("grankbc", [128, TOK])
        d4 = dbg("gatebc", [128, TOK], BF16)
        d5 = dbg("hTd", [128, NTI, D], BF16)
        k.wait_at("sync", vS, vS.n)
        k.wait_at("sync", ld, ld.n)
        A("sync", lambda e: e.dma_start(out=d1[:], in_=grankT[:]), inc=st, inc_n=16)
        A("sync", lambda e: e.dma_start(out=d2[:], in_=cntc[:]), inc=st, inc_n=16)
        A("sync", lambda e: e.dma_start(out=d3[:], in_=grank_bc[:]), inc=st, inc_n=16)
        A("sync", lambda e: e.dma_start(out=d4[:], in_=gate_bc[:]), inc=st, inc_n=16)
        A("sync", lambda e: e.dma_start(out=d5[:], in_=hT[:]), inc=st, inc_n=16)
        k.wait_at("sync", st, st.n)
        k.emit()
        return nc, dbg_outs

    JMAP = [0, 1, 2, 2]
    we1r = [we1b[ee] for ee in range(NE)]
    we2r = [we2b[ee] for ee in range(NE)]

    for ee in range(NE):
        eb = ee % 2
        A("sync", lambda e, ee=ee, eb=eb: e.dma_start(
            out=bex[eb][:], in_=be1_c[ee]), waits=emw(("bex_used", eb)),
          inc=ld, inc_n=16)
        A("sync", lambda e, ee=ee, eb=eb: e.dma_start(
            out=be2rw[0][:], in_=be2_r[ee:ee + 1, :]),
          waits=emw("be2u"), inc=ld, inc_n=16)
        bex_ld = (ld, ld.n)
        for si in range(4):
            base_g = 2048 * ee + 512 * si
            fap = (lambda ee=ee, jf=JMAP[si]:
                   flgd[ee:ee + 1, 32 * jf:32 * jf + 1])

            k.begin_region(fap, flags_ready)
            for ph in range(2):
                A("sync", lambda e, ee=ee, ph=ph: e.dma_start(
                    out=we1h[ph][:], in_=we1r[ee][:, :, 512 * ph:512 * (ph + 1)]),
                  waits=[wpark_k] + emw(("w1u", ph)), inc=ld, inc_n=16)
                EM[("w1l", ph)] = (ld, ld.n)
            for tc in range(NTI):
                pb = tc % 2
                A("vector", lambda e, tc=tc, pb=pb, base_g=base_g: e.tensor_scalar(
                    ptb[pb][:], sb_iom[:], grankT[:, tc:tc + 1], float(-base_g),
                    ALU.subtract, ALU.is_equal),
                  waits=emw(("ptu", pb)), inc=vS)
                ptk = (vS, vS.n)
                for m in range(8):
                    A("tensor", lambda e, tc=tc, m=m, pb=pb: e.matmul(
                        PSB[m][:], hT[:, tc, 128 * m:128 * (m + 1)], ptb[pb][:],
                        start=(tc == 0), stop=(tc == NTI - 1)),
                      waits=(([ptk] if m == 0 else [])
                             + (emw(("sct", m)) if tc == 0 else [])), inc=tS)
                EM[("ptu", pb)] = (tS, tS.n)
            gk = (tS, tS.n)
            for m in range(8):
                A("vector", lambda e, m=m: e.tensor_copy(ghc[:, m, :], PSB[m][:]),
                  waits=[gk] + (emw("ghu") if m == 0 else []), inc=vS)
            ghk = (vS, vS.n)
            for fc in range(32):
                pf = fc % 4
                ph = fc // 4
                for cch in range(8):
                    A("tensor", lambda e, fc=fc, cch=cch, pf=pf, ph=ph: e.matmul(
                        PSB[pf][:], we1h[ph % 2][:, cch, 128 * (fc % 4):128 * (fc % 4 + 1)],
                        ghc[:, cch, :], start=(cch == 0), stop=(cch == 7)),
                      waits=(([EM[("w1l", ph % 2)], ghk, bex_ld] + emw(("mb", pf)))
                             if cch == 0 else []), inc=tS)
                ff1k = (tS, tS.n)
                if fc % 4 == 3:
                    EM[("w1u", ph % 2)] = ff1k
                    if ph + 2 < 8:
                        A("sync", lambda e, ee=ee, ph=ph: e.dma_start(
                            out=we1h[ph % 2][:],
                            in_=we1r[ee][:, :, 512 * (ph + 2):512 * (ph + 3)]),
                          waits=emw(("w1u", ph % 2)), inc=ld, inc_n=16)
                        EM[("w1l", ph % 2)] = (ld, ld.n)
                A("vector", lambda e, fc=fc, pf=pf, eb=eb: e.tensor_scalar(
                    mhid[:, fc, :], PSB[pf][:], bex[eb][:, fc:fc + 1], 0.0,
                    ALU.add, ALU.max),
                  waits=[ff1k, bex_ld] + emw(("mhu", fc)), inc=vS)
                EM[("mb", pf)] = (vS, vS.n)
            relu_k = (vS, vS.n)
            EM["ghu"] = (tS, tS.n)
            k.end_region()

            k.begin_region(fap, flags_ready)
            for sc in range(4):
                A("vector", lambda e, sc=sc, base_g=base_g: e.tensor_scalar(
                    psm[:, sc, :], grank_bc[:], sb_iosc[:, 0:1],
                    float(base_g + 128 * sc), ALU.subtract, ALU.is_equal),
                  waits=emw("psmu") if sc == 0 else [], inc=vS)
                A("vector", lambda e, sc=sc: e.tensor_tensor(
                    psm[:, sc, :], psm[:, sc, :], gate_bc[:], ALU.mult), inc=vS)
            psm_k = (vS, vS.n)
            for half in range(2):
                for g4 in range(4):
                    pw = (half * 4 + g4) % 2
                    A("sync", lambda e, ee=ee, g4=g4, half=half, pw=pw: e.dma_start(
                        out=we2h[pw][:],
                        in_=we2r[ee][:, 8 * g4:8 * (g4 + 1), 512 * half:512 * (half + 1)]),
                      waits=emw(("w2u", pw)), inc=ld, inc_n=16)
                    EM[("w2l", pw)] = (ld, ld.n)
                    for fc8 in range(8):
                        fc = 8 * g4 + fc8
                        for sc in range(4):
                            bank = (4 + sc) if half == 0 else sc
                            A("tensor", lambda e, fc=fc, sc=sc, bank=bank, pw=pw,
                              half=half: e.matmul(
                                PSB[bank][:], mhid[:, fc, 128 * sc:128 * (sc + 1)],
                                we2h[pw][:, fc % 8, :],
                                start=(fc == 0), stop=False),
                              waits=(([relu_k, EM[("w2l", pw)]]
                                      + (emw(("scv", 7)) if (fc == 0 and half == 0)
                                         else []))
                                     if (fc8 == 0 and sc == 0) else []), inc=tS)
                        if fc8 == 7:
                            EM[("w2u", pw)] = (tS, tS.n)
                        if half == 1:
                            EM[("mhu", fc)] = (tS, tS.n)
                for sc in range(4):
                    bank = (4 + sc) if half == 0 else sc
                    A("tensor", lambda e, sc=sc, bank=bank, half=half, eb=eb: e.matmul(
                        PSB[bank][:], sb_ones_r[:],
                        be2rw[eb][0:1, 512 * half:512 * (half + 1)],
                        start=False, stop=True), inc=tS)
                ff2k = (tS, tS.n)
                if half == 1:
                    EM["be2u"] = ff2k
                for sc in range(4):
                    bank = (4 + sc) if half == 0 else sc
                    A("vector", lambda e, sc=sc, bank=bank, half=half: e.tensor_copy(
                        eout[:, sc, 512 * half:512 * (half + 1)], PSB[bank][:]),
                      waits=[ff2k] + (emw("eou") if half + sc == 0 else []), inc=vS)
            eo_k = (vS, vS.n)
            for tp in range(4):
                for m in range(8):
                    for sc in range(4):
                        A("tensor", lambda e, tp=tp, m=m, sc=sc: e.matmul(
                            PSB[m][:], eout[:, sc, 128 * m:128 * (m + 1)],
                            psm[:, sc, 512 * tp:512 * (tp + 1)],
                            start=(sc == 0), stop=(sc == 3)),
                          waits=(([eo_k, psm_k] + emw(("scv", m)))
                                 if sc == 0 else []), inc=tS)
                    sck = (tS, tS.n)
                    A("vector", lambda e, tp=tp, m=m: e.tensor_tensor(
                        moe_acc[:, m, 512 * tp:512 * (tp + 1)],
                        moe_acc[:, m, 512 * tp:512 * (tp + 1)], PSB[m][:], ALU.add),
                      waits=[sck], inc=vS)
                    EM[("scv", m)] = (vS, vS.n)
                    if tp == 3:
                        EM[("sct", m)] = (vS, vS.n)
            EM["psmu"] = (tS, tS.n)
            EM["eou"] = (tS, tS.n)
            k.end_region()
        EM[("bex_used", eb)] = (vS, vS.n)

    k.barrier(drain=[ld, gld, aS])

    if debug_phase == "E6":
        d1 = dbg("moe", [128, 8, TOK], BF16)
        d2 = dbg("cnt", [8, 1])
        d6 = dbg("mgm32", [128, 16, NE])
        A("sync", lambda e: e.dma_start(out=d1[:], in_=moe_acc[:]), inc=st, inc_n=16)
        A("sync", lambda e: e.dma_start(out=d2[:], in_=cntc[:]), inc=st, inc_n=16)
        A("sync", lambda e: e.dma_start(out=d6[:], in_=mgm32s[:]), inc=st, inc_n=16)
        k.wait_at("sync", st, st.n)
        k.emit()
        return nc, dbg_outs

    sE6.close()


    sE7F = ExitStack()
    gT = sE7F.enter_context(nc.sbuf_tensor("f_gT", [128, 8, TOK], F32))
    sE7 = ExitStack()
    SE7 = lambda n, s, d=F32: sE7.enter_context(nc.sbuf_tensor(n, list(s), d))
    h316c = [SE7(f"e_h316c{i}", [128, 8, 512], BF16) for i in range(2)]
    h3c = SE7("e_h3c", [128, 8, 512])
    wof_sb = SE7("e_wof", [128, 8, D], BF16)
    lnB = mk_lnbufs(sE7, "lnB")
    A("gpsimd", lambda e: e.dma_start(
        out=wof_sb[:], in_=wout_f.rearrange("(c p) e -> p c e", p=128)),
      inc=gld, inc_n=16)
    wof_ld = (gld, gld.n)

    for j in range(TOK // 512):
        A("sync", lambda e, j=j: e.dma_start(
            out=h3c[:], in_=h3_park[:, :, 512 * j:512 * (j + 1)]),
          waits=W(tS, sS, vS), inc=ld, inc_n=16)
        A("vector", lambda e, j=j: e.tensor_tensor(
            h3c[:], h3c[:], moe_acc[:, :, 512 * j:512 * (j + 1)],
            ALU.add), waits=W(ld), inc=vS)
        ln_feature(lnB, lambda cch: h3c[:, cch, :], j, 2,
                   lambda cch, j=j: h316c[j % 2][:, cch, :],
                   None, None, None)
        ln_k = (vS, vS.n)
        for m in range(8):
            pm = m % 2
            for cch in range(8):
                A("tensor", lambda e, j=j, m=m, cch=cch, pm=pm: e.matmul(
                    PSB[2 + pm][:], wof_sb[:, cch, 128 * m:128 * (m + 1)],
                    h316c[j % 2][:, cch, :],
                    start=(cch == 0), stop=(cch == 7)),
                  waits=(([wof_ld, ln_k] + emw(("gb", pm)) +
                          ([(sS, sS.n)] if (j + m == 0) else []))
                         if cch == 0 else []), inc=tS)
            A("scalar", lambda e, j=j, m=m, pm=pm: e.activation(
                gT[:, m, 512 * j:512 * (j + 1)], PSB[2 + pm][:], AF.Silu),
              waits=[(tS, tS.n)], inc=sS)
            EM[("gb", pm)] = (sS, sS.n)

    k.barrier(drain=[ld, gld])
    sE7.close()

    sF = ExitStack()
    SF = lambda n, s, d=F32: sF.enter_context(nc.sbuf_tensor(n, list(s), d))
    cnc = [SF(f"f_cnc{i}", [128, 8, 512], BF16) for i in range(2)]
    woa_sb = SF("f_woa", [128, 8, D], BF16)
    prod = SF("f_prod", [128, 8, 512])
    otok = SF("f_otok", [128, D])
    xqt = SF("f_xqt", [128, D])
    A("gpsimd", lambda e: e.dma_start(
        out=woa_sb[:], in_=wout_a.rearrange("(c p) e -> p c e", p=128)),
      inc=gld, inc_n=16)

    woa_ld = (gld, gld.n)
    for j in range(TOK // 512):
        A("sync", lambda e, j=j: e.dma_start(
            out=cnc[j % 2][:], in_=cn_park[:, :, 512 * j:512 * (j + 1)]),
          waits=emw(("cnc_used", j % 2)), inc=ld, inc_n=16)
        cn_ld = (ld, ld.n)
        for m in range(8):
            pm = m % 2
            for cch in range(8):
                A("tensor", lambda e, j=j, m=m, cch=cch, pm=pm: e.matmul(
                    PSB[pm][:], woa_sb[:, cch, 128 * m:128 * (m + 1)],
                    cnc[j % 2][:, cch, :], start=(cch == 0), stop=(cch == 7)),
                  waits=(([cn_ld, woa_ld] + emw(("sb", pm)) +
                          ([(vS, vS.n), (sS, sS.n)] if (j + m == 0) else []))
                         if cch == 0 else []), inc=tS)
            if m == 7:
                EM[("cnc_used", j % 2)] = (tS, tS.n)
            A("vector", lambda e, j=j, m=m, pm=pm: e.tensor_tensor(
                prod[:, m, :], PSB[pm][:], gT[:, m, 512 * j:512 * (j + 1)], ALU.mult),
              waits=[(tS, tS.n)] + emw(("prod_used", m)), inc=vS)
            EM[("sb", pm)] = (vS, vS.n)
        for s2 in range(4):
            A("sync", lambda e, j=j, s2=s2: e.dma_start(
                out=xqt[:], in_=xq[512 * j + 128 * s2:512 * j + 128 * (s2 + 1), :]),
              waits=emw("xqt_used"), inc=ld, inc_n=16)
            xq_ld = (ld, ld.n)
            for m in range(8):
                pm = 2 + (m % 2)
                A("tensor", lambda e, s2=s2, m=m, pm=pm, j=j: e.transpose(
                    PSB[pm][:, 0:128], prod[:, m, 128 * s2:128 * (s2 + 1)], sb_ident[:]),
                  waits=(emw(("sb", 0)) + emw(("sb", 1)) if (s2 == 0 and m == 0) else [])
                        + emw(("fb", pm)), inc=tS)
                if s2 == 3:
                    EM[("prod_used", m)] = (tS, tS.n)
                A("vector", lambda e, m=m, pm=pm: e.tensor_copy(
                    otok[:, 128 * m:128 * (m + 1)], PSB[pm][:, 0:128]),
                  waits=[(tS, tS.n)] + (emw("otok_stored") if m == 0 else []), inc=vS)
                EM[("fb", pm)] = (vS, vS.n)
            A("vector", lambda e: e.tensor_add(otok[:], otok[:], xqt[:]),
              waits=[xq_ld], inc=vS)
            EM["xqt_used"] = (vS, vS.n)
            A("sync", lambda e, j=j, s2=s2: e.dma_start(
                out=out[512 * j + 128 * s2:512 * j + 128 * (s2 + 1), :], in_=otok[:]),
              waits=[(vS, vS.n)], inc=st, inc_n=16)
            EM["otok_stored"] = (st, st.n)

    k.wait_at("sync", st, st.n)
    k.emit()
    return nc, dbg_outs



def _pe_table():
    pos = np.arange(S - 1, -1, -1, dtype=np.float32)
    inv_freq = (1.0 / 10000.0 ** (np.arange(0, D, 2, dtype=np.float32) / D)).astype(np.float32)
    sinusoid = pos[:, None] * inv_freq[None, :]
    return np.concatenate([np.sin(sinusoid), np.cos(sinusoid)], axis=-1).astype(np.float32)


_CACHED = {}


def _get_program():
    if "nc" not in _CACHED:
        _CACHED["nc"] = build_program(None)[0]
    return _CACHED["nc"]


def make_inmaps(inputs):
    x = np.ascontiguousarray(inputs["L0_attn_x"])

    pe = _pe_table()
    peT_full = pe.T
    ii = np.arange(128)[:, None]
    cc = np.arange(256)[None, :]
    ident = np.eye(128, dtype=np.float32)
    rmask = (np.arange(128)[:, None] >= np.arange(128)[None, :]).astype(np.float32)

    gq = np.zeros((64, NH, 512), np.float32)
    gk = np.zeros((64, NH, 512), np.float32)
    jj = np.arange(512) % 128
    for h in range(NH):
        g = GAMMAS[h]
        gq[:, h, :] = 0.125 * (g ** jj)[None, :]
        gk[:, h, :] = (g ** (-jj.astype(np.float64))).astype(np.float32)[None, :]

    com = dict(
        peT=None,
        rmask=rmask, ident=ident,
        gq_t=gq, gk_t=gk,
        wq_a=inputs["L0_attn_Wq"], wk_a=inputs["L0_attn_Wk"], wv_a=inputs["L0_attn_Wv"],
        gnw_t=np.ascontiguousarray(inputs["L0_attn_gn_weight"].reshape(NH, 64).T),
        gnb_t=np.ascontiguousarray(inputs["L0_attn_gn_bias"].reshape(NH, 64).T),
        wout_a=inputs["L0_attn_Wout"],
        wqkv=inputs["L0_ffn_Wqkv"], wr=inputs["L0_ffn_Wr"],
        ub_t=np.ascontiguousarray(inputs["L0_ffn_u_bias"].T),
        vb_t=np.ascontiguousarray(inputs["L0_ffn_v_bias"].T),
        wo=inputs["L0_ffn_Wo"],
        ln_g=np.stack([inputs["L0_ffn_ln1_g"].reshape(8, 128).T,
                       inputs["L0_ffn_ln2_g"].reshape(8, 128).T,
                       inputs["L0_ffn_ln3_g"].reshape(8, 128).T], axis=1),
        ln_b=np.stack([inputs["L0_ffn_ln1_b"].reshape(8, 128).T,
                       inputs["L0_ffn_ln2_b"].reshape(8, 128).T,
                       inputs["L0_ffn_ln3_b"].reshape(8, 128).T], axis=1),
        wff1=inputs["L0_ffn_Wff1"],
        bff1_c=np.ascontiguousarray(inputs["L0_ffn_bff1"].reshape(32, 128).T),
        wff2=inputs["L0_ffn_Wff2"],
        bff2_c=np.ascontiguousarray(inputs["L0_ffn_bff2"].reshape(8, 128).T),
        wg=inputs["L0_ffn_Wg"],
        we1=inputs["L0_ffn_We1"],
        be1_c=np.ascontiguousarray(inputs["L0_ffn_be1"].reshape(NE, 32, 128).transpose(0, 2, 1)),
        we2=inputs["L0_ffn_We2"],
        be2_r=inputs["L0_ffn_be2"],
        wout_f=inputs["L0_ffn_Wout"],
        eoffm=np.broadcast_to(2048.0 * np.arange(NE, dtype=np.float32)[None, None, :],
                              (128, 16, NE)),
        offm=np.broadcast_to(
            np.repeat(np.array([0.0, 512.0, 1024.0, 1e9], np.float32), 32)[None, :],
            (NE, 128)),
        iom=np.broadcast_to(np.arange(512, dtype=np.float32)[None, :], (128, 512)),
        iosc=np.arange(128, dtype=np.float32)[:, None],
        tiebm=np.broadcast_to(
            ((7 - np.arange(NE, dtype=np.float32)) * 1e-6)[None, None, :],
            (128, 16, NE)),
    )
    com = {kk: (np.ascontiguousarray(v, dtype=np.float32) if v is not None else None)
           for kk, v in com.items()}
    com["trio"] = np.triu(np.ones((128, 128)), 1).astype(ml_dtypes.bfloat16)
    com["aon1"] = np.ones((128, 128), dtype=ml_dtypes.bfloat16)
    com["spillq"] = np.where(np.arange(128)[None, :] > np.arange(128)[:, None],
                             NEG, 0.0).astype(ml_dtypes.bfloat16)
    com["negt"] = np.full((128, 128), NEG, dtype=ml_dtypes.bfloat16)

    in_maps = []
    for c in range(8):
        b, p = c // 2, c % 2
        rows = np.concatenate([np.arange(256 * t + 128 * p, 256 * t + 128 * p + 128)
                               for t in range(NTI)])
        xqc = np.ascontiguousarray(x[b][rows])
        peTc = np.zeros((D, S + 256), np.float32)
        peTc[:, 128 * p:128 * p + S] = peT_full
        peTc_in = np.ascontiguousarray(
            peTc.reshape(8, 128, S + 256).transpose(1, 0, 2)).astype(ml_dtypes.bfloat16)
        am = np.where(cc <= ii + 128 * p, 0.0, NEG).astype(np.float32)
        sel = np.zeros((64, 2), np.float32)
        sel[:, p] = 1.0
        m = dict(com)
        m["peT"] = peTc_in
        m["pmq"] = np.full((128, 128), NEG if p == 0 else 0.0,
                           dtype=ml_dtypes.bfloat16)
        m["xb"] = np.ascontiguousarray(x[b])
        m["xq"] = xqc
        m["amask"] = am
        m["selc"] = sel
        in_maps.append(m)
    return in_maps


def kernel(**inputs):
    in_maps = make_inmaps(inputs)
    nc = _get_program()
    res = run_bass_kernel_spmd(nc, in_maps, core_ids=list(range(8)))

    outx = np.zeros((4, S, D), np.float32)
    for c in range(8):
        b, p = c // 2, c % 2
        oc = np.asarray(res.results[c]["out"], dtype=np.float32)
        for t in range(NTI):
            outx[b, 256 * t + 128 * p:256 * t + 128 * p + 128] = oc[128 * t:128 * (t + 1)]
    return outx


if __name__ == "__main__":
    build_program()


# version 25
# speedup vs baseline: 1.0315x; 1.0067x over previous
import numpy as np
import ml_dtypes
import concourse.bass as bass
import concourse.mybir as mybir
from concourse.bass_utils import run_bass_kernel_spmd
from concourse.ap import AP
from contextlib import ExitStack

F32 = mybir.dt.float32
BF16 = mybir.dt.bfloat16
AF = mybir.ActivationFunctionType
ALU = mybir.AluOpType
AX = mybir.AxisListType

S, D, NH, DH, FFD, NE = 4096, 1024, 16, 64, 4096, 8
TOK = 2048
NTI = 16
EPS = 1e-5
NEG = -30000.0
GAMMAS = [1.0 - 2.0 ** (-5 - (h // 2)) for h in range(NH)]

ENGS = ("sync", "tensor", "vector", "scalar", "gpsimd")


class Sem:
    __slots__ = ("h", "n", "name")

    def __init__(self, h, name):
        self.h = h
        self.n = 0
        self.name = name


COND_ENGS = ("tensor", "vector", "sync")


class KB:
    def __init__(self, nc, es):
        self.nc = nc
        self.es = es
        self.ops = {e: [] for e in ENGS}
        self.bar = None
        self.region = None
        self.regs = None

    def sem(self, name):
        return Sem(self.es.enter_context(self.nc.semaphore(name)), name)

    def add(self, eng, build, waits=(), inc=None, inc_n=1):
        wl = [(s.h, v) for (s, v) in waits]

        def f(e, wl=wl, build=build, inc=inc, inc_n=inc_n):
            for h, v in wl:
                e.wait_ge(h, v)
            ins = build(e)
            if inc is not None:
                ins.then_inc(inc.h, inc_n)
        self.ops[eng].append([f, build, wl, inc, inc_n])
        if inc is not None:
            inc.n += inc_n
            if self.region is not None and eng in self.region:
                t = self.region[eng]
                t[inc] = t.get(inc, 0) + inc_n

    def begin_region(self, flag_ap_fn, ready):
        assert self.region is None
        self.region = {eng: {} for eng in COND_ENGS}
        for eng in COND_ENGS:
            self.ops[eng].append(["IF", flag_ap_fn, (ready[0].h, ready[1])])

    def end_region(self):
        for eng in COND_ENGS:
            comp = self.region[eng]
            self.ops[eng].append(["ELSE"])
            for sem, n in comp.items():
                self.ops[eng].append(["COMP", sem.h, n])
            self.ops[eng].append(["END"])
        self.region = None

    def w(self, *sems):
        return [(s, s.n) for s in sems]

    def wait_at(self, eng, sem, v):
        self.ops[eng].append([lambda e, h=sem.h, vv=v: e.wait_ge(h, vv), None, [], None, 0])

    def barrier(self, drain=()):
        if self.bar is None:
            self.bar = self.sem("bar")
        for eng in ENGS:
            q = self.ops[eng]
            if eng in ("sync", "gpsimd"):
                for s in drain:
                    self.wait_at(eng, s, s.n)
            q.append([lambda e: e.sem_inc(self.bar.h, 1), None, [], None, 0])
            self.bar.n += 1
        target = self.bar.n
        for eng in ENGS:
            self.wait_at(eng, self.bar, target)

    def emit(self):
        nreg = {eng: [getattr(self.nc, eng).alloc_register(f"fl_{eng}{i}")
                      for i in range(4)] for eng in COND_ENGS}
        with self.nc.Block() as blk:
            for eng in ENGS:
                items = self.ops[eng]
                regs = nreg.get(eng)

                def run(e, items=items, regs=regs):
                    guard = None
                    ri = 0
                    for it in items:
                        tag = it[0]
                        if tag == "IF":
                            _, apf, rd = it
                            e.wait_ge(rd[0], rd[1])
                            reg = regs[ri % 4]
                            ri += 1
                            e.reg_load(reg, apf())
                            guard = e.If_ne(reg, 0)
                            guard.__enter__()
                        elif tag == "ELSE":
                            guard.__exit__(None, None, None)
                            guard = e.Else()
                            guard.__enter__()
                        elif tag == "COMP":
                            e.sem_inc(it[1], it[2])
                        elif tag == "END":
                            guard.__exit__(None, None, None)
                            guard = None
                        else:
                            it[0](e)
                getattr(blk, eng)(run)


def cdiv(a, b):
    return (a + b - 1) // b



def build_program(debug_phase=None):
    nc = bass.Bass("TRN2", target_bir_lowering=False, debug=False)
    es = ExitStack()
    k = KB(nc, es)
    A = k.add
    W = k.w

    def inp(name, shape, dt=F32):
        return nc.dram_tensor(name, list(shape), dt, kind="ExternalInput").ap()

    xb = inp("xb", [S, D])
    xq = inp("xq", [TOK, D])
    peT = inp("peT", [128, 8, S + 256], BF16)
    amask = inp("amask", [128, 256])
    rmask = inp("rmask", [128, 128])
    ident_i = inp("ident", [128, 128])
    selc = inp("selc", [64, 2])
    gq_t = inp("gq_t", [64, NH, 512])
    gk_t = inp("gk_t", [64, NH, 512])
    wq_a = inp("wq_a", [NH, D, DH])
    wk_a = inp("wk_a", [NH, D, DH])
    wv_a = inp("wv_a", [NH, D, DH])
    gnw_t = inp("gnw_t", [64, NH])
    gnb_t = inp("gnb_t", [64, NH])
    wout_a = inp("wout_a", [D, D])
    wqkv = inp("wqkv", [D, 3 * D])
    wr = inp("wr", [D, D])
    ub_t = inp("ub_t", [DH, NH])
    vb_t = inp("vb_t", [DH, NH])
    wo = inp("wo", [D, D])
    ln_g = inp("ln_g", [128, 3, 8])
    ln_b = inp("ln_b", [128, 3, 8])
    wff1 = inp("wff1", [D, FFD])
    bff1_c = inp("bff1_c", [128, 32])
    wff2 = inp("wff2", [FFD, D])
    bff2_c = inp("bff2_c", [128, 8])
    wg = inp("wg", [D, NE])
    we1 = inp("we1", [NE, D, FFD])
    be1_c = inp("be1_c", [NE, 128, 32])
    we2 = inp("we2", [NE, FFD, D])
    be2_r = inp("be2_r", [NE, D])
    wout_f = inp("wout_f", [D, D])
    trio = inp("trio", [128, 128], BF16)
    aon1 = inp("aon1", [128, 128], BF16)
    eoffm = inp("eoffm", [128, 16, NE])
    offm = inp("offm", [NE, 128])
    iom = inp("iom", [128, 512])
    iosc = inp("iosc", [128, 1])
    tiebm = inp("tiebm", [128, 16, NE])
    spillq = inp("spillq", [128, 128], BF16)
    pmq = inp("pmq", [128, 128], BF16)
    negt = inp("negt", [128, 128], BF16)

    out = nc.dram_tensor("out", [TOK, D], F32, kind="ExternalOutput").ap()

    def scr(name, shape, dt=F32):
        return nc.dram_tensor(name, list(shape), dt, kind="Internal").ap()

    pskew = [scr("pskew0", [128 * (S + 256)], BF16), scr("pskew1", [128 * (S + 256)], BF16)]
    pskew_dbg = scr("pskew_dbg", [128 * 256], BF16)
    ctx_park = scr("ctx_park", [128, 8, TOK], BF16)
    cn_park = scr("cn_park", [128, 8, TOK], BF16)
    h1_park = scr("h1_park", [128, 8, TOK], F32)
    h2b_park = scr("h2b_park", [128, 8, TOK], BF16)
    h3_park = scr("h3_park", [128, 8, TOK], F32)

    dbg_outs = {}

    def dbg(name, shape, dt=F32):
        ap = nc.dram_tensor("dbg_" + name, list(shape), dt, kind="ExternalOutput").ap()
        dbg_outs[name] = ap
        return ap

    SBp = lambda n, s, d=F32: es.enter_context(nc.sbuf_tensor(n, list(s), d))
    sb_ident = SBp("c_ident", [128, 128])
    sb_ident16 = SBp("c_ident16", [128, 128], BF16)
    sb_rmask = SBp("c_rmask", [128, 128])
    sb_ones_c = SBp("c_ones_c", [128, 1])
    sb_ones_r = SBp("c_ones_r", [1, 128])
    sb_sel = SBp("c_sel", [64, 2])
    sb_scrv = SBp("c_scrv", [128, 16])
    sb_scrs = SBp("c_scrs", [128, 16])
    sb_lng = SBp("c_lng", [128, 3, 8])
    sb_lnb = SBp("c_lnb", [128, 3, 8])

    ld = k.sem("ld")
    gld = k.sem("gld")
    st = k.sem("st")
    tS = k.sem("tS")
    vS = k.sem("vS")
    sS = k.sem("sS")
    pS = k.sem("pS")
    bS = k.sem("bS")
    aS = k.sem("aS")

    A("sync", lambda e: e.dma_start(out=sb_ident[:], in_=ident_i[:]), inc=ld, inc_n=16)
    A("sync", lambda e: e.dma_start(out=sb_rmask[:], in_=rmask[:]), inc=ld, inc_n=16)
    A("sync", lambda e: e.dma_start(out=sb_sel[:], in_=selc[:]), inc=ld, inc_n=16)
    A("sync", lambda e: e.dma_start(out=sb_lng[:], in_=ln_g[:]), inc=ld, inc_n=16)
    A("sync", lambda e: e.dma_start(out=sb_lnb[:], in_=ln_b[:]), inc=ld, inc_n=16)
    A("vector", lambda e: e.memset(sb_ones_c[:], 1.0), inc=vS)
    A("vector", lambda e: e.memset(sb_ones_r[:], 1.0), inc=vS)
    A("vector", lambda e: e.tensor_copy(sb_ident16[:], sb_ident[:]),
      waits=[(ld, 16)], inc=vS)

    sAC = ExitStack()
    xnT = sAC.enter_context(nc.sbuf_tensor("xnT", [128, 8, S], BF16))
    xnqT16 = sAC.enter_context(nc.sbuf_tensor("xnqT16", [128, 8, TOK], BF16))

    sA = ExitStack()
    SA = lambda n, s, d=F32: sA.enter_context(nc.sbuf_tensor(n, list(s), d))
    PA = lambda n, s, d=F32: sA.enter_context(nc.psum_tensor(n, list(s), d))
    a_xt = [SA(f"a_xt{i}", [128, D]) for i in range(2)]
    a_xn = [SA(f"a_xn{i}", [128, D]) for i in range(2)]
    a_sq = SA("a_sqs", [128, D])
    a_stat = SA("a_stat", [128, 8 * 32])
    ps_tr = [PA(f"a_ptr{i}", [128, 128]) for i in range(2)]

    def fv():
        A("vector", lambda e: e.tensor_copy(sb_scrv[:], sb_ident[:, 0:16]), inc=vS)

    def fs():
        A("scalar", lambda e: e.copy(sb_scrs[:], sb_ident[:, 0:16]), inc=sS)

    def emit_ln_pass(src_fn, nt, stat, xtb, xnb, sqb, trb, dst16, dst32):
        n = nt
        for j in range(nt):
            bi = j % 2
            A("sync", lambda e, j=j, bi=bi: e.dma_start(out=xtb[bi][:], in_=src_fn(j)),
              waits=W(vS, sS), inc=ld, inc_n=16)
            A("vector", lambda e, j=j, bi=bi: e.reduce_sum(
                stat[:, j:j + 1], xtb[bi][:], axis=AX.X), waits=W(ld), inc=vS)
            A("scalar", lambda e, j=j, bi=bi: e.activation(
                sqb[:], xtb[bi][:], AF.Square, accum_out=stat[:, n + j:n + j + 1]),
              waits=W(ld), inc=sS)
        fs()
        fv()
        A("vector", lambda e: e.tensor_scalar_mul(stat[:, 2 * n:3 * n], stat[:, 0:n],
                                                  1.0 / D), waits=W(sS), inc=vS)
        A("vector", lambda e: e.tensor_scalar(stat[:, 7 * n:8 * n], stat[:, n:2 * n],
                                              1.0 / D, EPS, ALU.mult, ALU.add), inc=vS)
        A("vector", lambda e: e.tensor_tensor(stat[:, 6 * n:7 * n], stat[:, 2 * n:3 * n],
                                              stat[:, 2 * n:3 * n], ALU.mult), inc=vS)
        fv()
        A("vector", lambda e: e.tensor_sub(stat[:, 3 * n:4 * n], stat[:, 7 * n:8 * n],
                                           stat[:, 6 * n:7 * n]), inc=vS)
        fv()
        A("scalar", lambda e: e.activation(stat[:, 4 * n:5 * n], stat[:, 3 * n:4 * n],
                                           AF.Sqrt), waits=W(vS), inc=sS)
        fs()
        A("vector", lambda e: e.reciprocal(stat[:, 4 * n:5 * n], stat[:, 4 * n:5 * n]),
          waits=W(sS), inc=vS)
        fv()
        A("vector", lambda e: e.scalar_tensor_tensor(
            stat[:, 5 * n:6 * n], stat[:, 2 * n:3 * n], -1.0, stat[:, 4 * n:5 * n],
            ALU.mult, ALU.mult), inc=vS)
        fv()
        fv()
        fmark = vS.n
        for j in range(nt):
            bi = j % 2
            A("sync", lambda e, j=j, bi=bi: e.dma_start(out=xtb[bi][:], in_=src_fn(j)),
              waits=W(vS, sS), inc=ld, inc_n=16)
            A("scalar", lambda e, j=j, bi=bi: e.activation(
                xnb[bi][:], xtb[bi][:], AF.Identity, bias=stat[:, 5 * n + j:5 * n + j + 1],
                scale=stat[:, 4 * n + j:4 * n + j + 1]),
              waits=[(vS, fmark), (ld, ld.n), (tS, tS.n)], inc=sS)
            for cch in range(8):
                pb = cch % 2
                A("tensor", lambda e, cch=cch, pb=pb, bi=bi: e.transpose(
                    trb[pb], xnb[bi][:, 128 * cch:128 * (cch + 1)], sb_ident[:]),
                  waits=W(sS, vS), inc=tS)
                first = True
                for dd in (dst16, dst32):
                    if dd is None:
                        continue
                    A("vector", lambda e, cch=cch, pb=pb, j=j, dd=dd: e.tensor_copy(
                        dd[:, cch, 128 * j:128 * (j + 1)], trb[pb]),
                      waits=(W(tS) if first else []), inc=vS)
                    first = False

    emit_ln_pass(lambda j: xb[128 * j:128 * (j + 1), :], 32, a_stat, a_xt, a_xn,
                 a_sq, [p[:] for p in ps_tr], xnT, None)
    emit_ln_pass(lambda j: xq[128 * j:128 * (j + 1), :], 16, a_stat, a_xt, a_xn,
                 a_sq, [p[:] for p in ps_tr], xnqT16, None)

    if debug_phase == "A":
        d1 = dbg("xnT", [128, 8, S], BF16)
        d2 = dbg("xnqT", [128, 8, TOK], BF16)
        k.wait_at("sync", vS, vS.n)
        A("sync", lambda e: e.dma_start(out=d1[:], in_=xnT[:]), inc=st, inc_n=16)
        A("sync", lambda e: e.dma_start(out=d2[:], in_=xnqT16[:]), inc=st, inc_n=16)
        k.wait_at("sync", st, st.n)
        k.emit()
        return nc, dbg_outs

    k.barrier(drain=[ld])
    sA.close()
    ln_pass = emit_ln_pass

    sB = ExitStack()
    SB_ = lambda n, s, d=F32: sB.enter_context(nc.sbuf_tensor(n, list(s), d))
    PB = lambda n, s, d=F32: sB.enter_context(nc.psum_tensor(n, list(s), d))

    wqh = SB_("b_wqh", [128, 8, DH], BF16)
    wkh = SB_("b_wkh", [128, 8, DH], BF16)
    wvh = SB_("b_wvh", [128, 8, DH], BF16)
    wrh = SB_("b_wrh", [128, 8, DH], BF16)
    ubv = SB_("b_ubv", [DH, 2 * NH])
    qu = SB_("b_qu", [DH, TOK], BF16)
    qv = SB_("b_qv", [DH, TOK], BF16)
    kvT = SB_("b_kvT", [128, S], BF16)
    vtok = SB_("b_vtok", [128, S // 128, 72], BF16)
    rcT = SB_("b_rcT", [DH, S + 256], BF16)
    pech = [SB_(f"b_pech{i}", [128, 8, 256], BF16) for i in range(2)]
    pbuf = [SB_(f"b_pbuf{i}", [128, S + 256], BF16) for i in range(2)]
    bdb = [SB_(f"b_bdb{i}", [128, S + 256], BF16) for i in range(2)]
    asbT = SB_("b_asbT", [128, 8, 256], BF16)
    sb_negt = SB_("b_negt", [128, 128], BF16)
    rrall = SB_("b_rrall", [1, TOK])
    bcball = SB_("b_bcball", [DH, TOK])
    ctxu65 = SB_("b_ctxu65", [DH + 1, TOK])
    ctxh = SB_("b_ctxh", [DH, TOK], BF16)
    sb_spq = SB_("b_spq", [128, 128], BF16)
    sb_pmq = SB_("b_pmq", [128, 128], BF16)

    bigS = [PB(f"b_bigs{i}", [128, 1024]) for i in range(2)]
    bps = [PB("b_ps0", [128, 512])]
    msc = [bigS[1][:, 0:512], bigS[1][:, 512:1024]]
    ps_t = [PB(f"b_pst{i}", [128, 128], BF16) for i in range(2)]
    ps_ctx = PB("b_psctx", [DH + 1, 256])

    A("sync", lambda e: e.dma_start(out=ubv[:, 0:NH], in_=ub_t[:]), inc=ld, inc_n=16)
    A("sync", lambda e: e.dma_start(out=ubv[:, NH:2 * NH], in_=vb_t[:]), inc=ld, inc_n=16)
    A("sync", lambda e: e.dma_start(out=sb_spq[:], in_=spillq[:]), inc=ld, inc_n=16)
    A("sync", lambda e: e.dma_start(out=sb_pmq[:], in_=pmq[:]), inc=ld, inc_n=16)
    A("sync", lambda e: e.dma_start(out=sb_negt[:], in_=negt[:]), inc=ld, inc_n=16)
    for cch in range(S // 128):
        A("vector", lambda e, cch=cch: e.memset(vtok[:, cch, 64:65], 1.0), inc=vS)

    wqkv_r = wqkv.rearrange("(c p) e -> p c e", p=128)
    wr_r = wr.rearrange("(c p) e -> p c e", p=128)

    M = {}
    BM = {}

    def bw(key):
        m = BM.get(key)
        return [m] if m else []

    def mw(key):
        m = M.get(key)
        return [m] if m else []

    nh_b = 0 if debug_phase in ("C0", "C0a", "C0b") else (1 if debug_phase == "B0" else NH)
    for h in range(nh_b):
        A("gpsimd", lambda e, h=h: e.dma_start(
            out=wqh[:], in_=wqkv_r[:, :, DH * h:DH * (h + 1)]),
          waits=mw("proj_done"), inc=gld, inc_n=16)
        A("gpsimd", lambda e, h=h: e.dma_start(
            out=wkh[:], in_=wqkv_r[:, :, D + DH * h:D + DH * (h + 1)]), inc=gld, inc_n=16)
        A("gpsimd", lambda e, h=h: e.dma_start(
            out=wvh[:], in_=wqkv_r[:, :, 2 * D + DH * h:2 * D + DH * (h + 1)]),
          inc=gld, inc_n=16)
        A("gpsimd", lambda e, h=h: e.dma_start(
            out=wrh[:], in_=wr_r[:, :, DH * h:DH * (h + 1)]), inc=gld, inc_n=16)
        w_loaded = (gld, gld.n)

        def bproj(w_sb, src, ntok, evac, free_mark):
            for j in range(ntok // 512):
                pj = j % 2
                for cch in range(8):
                    A("tensor", lambda e, j=j, cch=cch, pj=pj, w_sb=w_sb, src=src:
                      e.matmul(msc[pj][0:64, :], w_sb[:, cch, :],
                               src[:, cch, 512 * j:512 * (j + 1)],
                               start=(cch == 0), stop=(cch == 7)),
                      waits=(bw(("b", pj)) + [w_loaded] + (mw(free_mark) if j < 2 else [])
                             if cch == 0 else []), inc=tS)
                mmk = (tS, tS.n)
                evac(j, msc[pj][0:64, :], mmk)

        def tpos(t):
            return 2 * (15 - t) if t >= 8 else 2 * t + 1

        def evac_q(j, ps, mmk, h=h):
            for c4 in range(4):
                t4 = 4 * j + c4
                A("scalar", lambda e, c4=c4, t4=t4, ps=ps: e.activation(
                    qu[:, 128 * tpos(t4):128 * tpos(t4) + 128],
                    ps[0:64, 128 * c4:128 * (c4 + 1)], AF.Identity,
                    bias=ubv[:, h:h + 1], scale=1.0),
                  waits=([mmk, (ld, 32)] + (mw("qu_free") if j == 0 else [])
                         if c4 == 0 else []), inc=sS)
            A("scalar", lambda e, j=j, ps=ps: e.activation(
                qv[:, 512 * j:512 * (j + 1)], ps, AF.Identity,
                bias=ubv[:, NH + h:NH + h + 1], scale=1.0), inc=sS)
            BM[("b", j % 2)] = (sS, sS.n)
        bproj(wqh, xnqT16, TOK, evac_q, "qu_free")

        def evac_k(j, ps, mmk):
            if j % 2 == 0:
                A("vector", lambda e, j=j, ps=ps: e.tensor_copy(
                    kvT[0:64, 512 * j:512 * (j + 1)], ps),
                  waits=[mmk] + (mw("kvT_free") if j == 0 else []), inc=vS)
                BM[("b", j % 2)] = (vS, vS.n)
            else:
                A("scalar", lambda e, j=j, ps=ps: e.activation(
                    kvT[0:64, 512 * j:512 * (j + 1)], ps, AF.Identity),
                  waits=[mmk], inc=sS)
                BM[("b", j % 2)] = (sS, sS.n)
        bproj(wkh, xnT, S, evac_k, "kvT_free")

        def evac_v(j, ps, mmk):
            if j % 2 == 0:
                A("vector", lambda e, j=j, ps=ps: e.tensor_copy(
                    kvT[64:128, 512 * j:512 * (j + 1)], ps), waits=[mmk], inc=vS)
                BM[("b", j % 2)] = (vS, vS.n)
            else:
                A("scalar", lambda e, j=j, ps=ps: e.activation(
                    kvT[64:128, 512 * j:512 * (j + 1)], ps, AF.Identity),
                  waits=[mmk], inc=sS)
                BM[("b", j % 2)] = (sS, sS.n)
        bproj(wvh, xnT, S, evac_v, None)
        kv_ready = (vS, vS.n)

        for cch in range(S // 128):
            pb = cch % 2
            A("tensor", lambda e, cch=cch, pb=pb: e.transpose(
                ps_t[pb][:], kvT[:, 128 * cch:128 * (cch + 1)], sb_ident16[:]),
              waits=[kv_ready] + bw(("t", pb)) + (mw("vtok_free") if cch == 0 else []),
              inc=tS)
            A("vector", lambda e, cch=cch, pb=pb: e.tensor_copy(
                vtok[:, cch, 0:64], ps_t[pb][:, 64:128]),
              waits=[(tS, tS.n)], inc=vS)
            BM[("t", pb)] = (vS, vS.n)

        for wch in range(cdiv(S + 256, 256)):
            cw = min(256, S + 256 - 256 * wch)
            pe_b = wch % 2
            A("sync", lambda e, wch=wch, cw=cw, pe_b=pe_b: e.dma_start(
                out=pech[pe_b][:, :, 0:cw], in_=peT[:, :, 256 * wch:256 * wch + cw]),
              waits=bw(("pe", pe_b)), inc=ld, inc_n=16)
            pk_ld = (ld, ld.n)
            pj = wch % 2
            for cch in range(8):
                A("tensor", lambda e, wch=wch, cw=cw, cch=cch, pj=pj, pe_b=pe_b: e.matmul(
                    msc[pj][0:64, 0:cw], wrh[:, cch, :], pech[pe_b][:, cch, 0:cw],
                    start=(cch == 0), stop=(cch == 7)),
                  waits=([pk_ld] + bw(("b", pj)) + (mw("rcT_free") if wch == 0 else [])
                         if cch == 0 else []), inc=tS)
            BM[("pe", pe_b)] = (tS, tS.n)
            if wch % 2 == 0:
                A("vector", lambda e, wch=wch, cw=cw, pj=pj: e.tensor_copy(
                    rcT[:, 256 * wch:256 * wch + cw], msc[pj][0:64, 0:cw]),
                  waits=[(tS, tS.n)], inc=vS)
                BM[("b", pj)] = (vS, vS.n)
            else:
                A("scalar", lambda e, wch=wch, cw=cw, pj=pj: e.activation(
                    rcT[:, 256 * wch:256 * wch + cw], msc[pj][0:64, 0:cw],
                    AF.Identity),
                  waits=[(tS, tS.n)], inc=sS)
                BM[("b", pj)] = (sS, sS.n)
        rc_ready = (vS, vS.n)
        M["proj_done"] = (tS, tS.n)

        def p_stage(t, pp, off, pkey):
            E = 256 * (t + 1)
            W0 = S - 128 - 256 * t
            nw = cdiv(E, 512)
            for w in range(nw):
                cw = min(512, E - 512 * w)
                pj = w % 2
                A("tensor", lambda e, w=w, cw=cw, t=t, W0=W0, pj=pj: e.matmul(
                    msc[pj][:, 0:cw], qv[:, 128 * t:128 * (t + 1)],
                    rcT[:, W0 + 512 * w:W0 + 512 * w + cw], start=True, stop=True),
                  waits=[rc_ready] + bw(("b", pj)), inc=tS)
                mmk = (tS, tS.n)
                if w % 2 == 0:
                    A("vector", lambda e, w=w, cw=cw, pp=pp, off=off, pj=pj:
                      e.tensor_copy(
                        pbuf[pp][:, off + 512 * w:off + 512 * w + cw],
                        msc[pj][:, 0:cw]),
                      waits=[mmk] + (mw(("pbuf_free", pp)) if w == 0 else []), inc=vS)
                    BM[("b", pj)] = (vS, vS.n)
                else:
                    A("scalar", lambda e, w=w, cw=cw, pp=pp, off=off, pj=pj:
                      e.activation(
                        pbuf[pp][:, off + 512 * w:off + 512 * w + cw],
                        msc[pj][:, 0:cw], AF.Identity),
                      waits=[mmk], inc=sS)
                    BM[("b", pj)] = (sS, sS.n)
            A("vector", lambda e, E=E, pp=pp, off=off: e.tensor_tensor(
                pbuf[pp][:, off + E - 128:off + E],
                pbuf[pp][:, off + E - 128:off + E], sb_pmq[:],
                ALU.add), waits=W(ld), inc=vS)
            pev = (vS, vS.n)
            A("sync", lambda e, E=E, pp=pp, off=off: e.dma_start(
                out=AP(pskew[pp].tensor, 128 * off, [[E, 128], [1, E]]),
                in_=pbuf[pp][:, off:off + E]),
              waits=[pev] + mw(("skew_done", pp)), inc=pS, inc_n=16)
            pok = (pS, pS.n)
            A("gpsimd", lambda e, E=E, pp=pp, off=off: e.dma_start(
                out=bdb[pp][:, off:off + E],
                in_=AP(pskew[pp].tensor, 128 * off + 127, [[E - 1, 128], [1, E]])),
              waits=[pok] + mw(("bd_used", pp)), inc=bS, inc_n=16)
            A("vector", lambda e, E=E, pp=pp, off=off: e.tensor_tensor(
                bdb[pp][:, off + E - 128:off + E],
                bdb[pp][:, off + E - 128:off + E], sb_spq[:],
                ALU.add), waits=[(bS, bS.n)], inc=vS)
            M[("bd_ready", pkey)] = (vS, vS.n)
            M[("skew_done", pp)] = (bS, bS.n)
            if off == 0:
                M[("pbuf_free", pp)] = pok

        def s_pair(u, pos):
            pp = pos % 2
            tl, ts = 15 - u, u
            El, Es = 256 * (tl + 1), 256 * (ts + 1)
            ncc, ncs = El // 128, Es // 128
            psS = (bigS[0][:, 0:512], bigS[0][:, 512:1024], bps[0][:, 0:512])
            ng = cdiv(ncc, 2)

            def avs(g):
                ncg = min(2, ncc - 2 * g)
                for c2 in range(ncg):
                    kc = 2 * g + c2
                    A("tensor", lambda e, kc=kc, ncc=ncc: e.matmul(
                        ps_ctx[:], vtok[:, kc, 0:65], asbT[:, kc % 8, :],
                        start=(kc == 0), stop=(kc == ncc - 1)),
                      waits=([M[("exp", u, g)]] if c2 == 0 else [])
                            + (mw("ctx_free") if kc == 0 else []), inc=tS)
                    M[("asbs", kc % 8)] = (tS, tS.n)

            prevg = None
            for g in range(ng):
                ncg = min(2, ncc - 2 * g)
                bank = psS[g % 3]
                for c2 in range(ncg):
                    kc = 2 * g + c2
                    A("tensor", lambda e, kc=kc, c2=c2, bank=bank, u=u: e.matmul(
                        bank[:, 256 * c2:256 * c2 + 256],
                        kvT[0:64, 128 * kc:128 * (kc + 1)],
                        qu[:, 256 * u:256 * u + 256],
                        start=True, stop=False),
                      waits=((bw(("G", g % 3))
                              + ([M[("bd_ready", (u, 1))]] if g == 0 else []))
                             if c2 == 0 else []), inc=tS)
                    A("tensor", lambda e, kc=kc, c2=c2, bank=bank, pp=pp: e.matmul(
                        bank[:, 256 * c2:256 * c2 + 128],
                        bdb[pp][:, 128 * kc:128 * (kc + 1)], sb_ident16[:],
                        start=False, stop=False), inc=tS)
                    if kc < ncs:
                        A("tensor", lambda e, kc=kc, c2=c2, bank=bank, pp=pp, El=El:
                          e.matmul(
                            bank[:, 256 * c2 + 128:256 * c2 + 256],
                            bdb[pp][:, El + 128 * kc:El + 128 * (kc + 1)],
                            sb_ident16[:], start=False, stop=True), inc=tS)
                    else:
                        A("tensor", lambda e, kc=kc, c2=c2, bank=bank: e.matmul(
                            bank[:, 256 * c2 + 128:256 * c2 + 256],
                            sb_negt[:], sb_ident16[:],
                            start=False, stop=True), inc=tS)
                ek = (tS, tS.n)
                sl = (2 * g) % 8
                A("scalar", lambda e, g=g, ncg=ncg, sl=sl, bank=bank: e.activation(
                    asbT[:, sl:sl + ncg, :], bank[:, 0:256 * ncg], AF.Exp,
                    scale=0.125),
                  waits=[ek] + mw(("asbs", sl)) + mw(("asbs", sl + 1)), inc=sS)
                M[("exp", u, g)] = (sS, sS.n)
                BM[("G", g % 3)] = (sS, sS.n)
                if prevg is not None:
                    avs(prevg)
                prevg = g
            avs(prevg)
            avk = (tS, tS.n)
            M[("bd_used", pp)] = avk
            A("vector", lambda e, tl=tl: e.tensor_copy(
                ctxu65[:, 128 * tl:128 * (tl + 1)], ps_ctx[:, 0:128]),
              waits=[avk], inc=vS)
            A("vector", lambda e, ts=ts: e.tensor_copy(
                ctxu65[:, 128 * ts:128 * (ts + 1)], ps_ctx[:, 128:256]), inc=vS)
            M["ctx_free"] = (vS, vS.n)

        p_stage(15, 0, 0, (0, 0))
        p_stage(0, 0, 256 * 16, (0, 1))
        for u in range(1, 8):
            p_stage(15 - u, u % 2, 0, (u, 0))
            p_stage(u, u % 2, 256 * (16 - u), (u, 1))
            s_pair(u - 1, u - 1)
        s_pair(7, 7)

        fv()
        A("vector", lambda e: e.tensor_copy(rrall[:], ctxu65[64:65, :]), inc=vS)
        fv()
        fv()
        A("vector", lambda e: e.reciprocal(rrall[:], rrall[:]), inc=vS)
        fv()
        fv()
        rr_ready = (vS, vS.n)
        for w in range(4):
            pj = w % 2
            A("tensor", lambda e, w=w, pj=pj: e.matmul(
                msc[pj][0:64, 0:512], sb_ones_r[0:1, 0:64],
                rrall[0:1, 512 * w:512 * (w + 1)], start=True, stop=True),
              waits=[rr_ready] + bw(("b", pj)), inc=tS)
            A("vector", lambda e, w=w, pj=pj: e.tensor_copy(
                bcball[:, 512 * w:512 * (w + 1)], msc[pj][0:64, 0:512]),
              waits=[(tS, tS.n)], inc=vS)
            BM[("b", pj)] = (vS, vS.n)
        A("vector", lambda e: e.tensor_tensor(ctxh[:], ctxu65[0:64, :], bcball[:],
                                              ALU.mult), inc=vS)
        A("sync", lambda e, h=h: e.dma_start(
            out=ctx_park[64 * (h % 2):64 * (h % 2) + 64, h // 2, :], in_=ctxh[:]),
          waits=[(vS, vS.n)], inc=ld, inc_n=16)
        M["ctx_free"] = (ld, ld.n)
        M["qu_free"] = (tS, tS.n)
        M["kvT_free"] = (tS, tS.n)
        M["vtok_free"] = (tS, tS.n)
        M["rcT_free"] = (tS, tS.n)

    if debug_phase == "B0":
        d1 = dbg("ctxu", [DH + 1, TOK])
        d2 = dbg("rs", [1, TOK])
        k.wait_at("sync", vS, vS.n)
        k.wait_at("sync", bS, bS.n)
        A("sync", lambda e: e.dma_start(out=d1[:], in_=ctxu65[:]), inc=st, inc_n=16)
        A("sync", lambda e: e.dma_start(out=d2[:], in_=rrall[:]), inc=st, inc_n=16)
        k.wait_at("sync", st, st.n)
        k.emit()
        return nc, dbg_outs

    if debug_phase == "B":
        d1 = dbg("ctx", [128, 8, TOK], BF16)
        k.wait_at("sync", ld, ld.n)
        A("sync", lambda e: e.dma_start(out=d1[:], in_=ctx_park[:]), inc=st, inc_n=16)
        k.wait_at("sync", st, st.n)
        k.emit()
        return nc, dbg_outs


    k.barrier(drain=[ld, pS, bS, gld])
    sB.close()

    sC = ExitStack()
    SC = lambda n, s, d=F32: sC.enter_context(nc.sbuf_tensor(n, list(s), d))
    PC = lambda n, s, d=F32: sC.enter_context(nc.psum_tensor(n, list(s), d))

    rwq = SC("c_rwq", [128, 8, DH], BF16)
    rwk = SC("c_rwk", [128, 8, DH], BF16)
    rwv = SC("c_rwv", [128, 8, DH], BF16)
    gqh = SC("c_gqh", [64, 512])
    gkh = SC("c_gkh", [64, 512])
    qgT = SC("c_qgT", [DH, S], BF16)
    rkvT = SC("c_rkvT", [128, S], BF16)
    kvtok = SC("c_kvtok", [128, S // 128, 128], BF16)
    stT = SC("c_stT", [64, 64])
    stT16 = SC("c_stT16", [64, 64], BF16)
    sc16 = [SC(f"c_sc16{i}", [128, 128], BF16) for i in range(2)]
    atr = [SC(f"c_atr{i}", [128, 128], BF16) for i in range(2)]
    ob = [SC(f"c_ob{i}", [64, TOK]) for i in range(2)]
    stats = SC("c_stats", [64, 64])
    sqsc = SC("c_sqsc", [64, 128])
    mrow = SC("c_mrow", [1, 8])
    rncol = SC("c_rncol", [64, 2])
    scbi = SC("c_scbi", [64, 2])
    gw = SC("c_gw", [64, 2 * NH])
    cnh = SC("c_cnh", [64, TOK], BF16)

    cps = [PC(f"c_ps{i}", [128, 512]) for i in range(2)]
    ctk = [PC(f"c_ptk{i}", [128, 128], BF16) for i in range(2)]
    cpo = [PC(f"c_po{i}", [64, 128]) for i in range(2)]
    cpu = [PC(f"c_pu{i}", [64, 128]) for i in range(2)]
    ps_u = cpu[0]

    A("sync", lambda e: e.dma_start(out=gw[:, 0:NH], in_=gnw_t[:]), inc=ld, inc_n=16)
    A("sync", lambda e: e.dma_start(out=gw[:, NH:2 * NH], in_=gnb_t[:]), inc=ld, inc_n=16)

    CM = {}

    def cmw(key):
        m = CM.get(key)
        return [m] if m else []

    def projc(w_sb, src, ntok, evac, wk_ld):
        for j in range(ntok // 512):
            pj = j % 2
            for cch in range(8):
                A("tensor", lambda e, j=j, cch=cch, pj=pj: e.matmul(
                    cps[pj][0:64, :], w_sb[:, cch, :], src[:, cch, 512 * j:512 * (j + 1)],
                    start=(cch == 0), stop=(cch == 7)),
                  waits=([wk_ld] + cmw(("cb", pj)) if cch == 0 else []), inc=tS)
            evac(j, cps[pj][0:64, :], (tS, tS.n))
            CM[("cb", pj)] = (vS, vS.n)

    c_dbg = debug_phase in ("C0", "C0a", "C0b")
    for h in range(1 if c_dbg else NH):
        g = GAMMAS[h]
        gC = g ** 128
        A("gpsimd", lambda e, h=h: e.dma_start(
            out=rwq[:], in_=wq_a[h].rearrange("(c p) e -> p c e", p=128)),
          waits=cmw("w_free"), inc=gld, inc_n=16)
        A("gpsimd", lambda e, h=h: e.dma_start(
            out=rwk[:], in_=wk_a[h].rearrange("(c p) e -> p c e", p=128)), inc=gld, inc_n=16)
        A("gpsimd", lambda e, h=h: e.dma_start(
            out=rwv[:], in_=wv_a[h].rearrange("(c p) e -> p c e", p=128)), inc=gld, inc_n=16)
        cw_ld = (gld, gld.n)
        A("sync", lambda e, h=h: e.dma_start(out=gqh[:], in_=gq_t[:, h, :]),
          waits=cmw("g_free"), inc=ld, inc_n=16)
        A("sync", lambda e, h=h: e.dma_start(out=gkh[:], in_=gk_t[:, h, :]), inc=ld, inc_n=16)
        g_ld = (ld, ld.n)

        def evac_qg(j, ps, mmk):
            A("vector", lambda e, j=j, ps=ps: e.tensor_tensor(
                qgT[:, 512 * j:512 * (j + 1)], ps, gqh[:], ALU.mult),
              waits=[mmk, g_ld] + (cmw("qg_free") if j == 0 else []), inc=vS)
        projc(rwq, xnT, S, evac_qg, cw_ld)

        def evac_kg(j, ps, mmk):
            A("vector", lambda e, j=j, ps=ps: e.tensor_tensor(
                rkvT[0:64, 512 * j:512 * (j + 1)], ps, gkh[:], ALU.mult),
              waits=[mmk] + (cmw("kv_free") if j == 0 else []), inc=vS)
        projc(rwk, xnT, S, evac_kg, cw_ld)

        def evac_rv(j, ps, mmk):
            A("vector", lambda e, j=j, ps=ps: e.tensor_copy(
                rkvT[64:128, 512 * j:512 * (j + 1)], ps), waits=[mmk], inc=vS)
        projc(rwv, xnT, S, evac_rv, cw_ld)
        qgkv_ready = (vS, vS.n)
        CM["w_free"] = (tS, tS.n)

        A("vector", lambda e: e.memset(stT[:], 0.0), waits=cmw("st_free"), inc=vS)
        A("vector", lambda e: e.memset(stT16[:], 0.0), inc=vS)
        st_ready = (vS, vS.n)

        for j in range(0 if debug_phase == "C0a" else S // 128):
            jb = j % 2
            A("tensor", lambda e, j=j, jb=jb: e.transpose(
                ctk[jb][:], rkvT[:, 128 * j:128 * (j + 1)], sb_ident16[:]),
              waits=[qgkv_ready] + cmw(("tk", jb)), inc=tS)
            A("vector", lambda e, j=j, jb=jb: e.tensor_copy(kvtok[:, j, :], ctk[jb][:]),
              waits=[(tS, tS.n)], inc=vS)
            A("vector", lambda e, j=j, gC=gC: e.tensor_scalar_mul(
                kvtok[:, j, 0:64], kvtok[:, j, 0:64], gC), inc=vS)
            kvtok_k = (vS, vS.n)
            A("tensor", lambda e, j=j, jb=jb: e.matmul(
                cps[jb][0:128, 0:128], qgT[:, 128 * j:128 * (j + 1)],
                rkvT[0:64, 128 * j:128 * (j + 1)],
                start=True, stop=True), waits=[qgkv_ready] + cmw(("cb", jb)), inc=tS)
            A("vector", lambda e, jb=jb: e.tensor_tensor(
                sc16[jb][:], cps[jb][0:128, 0:128], sb_rmask[:], ALU.mult),
              waits=[(tS, tS.n)] + cmw(("sc16u", jb)), inc=vS)
            CM[("cb", jb)] = (vS, vS.n)
            A("tensor", lambda e, jb=jb: e.transpose(ctk[jb][:], sc16[jb][:],
                                                     sb_ident16[:]),
              waits=[(vS, vS.n)], inc=tS)
            CM[("sc16u", jb)] = (tS, tS.n)
            A("vector", lambda e, jb=jb: e.tensor_copy(atr[jb][:], ctk[jb][:]),
              waits=[(tS, tS.n)] + cmw(("atru", jb)), inc=vS)
            CM[("tk", jb)] = (vS, vS.n)
            atr_k = (vS, vS.n)
            A("tensor", lambda e, j=j, jb=jb: e.matmul(
                cpo[jb][:, 0:128], kvtok[:, j, 64:128], atr[jb][:],
                start=True, stop=False),
              waits=[atr_k, kvtok_k if j == 0 else st_ready] + cmw(("ob", jb)), inc=tS)
            A("tensor", lambda e, j=j, jb=jb: e.matmul(
                cpo[jb][:, 0:128], stT16[:], qgT[:, 128 * j:128 * (j + 1)],
                start=False, stop=True), inc=tS)
            CM[("atru", jb)] = (tS, tS.n)
            ok_ = (tS, tS.n)
            A("vector", lambda e, j=j, jb=jb: e.tensor_copy(
                ob[jb][:, 128 * (j // 2):128 * (j // 2 + 1)], cpo[jb][:, 0:128]),
              waits=[ok_], inc=vS)
            CM[("ob", jb)] = (vS, vS.n)
            A("vector", lambda e, j=j, jb=jb: e.reduce_sum(
                stats[:, j:j + 1], ob[jb][:, 128 * (j // 2):128 * (j // 2 + 1)],
                axis=AX.X), inc=vS)
            A("scalar", lambda e, j=j, jb=jb: e.activation(
                sqsc[:], ob[jb][:, 128 * (j // 2):128 * (j // 2 + 1)], AF.Square,
                accum_out=stats[:, 32 + j:33 + j]), waits=[(vS, vS.n)], inc=sS)
            A("tensor", lambda e, j=j, jb=jb: e.matmul(
                cpu[jb][:, 0:64], kvtok[:, j, 0:64], kvtok[:, j, 64:128],
                start=True, stop=True),
              waits=[kvtok_k] + cmw(("ub", jb)), inc=tS)
            upd_k = (tS, tS.n)
            A("vector", lambda e, gC=gC, jb=jb: e.scalar_tensor_tensor(
                stT16[:], stT[:], gC, cpu[jb][:, 0:64], ALU.mult, ALU.add),
              waits=[upd_k, st_ready], inc=vS)
            A("vector", lambda e, gC=gC, jb=jb: e.scalar_tensor_tensor(
                stT[:], stT[:], gC, cpu[jb][:, 0:64], ALU.mult, ALU.add), inc=vS)
            CM[("ub", jb)] = (vS, vS.n)
            st_ready = (vS, vS.n)

        if debug_phase in ("C0a", "C0b"):
            continue
        fs()
        A("vector", lambda e: e.reduce_sum(rncol[:, 0:1], stats[:, 0:32], axis=AX.X),
          waits=W(sS), inc=vS)
        A("vector", lambda e: e.reduce_sum(rncol[:, 1:2], stats[:, 32:64], axis=AX.X),
          inc=vS)
        fv()
        A("tensor", lambda e: e.matmul(ps_u[0:1, 64:66], sb_ones_c[0:64, 0:1],
                                       rncol[:, 0:2], start=True, stop=True),
          waits=W(vS), inc=tS)
        A("vector", lambda e: e.tensor_copy(mrow[0:1, 0:2], ps_u[0:1, 64:66]),
          waits=W(tS), inc=vS)
        fv()
        A("vector", lambda e: e.tensor_scalar_mul(mrow[0:1, 2:4], mrow[0:1, 0:2],
                                                  1.0 / (64.0 * S)), inc=vS)
        fv()
        A("vector", lambda e: e.tensor_tensor(mrow[0:1, 4:5], mrow[0:1, 2:3],
                                              mrow[0:1, 2:3], ALU.mult), inc=vS)
        fv()
        A("vector", lambda e: e.tensor_sub(mrow[0:1, 5:6], mrow[0:1, 3:4],
                                           mrow[0:1, 4:5]), inc=vS)
        fv()
        A("vector", lambda e: e.tensor_scalar_add(mrow[0:1, 5:6], mrow[0:1, 5:6], EPS),
          inc=vS)
        fv()
        A("scalar", lambda e: e.activation(mrow[0:1, 6:7], mrow[0:1, 5:6], AF.Sqrt),
          waits=W(vS), inc=sS)
        fs()
        A("vector", lambda e: e.reciprocal(mrow[0:1, 7:8], mrow[0:1, 6:7]),
          waits=W(sS), inc=vS)
        fv()
        A("vector", lambda e: e.tensor_scalar_mul(mrow[0:1, 2:3], mrow[0:1, 2:3], -1.0),
          inc=vS)
        A("vector", lambda e: e.tensor_copy(mrow[0:1, 3:4], mrow[0:1, 7:8]), inc=vS)
        fv()
        A("tensor", lambda e: e.matmul(ps_u[0:64, 66:68], sb_ones_r[0:1, 0:64],
                                       mrow[0:1, 2:4], start=True, stop=True),
          waits=W(vS), inc=tS)
        A("vector", lambda e: e.tensor_copy(rncol[:], ps_u[0:64, 66:68]),
          waits=W(tS), inc=vS)
        fv()
        A("vector", lambda e, h=h: e.tensor_tensor(scbi[:, 0:1], gw[:, h:h + 1],
                                                   rncol[:, 1:2], ALU.mult), inc=vS)
        fv()
        A("vector", lambda e, h=h: e.scalar_tensor_tensor(
            scbi[:, 1:2], scbi[:, 0:1], rncol[:, 0:1], gw[:, NH + h:NH + h + 1],
            ALU.mult, ALU.add), inc=vS)
        A("vector", lambda e: e.tensor_scalar(ob[0][:], ob[0][:], sb_sel[:, 0:1], None,
                                              ALU.mult), inc=vS)
        A("vector", lambda e: e.scalar_tensor_tensor(ob[0][:], ob[1][:], sb_sel[:, 1:2],
                                                     ob[0][:], ALU.mult, ALU.add), inc=vS)
        fv()
        A("scalar", lambda e: e.activation(cnh[:], ob[0][:], AF.Identity,
                                           bias=scbi[:, 1:2], scale=scbi[:, 0:1]),
          waits=W(vS, ld), inc=sS)
        A("sync", lambda e, h=h: e.dma_start(
            out=cn_park[64 * (h % 2):64 * (h % 2) + 64, h // 2, :], in_=cnh[:]),
          waits=W(sS), inc=ld, inc_n=16)
        CM["g_free"] = (vS, vS.n)
        CM["qg_free"] = (tS, tS.n)
        CM["kv_free"] = (tS, tS.n)
        CM["st_free"] = (vS, vS.n)

    if debug_phase in ("C0a", "C0b"):
        d1 = dbg("qg", [DH, S], BF16)
        d2 = dbg("rkv", [128, S], BF16)
        k.wait_at("sync", vS, vS.n)
        A("sync", lambda e: e.dma_start(out=d1[:], in_=qgT[:]), inc=st, inc_n=16)
        A("sync", lambda e: e.dma_start(out=d2[:], in_=rkvT[:]), inc=st, inc_n=16)
        if debug_phase == "C0b":
            d3 = dbg("ob0", [DH, TOK])
            d4 = dbg("ob1", [DH, TOK])
            d5 = dbg("stats", [DH, 64])
            A("sync", lambda e: e.dma_start(out=d3[:], in_=ob[0][:]), inc=st, inc_n=16)
            A("sync", lambda e: e.dma_start(out=d4[:], in_=ob[1][:]), inc=st, inc_n=16)
            A("sync", lambda e: e.dma_start(out=d5[:], in_=stats[:]), inc=st, inc_n=16)
        k.wait_at("sync", st, st.n)
        k.emit()
        return nc, dbg_outs

    if debug_phase in ("C", "C0"):
        dm = dbg("mrow", [1, 8])
        dsc = dbg("scbi", [DH, 2])
        k.wait_at("sync", vS, vS.n)
        A("sync", lambda e: e.dma_start(out=dm[:], in_=mrow[:]), inc=st, inc_n=16)
        A("sync", lambda e: e.dma_start(out=dsc[:], in_=scbi[:]), inc=st, inc_n=16)
        d1 = dbg("cn", [128, 8, TOK], BF16)
        k.wait_at("sync", ld, ld.n)
        A("sync", lambda e: e.dma_start(out=d1[:], in_=cn_park[:]), inc=st, inc_n=16)
        k.wait_at("sync", st, st.n)
        k.emit()
        return nc, dbg_outs

    k.barrier(drain=[ld, gld])
    sC.close()
    sAC.close()

    sEF = ExitStack()
    PSB = [sEF.enter_context(nc.psum_tensor(f"e_psb{i}", [128, 512], F32))
           for i in range(8)]

    def mk_lnbufs(stack, tag):
        SE_ = lambda n, s, d=F32: stack.enter_context(nc.sbuf_tensor(n, list(s), d))
        return dict(
            f32t=SE_(f"{tag}_f32t", [128, 8, 512]),
            sqch=[SE_(f"{tag}_sqch{i}", [128, 512]) for i in range(2)],
            lrow=SE_(f"{tag}_lrow", [1, 8, 512]),
            bcn=SE_(f"{tag}_bcn", [128, 512]),
            lntmp=SE_(f"{tag}_lntmp", [128, 512]),
        )

    def ln_feature(lb, src_fn, j, ln_idx, out16_fn, park32_ap, park16_ap, park16_src):
        f32t, sqch, lrow, bcn, lntmp = (lb["f32t"], lb["sqch"], lb["lrow"],
                                        lb["bcn"], lb["lntmp"])
        for cch in range(8):
            A("tensor", lambda e, cch=cch: e.matmul(
                PSB[0][0:1, :], sb_ones_c[:], src_fn(cch),
                start=(cch == 0), stop=(cch == 7)),
              waits=(W(vS, ld) if cch == 0 else []), inc=tS)
        for cch in range(8):
            pb = cch % 2
            A("vector", lambda e, cch=cch, pb=pb: e.tensor_tensor(
                sqch[pb][:], src_fn(cch), src_fn(cch), ALU.mult),
              waits=W(tS, ld), inc=vS)
            A("tensor", lambda e, cch=cch, pb=pb: e.matmul(
                PSB[1][0:1, :], sb_ones_c[:], sqch[pb][:],
                start=(cch == 0), stop=(cch == 7)), waits=W(vS), inc=tS)
        A("vector", lambda e: e.tensor_scalar_mul(lrow[0:1, 0, :], PSB[0][0:1, :], 1.0 / D),
          waits=W(tS), inc=vS)
        A("vector", lambda e: e.tensor_scalar(lrow[0:1, 1, :], PSB[1][0:1, :], 1.0 / D,
                                              EPS, ALU.mult, ALU.add), inc=vS)
        A("vector", lambda e: e.tensor_tensor(lrow[0:1, 2, :], lrow[0:1, 0, :],
                                              lrow[0:1, 0, :], ALU.mult), inc=vS)
        fv()
        fv()
        A("vector", lambda e: e.tensor_sub(lrow[0:1, 5, :], lrow[0:1, 1, :],
                                           lrow[0:1, 2, :]), inc=vS)
        fv()
        fv()
        A("scalar", lambda e: e.activation(lrow[0:1, 3, :], lrow[0:1, 5, :], AF.Sqrt),
          waits=W(vS), inc=sS)
        fs()
        fs()
        A("vector", lambda e: e.reciprocal(lrow[0:1, 3, :], lrow[0:1, 3, :]),
          waits=W(sS), inc=vS)
        fv()
        fv()
        A("vector", lambda e: e.scalar_tensor_tensor(
            lrow[0:1, 4, :], lrow[0:1, 0, :], -1.0, lrow[0:1, 3, :],
            ALU.mult, ALU.mult), inc=vS)
        fv()
        fv()
        A("tensor", lambda e: e.matmul(PSB[0][:], sb_ones_r[:], lrow[0:1, 3, :],
                                       start=True, stop=True), waits=W(vS), inc=tS)
        A("tensor", lambda e: e.matmul(PSB[1][:], sb_ones_r[:], lrow[0:1, 4, :],
                                       start=True, stop=True), inc=tS)
        A("vector", lambda e: e.tensor_copy(bcn[:], PSB[1][:]), waits=W(tS), inc=vS)
        for cch in range(8):
            A("vector", lambda e, cch=cch: e.tensor_tensor(
                lntmp[:], src_fn(cch), PSB[0][:], ALU.mult), waits=W(ld), inc=vS)
            A("vector", lambda e: e.tensor_add(lntmp[:], lntmp[:], bcn[:]), inc=vS)
            A("vector", lambda e, cch=cch: e.tensor_scalar(
                f32t[:, cch, :], lntmp[:], sb_lng[:, ln_idx, cch:cch + 1],
                sb_lnb[:, ln_idx, cch:cch + 1], ALU.mult, ALU.add),
              waits=W(ld), inc=vS)
            if out16_fn is not None:
                A("vector", lambda e, cch=cch: e.tensor_copy(
                    out16_fn(cch), f32t[:, cch, :]), waits=W(ld), inc=vS)
        if park32_ap is not None:
            A("sync", lambda e: e.dma_start(
                out=park32_ap[:, :, 512 * j:512 * (j + 1)], in_=f32t[:]),
              waits=W(vS), inc=ld, inc_n=16)
        if park16_ap is not None:
            A("sync", lambda e: e.dma_start(
                out=park16_ap[:, :, 512 * j:512 * (j + 1)], in_=park16_src),
              waits=W(vS), inc=ld, inc_n=16)

    we1b = scr("we1b", [NE, 128, 8, FFD], BF16)
    we2b = scr("we2b", [NE, 128, 32, D], BF16)
    for ee in range(NE):
        A("gpsimd", lambda e, ee=ee: e.dma_start(
            out=we1b[ee], in_=we1[ee].rearrange("(c p) f -> p c f", p=128)),
          inc=gld, inc_n=16)
        A("gpsimd", lambda e, ee=ee: e.dma_start(
            out=we2b[ee], in_=we2[ee].rearrange("(c p) d2 -> p c d2", p=128)),
          inc=gld, inc_n=16)
    wpark_k = (gld, gld.n)

    sE25 = ExitStack()
    resf = sE25.enter_context(nc.sbuf_tensor("e_resf", [128, 8, TOK], F32))
    sE12 = ExitStack()
    SE1 = lambda n, s, d=F32: sE12.enter_context(nc.sbuf_tensor(n, list(s), d))
    xnqT32 = SE1("e_xnq32", [128, 8, TOK])
    ext = [SE1(f"e_xt{i}", [128, D]) for i in range(2)]
    exc = [SE1(f"e_xc{i}", [128, D]) for i in range(2)]
    esq = SE1("e_sqs", [128, D])
    e_stat = SE1("e_stat", [128, 8 * 16])

    ln_pass(lambda j: xq[128 * j:128 * (j + 1), :], 16, e_stat, ext, exc, esq,
            [PSB[0][:, 0:128], PSB[1][:, 0:128]], None, xnqT32)

    wo_sb = SE1("e_wo", [128, 8, D], BF16)
    ctxch = [SE1(f"e_ctxch{i}", [128, 8, 512], BF16) for i in range(2)]
    wo_r = wo.rearrange("(c p) e -> p c e", p=128)
    A("gpsimd", lambda e: e.dma_start(out=wo_sb[:], in_=wo_r[:]), inc=gld, inc_n=16)

    EM = {}

    def emw(key):
        m = EM.get(key)
        return [m] if m else []

    wo_ld = (gld, gld.n)
    for j in range(TOK // 512):
        A("sync", lambda e, j=j: e.dma_start(
            out=ctxch[j % 2][:], in_=ctx_park[:, :, 512 * j:512 * (j + 1)]),
          waits=emw(("ctxch", j % 2)), inc=ld, inc_n=16)
        ctx_ld = (ld, ld.n)
        for m in range(8):
            pm = 2 + (m % 2)
            for cch in range(8):
                A("tensor", lambda e, j=j, m=m, cch=cch, pm=pm: e.matmul(
                    PSB[pm][:], wo_sb[:, cch, 128 * m:128 * (m + 1)],
                    ctxch[j % 2][:, cch, :], start=(cch == 0), stop=(cch == 7)),
                  waits=([ctx_ld, wo_ld, (vS, vS.n)] if (j + m == 0) else
                         ([ctx_ld] + emw(("eb", pm)) if cch == 0 else [])), inc=tS)
            mmk = (tS, tS.n)
            if m == 7:
                EM[("ctxch", j % 2)] = mmk
            A("vector", lambda e, j=j, m=m, pm=pm: e.tensor_tensor(
                resf[:, m, 512 * j:512 * (j + 1)], PSB[pm][:],
                xnqT32[:, m, 512 * j:512 * (j + 1)], ALU.add), waits=[mmk], inc=vS)
            EM[("eb", pm)] = (vS, vS.n)

    k.barrier(drain=[ld, gld])
    sE12.close()

    sE35 = ExitStack()
    SE3 = lambda n, s, d=F32: sE35.enter_context(nc.sbuf_tensor(n, list(s), d))
    h1_16 = SE3("e_h116", [128, 8, TOK], BF16)
    lnA = mk_lnbufs(sE35, "lnA")
    for j in range(TOK // 512):
        ln_feature(lnA, lambda cch, j=j: resf[:, cch, 512 * j:512 * (j + 1)], j, 0,
                   lambda cch, j=j: h1_16[:, cch, 512 * j:512 * (j + 1)],
                   h1_park, None, None)

    if debug_phase == "E3":
        d1 = dbg("h1", [128, 8, TOK], BF16)
        k.wait_at("sync", vS, vS.n)
        A("sync", lambda e: e.dma_start(out=d1[:], in_=h1_16[:]), inc=st, inc_n=16)
        k.wait_at("sync", st, st.n)
        k.emit()
        return nc, dbg_outs

    sE4 = ExitStack()
    SE4 = lambda n, s, d=F32: sE4.enter_context(nc.sbuf_tensor(n, list(s), d))
    hid = SE4("e_hid", [128, 32, 512], BF16)
    wsl = [SE4(f"e_wsl{i}", [128, 8, 128], BF16) for i in range(2)]
    w2sl = [SE4(f"e_w2sl{i}", [128, D], BF16) for i in range(2)]
    h1c = SE4("e_h1c", [128, 8, 512])
    bffs = SE4("e_bffs", [128, 40])
    A("sync", lambda e: e.dma_start(out=bffs[:, 0:32], in_=bff1_c[:]), inc=ld, inc_n=16)
    A("sync", lambda e: e.dma_start(out=bffs[:, 32:40], in_=bff2_c[:]), inc=ld, inc_n=16)
    wff1_r = wff1.rearrange("(c p) f -> p c f", p=128)

    bff_ld = None
    for j in range(TOK // 512):
        A("sync", lambda e, j=j: e.dma_start(
            out=h1c[:], in_=h1_park[:, :, 512 * j:512 * (j + 1)]),
          waits=emw("h1c_used"), inc=ld, inc_n=16)
        h1c_ld = (ld, ld.n)
        if bff_ld is None:
            bff_ld = h1c_ld
        for fc in range(32):
            wb = fc % 2
            A("gpsimd", lambda e, fc=fc, wb=wb: e.dma_start(
                out=wsl[wb][:], in_=wff1_r[:, :, 128 * fc:128 * (fc + 1)]),
              waits=emw(("wsl", wb)), inc=gld, inc_n=16)
            sl_ld = (gld, gld.n)
            pf = fc % 4
            for cch in range(8):
                A("tensor", lambda e, fc=fc, cch=cch, pf=pf, wb=wb, j=j: e.matmul(
                    PSB[pf][:], wsl[wb][:, cch, :], h1_16[:, cch, 512 * j:512 * (j + 1)],
                    start=(cch == 0), stop=(cch == 7)),
                  waits=(([sl_ld] + emw(("fb", pf)) +
                          (emw(("res", pf)) if fc < 4 else []) +
                          ([(vS, vS.n), (sS, sS.n)] if (j + fc == 0) else []))
                         if cch == 0 else []), inc=tS)
            EM[("wsl", wb)] = (tS, tS.n)
            A("scalar", lambda e, fc=fc, pf=pf: e.activation(
                hid[:, fc, :], PSB[pf][:], AF.Relu, bias=bffs[:, fc:fc + 1], scale=1.0),
              waits=[(tS, tS.n), bff_ld] + emw(("hid_used", fc)), inc=sS)
            EM[("fb", pf)] = (sS, sS.n)
        for fc in range(32):
            wb = fc % 2
            A("gpsimd", lambda e, fc=fc, wb=wb: e.dma_start(
                out=w2sl[wb][:], in_=wff2[128 * fc:128 * (fc + 1), :]),
              waits=emw(("w2sl", wb)), inc=gld, inc_n=16)
            sl_ld = (gld, gld.n)
            relu_k = EM[("fb", fc % 2)] if fc >= 30 else None
            for m in range(8):
                A("tensor", lambda e, fc=fc, m=m, wb=wb: e.matmul(
                    PSB[m][:], w2sl[wb][:, 128 * m:128 * (m + 1)], hid[:, fc, :],
                    start=(fc == 0), stop=(fc == 31)),
                  waits=([sl_ld] + ([(sS, sS.n)] if fc == 0 else []) +
                         (emw(("res", m)) if fc == 0 else [])
                         if (m == 0 or fc == 0) else []), inc=tS)
            EM[("w2sl", wb)] = (tS, tS.n)
            EM[("hid_used", fc)] = (tS, tS.n)
        ff2k = (tS, tS.n)
        for m in range(8):
            A("vector", lambda e, j=j, m=m: e.scalar_tensor_tensor(
                resf[:, m, 512 * j:512 * (j + 1)], PSB[m][:], bffs[:, 32 + m:33 + m],
                h1c[:, m, :], ALU.add, ALU.add),
              waits=([ff2k] if m == 0 else []), inc=vS)
            EM[("res", m)] = (vS, vS.n)
        EM["h1c_used"] = (vS, vS.n)

    k.barrier(drain=[ld, gld])
    sE4.close()

    if debug_phase == "E4":
        d1 = dbg("h2raw", [128, 8, TOK])
        k.wait_at("sync", vS, vS.n)
        A("sync", lambda e: e.dma_start(out=d1[:], in_=resf[:]), inc=st, inc_n=16)
        k.wait_at("sync", st, st.n)
        k.emit()
        return nc, dbg_outs

    mgm16_park = scr("mgm16_park", [128, 16, NE], BF16)
    mgm32_park = scr("mgm32_park", [128, 16, NE], F32)
    gatec_park = scr("gatec_park", [128, 16], F32)
    h2tb = SE3("e_h2tb", [128, 8, 512], BF16)
    wg_sb = SE3("e_wg", [128, 8, NE])
    lgs = SE3("e_lgs", [8, 512])
    lgtok = SE3("e_lgtok", [128, 16, NE])
    gmx = SE3("e_gmx", [128, 16, 3])
    getm = SE3("e_getm", [128, NE])
    mgm = SE3("e_mgm", [128, 16, NE])
    mgm16 = SE3("e_mgm16", [128, 16, NE], BF16)
    gatecs = SE3("e_gatecs", [128, 16])
    sb_tiebm = SE3("e_tiebm", [128, 16, NE])
    wg_r = wg.rearrange("(c p) e -> p c e", p=128)
    A("sync", lambda e: e.dma_start(out=wg_sb[:], in_=wg_r[:]), inc=ld, inc_n=16)
    A("sync", lambda e: e.dma_start(out=sb_tiebm[:], in_=tiebm[:]), inc=ld, inc_n=16)

    for j in range(TOK // 512):
        ln_feature(lnA, lambda cch, j=j: resf[:, cch, 512 * j:512 * (j + 1)], j, 1,
                   lambda cch: h2tb[:, cch, :], h3_park, h2b_park, h2tb[:])
        for cch in range(8):
            A("tensor", lambda e, cch=cch: e.matmul(
                PSB[4][0:8, :], wg_sb[:, cch, :], lnA["f32t"][:, cch, :],
                start=(cch == 0), stop=(cch == 7)),
              waits=(W(vS, sS, ld) if cch == 0 else []), inc=tS)
        A("vector", lambda e: e.tensor_copy(lgs[:], PSB[4][0:8, :]), waits=W(tS), inc=vS)
        for c2 in range(4):
            A("tensor", lambda e, c2=c2: e.transpose(
                PSB[5][0:128, 0:8], lgs[:, 128 * c2:128 * (c2 + 1)], sb_ident[0:8, 0:8]),
              waits=W(vS), inc=tS)
            A("vector", lambda e, j=j, c2=c2: e.tensor_copy(
                lgtok[:, 4 * j + c2, :], PSB[5][0:128, 0:8]), waits=W(tS), inc=vS)

    A("vector", lambda e: e.tensor_tensor(
        lgtok[:], lgtok[:], sb_tiebm[:], ALU.add), waits=W(ld), inc=vS)
    for ch in range(16):
        A("vector", lambda e, ch=ch: e.reduce_max(gmx[:, ch, 0:1], lgtok[:, ch, :],
                                                  axis=AX.X), inc=vS)
        A("vector", lambda e, ch=ch: e.reduce_max(gmx[:, ch, 1:2], lgtok[:, ch, :],
                                                  axis=AX.X, negate=True), inc=vS)
        fv()
        A("scalar", lambda e, ch=ch: e.activation(getm[:], lgtok[:, ch, :], AF.Exp,
                                                  bias=gmx[:, ch, 1:2],
                                                  accum_out=gmx[:, ch, 2:3]),
          waits=W(vS), inc=sS)
        fs()
        fs()
        A("vector", lambda e, ch=ch: e.reciprocal(gmx[:, ch, 2:3], gmx[:, ch, 2:3]),
          waits=W(sS), inc=vS)
        for ee in range(NE):
            A("vector", lambda e, ch=ch, ee=ee: e.tensor_tensor(
                mgm[:, ch, ee:ee + 1], lgtok[:, ch, ee:ee + 1], gmx[:, ch, 0:1],
                ALU.is_ge), inc=vS)
    A("vector", lambda e: e.tensor_copy(mgm16[:], mgm[:]), inc=vS)
    A("vector", lambda e: e.tensor_copy(gatecs[:], gmx[:, :, 2:3]), inc=vS)
    fv()
    fv()
    A("sync", lambda e: e.dma_start(out=mgm16_park[:], in_=mgm16[:]),
      waits=W(vS), inc=ld, inc_n=16)
    A("sync", lambda e: e.dma_start(out=mgm32_park[:], in_=mgm[:]),
      inc=ld, inc_n=16)
    A("sync", lambda e: e.dma_start(out=gatec_park[:], in_=gatecs[:]),
      inc=ld, inc_n=16)

    if debug_phase == "E5":
        d1 = dbg("h2", [128, 8, TOK], BF16)
        d2 = dbg("mgm", [128, 16, NE], BF16)
        d3 = dbg("gatec", [128, 16])
        k.wait_at("sync", vS, vS.n)
        k.wait_at("sync", ld, ld.n)
        A("sync", lambda e: e.dma_start(out=d1[:], in_=h2b_park[:]), inc=st, inc_n=16)
        A("sync", lambda e: e.dma_start(out=d2[:], in_=mgm16_park[:]), inc=st, inc_n=16)
        A("sync", lambda e: e.dma_start(out=d3[:], in_=gatec_park[:]), inc=st, inc_n=16)
        k.wait_at("sync", st, st.n)
        k.emit()
        return nc, dbg_outs

    k.barrier(drain=[ld, gld])
    sE35.close()
    sE25.close()

    flgd = scr("flgd", [NE, 128], mybir.dt.int32)
    sMO = ExitStack()
    moe_acc = sMO.enter_context(nc.sbuf_tensor("e_moeacc", [128, 8, TOK], BF16))
    sE6 = ExitStack()
    SE6 = lambda n, s, d=F32: sE6.enter_context(nc.sbuf_tensor(n, list(s), d))
    hT = SE6("e_hT", [128, NTI, D], BF16)
    grank_bc = SE6("e_grankbc", [128, TOK])
    gate_bc = SE6("e_gatebc", [128, TOK], BF16)
    we1h = [SE6(f"e_we1h{i}", [128, 8, 512], BF16) for i in range(2)]
    we2h = [SE6(f"e_we2h{i}", [128, 8, 512], BF16) for i in range(2)]
    mhid = SE6("e_mhid", [128, 32, 512], BF16)
    ghc = SE6("e_ghc", [128, 8, 512], BF16)
    eout = SE6("e_eout", [128, 4, D], BF16)
    psm = SE6("e_psm", [128, 4, TOK], BF16)
    ptb = [SE6(f"e_ptb{i}", [128, 512], BF16) for i in range(3)]
    hstrf = [SE6(f"e_hstrf{i}", [128, 8, 128]) for i in range(2)]
    mgm16s = SE6("e_mgm16s", [128, 16, NE], BF16)
    mgm32s = SE6("e_mgm32s", [128, 16, NE])
    gatecb = SE6("e_gatecb", [128, 16])
    rankT = SE6("e_rankT", [128, 16, NE])
    grankT = SE6("e_grankT", [128, 16])
    growrow = SE6("e_growrow", [1, TOK])
    gaterow = SE6("e_gaterow", [1, TOK], BF16)
    cntc = SE6("e_cntc", [8, 1])
    e_flg = SE6("e_flg", [8, 128])
    e_flgi = sE6.enter_context(nc.sbuf_tensor("e_flgi", [8, 128], mybir.dt.int32))
    sb_trio = SE6("e_trio", [128, 128], BF16)
    sb_aon1 = SE6("e_aon1", [128, 128], BF16)
    sb_eoffm = SE6("e_eoffm", [128, 16, NE])
    sb_offm = SE6("e_offm", [NE, 128])
    sb_iom = SE6("e_iom", [128, 512])
    sb_iosc = SE6("e_iosc", [128, 1])
    bex = [SE6(f"e_bex{i}", [128, 32]) for i in range(2)]
    be2rw = [SE6("e_be2rw0", [1, D])] * 2

    A("sync", lambda e: e.dma_start(out=sb_trio[:], in_=trio[:]), inc=ld, inc_n=16)
    A("sync", lambda e: e.dma_start(out=sb_aon1[:], in_=aon1[:]), inc=ld, inc_n=16)
    A("sync", lambda e: e.dma_start(out=sb_eoffm[:], in_=eoffm[:]), inc=ld, inc_n=16)
    A("sync", lambda e: e.dma_start(out=sb_offm[:], in_=offm[:]), inc=ld, inc_n=16)
    A("sync", lambda e: e.dma_start(out=sb_iom[:], in_=iom[:]), inc=ld, inc_n=16)
    A("sync", lambda e: e.dma_start(out=sb_iosc[:], in_=iosc[:]), inc=ld, inc_n=16)
    A("sync", lambda e: e.dma_start(out=mgm16s[:], in_=mgm16_park[:]), inc=ld, inc_n=16)
    A("sync", lambda e: e.dma_start(out=mgm32s[:], in_=mgm32_park[:]),
      inc=ld, inc_n=16)
    A("sync", lambda e: e.dma_start(out=gatecb[:], in_=gatec_park[:]), inc=ld, inc_n=16)
    pre_ld = (ld, ld.n)
    pre_gld = (gld, gld.n)
    for m in range(8):
        A("vector", lambda e, m=m: e.tensor_tensor(
            moe_acc[:, m, :], moe_acc[:, m, :], moe_acc[:, m, :], ALU.is_lt), inc=vS)
    for tc in range(NTI):
        hb = tc % 2
        A("sync", lambda e, tc=tc, hb=hb: e.dma_start(
            out=hstrf[hb][:], in_=h3_park[:, :, 128 * tc:128 * (tc + 1)]),
          waits=emw(("hstr_used", hb)), inc=ld, inc_n=16)
        hs_ld = (ld, ld.n)
        for cch in range(8):
            pb = cch % 2
            A("tensor", lambda e, tc=tc, cch=cch, pb=pb, hb=hb: e.transpose(
                PSB[6 + pb][:, 0:128], hstrf[hb][:, cch, :], sb_ident[:]),
              waits=([hs_ld] if cch == 0 else []) + emw(("htr", pb)), inc=tS)
            A("vector", lambda e, tc=tc, cch=cch, pb=pb: e.tensor_copy(
                hT[:, tc, 128 * cch:128 * (cch + 1)], PSB[6 + pb][:, 0:128]),
              waits=[(tS, tS.n)], inc=vS)
            EM[("htr", pb)] = (vS, vS.n)
        EM[("hstr_used", hb)] = (tS, tS.n)
    for ch in range(16):
        pb = ch % 2
        for ch2 in range(ch + 1):
            A("tensor", lambda e, ch=ch, ch2=ch2, pb=pb: e.matmul(
                PSB[pb][:, 0:8],
                sb_trio[:] if ch2 == ch else sb_aon1[:],
                mgm16s[:, ch2, :], start=(ch2 == 0), stop=(ch2 == ch)),
              waits=([pre_ld] + emw(("rkb", pb)) if ch2 == 0 else []), inc=tS)
        A("vector", lambda e, ch=ch, pb=pb: e.tensor_copy(
            rankT[:, ch, :], PSB[pb][:, 0:8]), waits=[(tS, tS.n)], inc=vS)
        EM[("rkb", pb)] = (vS, vS.n)
    for ch in range(16):
        A("tensor", lambda e, ch=ch: e.matmul(
            PSB[2][0:8, 0:1], mgm16s[:, ch, :], sb_aon1[:, 0:1],
            start=(ch == 0), stop=(ch == 15)),
          waits=([pre_ld] if ch == 0 else []), inc=tS)
    A("vector", lambda e: e.tensor_copy(cntc[:], PSB[2][0:8, 0:1]),
      waits=[(tS, tS.n)], inc=vS)
    fv()
    A("vector", lambda e: e.tensor_single_scalar(
        e_flg[:], sb_offm[:], cntc[0:8, 0:1], ALU.is_lt), waits=W(ld), inc=vS)
    A("vector", lambda e: e.tensor_copy(e_flgi[:], e_flg[:]), inc=vS)
    fv()
    fv()
    A("sync", lambda e: e.dma_start(out=flgd[:], in_=e_flgi[:]), waits=W(vS),
      inc=ld, inc_n=16)
    flags_ready = (ld, ld.n)
    A("vector", lambda e: e.tensor_tensor(
        rankT[:], rankT[:], sb_eoffm[:], ALU.add), waits=W(ld, gld), inc=vS)
    A("vector", lambda e: e.tensor_tensor(
        rankT[:], rankT[:], mgm32s[:], ALU.mult), inc=vS)
    for ch in range(16):
        A("vector", lambda e, ch=ch: e.reduce_sum(
            grankT[:, ch:ch + 1], rankT[:, ch, :], axis=AX.X), inc=vS)
    fv()
    for ch in range(16):
        pb = ch % 2
        A("tensor", lambda e, ch=ch, pb=pb: e.transpose(
            PSB[4 + pb][0:1, 0:128], grankT[:, ch:ch + 1], sb_ident[:]),
          waits=W(vS) + emw(("grb", pb)), inc=tS)
        A("vector", lambda e, ch=ch, pb=pb: e.tensor_copy(
            growrow[0:1, 128 * ch:128 * (ch + 1)], PSB[4 + pb][0:1, 0:128]),
          waits=[(tS, tS.n)], inc=vS)
        EM[("grb", pb)] = (vS, vS.n)
        A("tensor", lambda e, ch=ch, pb=pb: e.transpose(
            PSB[4 + pb][0:1, 0:128], gatecb[:, ch:ch + 1], sb_ident[:]),
          waits=W(vS) + emw(("gtb", pb)), inc=tS)
        A("vector", lambda e, ch=ch, pb=pb: e.tensor_copy(
            gaterow[0:1, 128 * ch:128 * (ch + 1)], PSB[4 + pb][0:1, 0:128]),
          waits=[(tS, tS.n)], inc=vS)
        EM[("gtb", pb)] = (vS, vS.n)
    fv()
    fv()
    for w in range(4):
        A("tensor", lambda e, w=w: e.matmul(
            PSB[4][:, 0:512], sb_ones_r[:], growrow[0:1, 512 * w:512 * (w + 1)],
            start=True, stop=True), waits=W(vS) + emw("bcb"), inc=tS)
        A("vector", lambda e, w=w: e.tensor_copy(
            grank_bc[:, 512 * w:512 * (w + 1)], PSB[4][:, 0:512]),
          waits=[(tS, tS.n)], inc=vS)
        EM["bcb"] = (vS, vS.n)
        A("tensor", lambda e, w=w: e.matmul(
            PSB[4][:, 0:512], sb_aon1[0:1, 0:128], gaterow[0:1, 512 * w:512 * (w + 1)],
            start=True, stop=True), waits=W(vS) + emw("bcb"), inc=tS)
        A("vector", lambda e, w=w: e.tensor_copy(
            gate_bc[:, 512 * w:512 * (w + 1)], PSB[4][:, 0:512]),
          waits=[(tS, tS.n)], inc=vS)
        EM["bcb"] = (vS, vS.n)
    fv()
    routing_done_v = (vS, vS.n)

    if debug_phase == "E6r":
        d1 = dbg("grankT", [128, 16])
        d2 = dbg("cnt", [8, 1])
        d6 = dbg("mgm32", [128, 16, NE])
        d7 = dbg("rankTf", [128, 16, NE])
        d8 = dbg("h3p", [128, 8, TOK])
        A("sync", lambda e: e.dma_start(out=d8[:], in_=h3_park[:]), inc=st, inc_n=16)
        A("sync", lambda e: e.dma_start(out=d6[:], in_=mgm32s[:]), inc=st, inc_n=16)
        A("sync", lambda e: e.dma_start(out=d7[:], in_=rankT[:]), inc=st, inc_n=16)
        d3 = dbg("grankbc", [128, TOK])
        d4 = dbg("gatebc", [128, TOK], BF16)
        d5 = dbg("hTd", [128, NTI, D], BF16)
        k.wait_at("sync", vS, vS.n)
        k.wait_at("sync", ld, ld.n)
        A("sync", lambda e: e.dma_start(out=d1[:], in_=grankT[:]), inc=st, inc_n=16)
        A("sync", lambda e: e.dma_start(out=d2[:], in_=cntc[:]), inc=st, inc_n=16)
        A("sync", lambda e: e.dma_start(out=d3[:], in_=grank_bc[:]), inc=st, inc_n=16)
        A("sync", lambda e: e.dma_start(out=d4[:], in_=gate_bc[:]), inc=st, inc_n=16)
        A("sync", lambda e: e.dma_start(out=d5[:], in_=hT[:]), inc=st, inc_n=16)
        k.wait_at("sync", st, st.n)
        k.emit()
        return nc, dbg_outs

    JMAP = [0, 1, 2, 2]
    we1r = [we1b[ee] for ee in range(NE)]
    we2r = [we2b[ee] for ee in range(NE)]

    for ee in range(NE):
        eb = ee % 2
        A("sync", lambda e, ee=ee, eb=eb: e.dma_start(
            out=bex[eb][:], in_=be1_c[ee]), waits=emw(("bex_used", eb)),
          inc=ld, inc_n=16)
        A("sync", lambda e, ee=ee, eb=eb: e.dma_start(
            out=be2rw[0][:], in_=be2_r[ee:ee + 1, :]),
          waits=emw("be2u"), inc=ld, inc_n=16)
        bex_ld = (ld, ld.n)
        for si in range(4):
            base_g = 2048 * ee + 512 * si
            fap = (lambda ee=ee, jf=JMAP[si]:
                   flgd[ee:ee + 1, 32 * jf:32 * jf + 1])

            k.begin_region(fap, flags_ready)
            for ph in range(2):
                A("sync", lambda e, ee=ee, ph=ph: e.dma_start(
                    out=we1h[ph][:], in_=we1r[ee][:, :, 512 * ph:512 * (ph + 1)]),
                  waits=[wpark_k] + emw(("w1u", ph)), inc=ld, inc_n=16)
                EM[("w1l", ph)] = (ld, ld.n)
            for tc in range(NTI):
                pb = tc % 3
                A("vector", lambda e, tc=tc, pb=pb, base_g=base_g: e.tensor_scalar(
                    ptb[pb][:], sb_iom[:], grankT[:, tc:tc + 1], float(-base_g),
                    ALU.subtract, ALU.is_equal),
                  waits=emw(("ptu", pb)), inc=vS)
                ptk = (vS, vS.n)
                for m in range(8):
                    A("tensor", lambda e, tc=tc, m=m, pb=pb: e.matmul(
                        PSB[m][:], hT[:, tc, 128 * m:128 * (m + 1)], ptb[pb][:],
                        start=(tc == 0), stop=(tc == NTI - 1)),
                      waits=(([ptk] if m == 0 else [])
                             + (emw(("sct", m)) if tc == 0 else [])), inc=tS)
                EM[("ptu", pb)] = (tS, tS.n)
            gk = (tS, tS.n)
            for m in range(8):
                A("vector", lambda e, m=m: e.tensor_copy(ghc[:, m, :], PSB[m][:]),
                  waits=[gk] + (emw("ghu") if m == 0 else []), inc=vS)
            ghk = (vS, vS.n)
            for fc in range(32):
                pf = fc % 4
                ph = fc // 4
                for cch in range(8):
                    A("tensor", lambda e, fc=fc, cch=cch, pf=pf, ph=ph: e.matmul(
                        PSB[pf][:], we1h[ph % 2][:, cch, 128 * (fc % 4):128 * (fc % 4 + 1)],
                        ghc[:, cch, :], start=(cch == 0), stop=(cch == 7)),
                      waits=(([EM[("w1l", ph % 2)], ghk, bex_ld] + emw(("mb", pf)))
                             if cch == 0 else []), inc=tS)
                ff1k = (tS, tS.n)
                if fc % 4 == 3:
                    EM[("w1u", ph % 2)] = ff1k
                    if ph + 2 < 8:
                        A("sync", lambda e, ee=ee, ph=ph: e.dma_start(
                            out=we1h[ph % 2][:],
                            in_=we1r[ee][:, :, 512 * (ph + 2):512 * (ph + 3)]),
                          waits=emw(("w1u", ph % 2)), inc=ld, inc_n=16)
                        EM[("w1l", ph % 2)] = (ld, ld.n)
                A("vector", lambda e, fc=fc, pf=pf, eb=eb: e.tensor_scalar(
                    mhid[:, fc, :], PSB[pf][:], bex[eb][:, fc:fc + 1], 0.0,
                    ALU.add, ALU.max),
                  waits=[ff1k, bex_ld] + emw(("mhu", fc)), inc=vS)
                EM[("mb", pf)] = (vS, vS.n)
            relu_k = (vS, vS.n)
            EM["ghu"] = (tS, tS.n)
            k.end_region()

            k.begin_region(fap, flags_ready)
            for sc in range(4):
                A("vector", lambda e, sc=sc, base_g=base_g: e.tensor_scalar(
                    psm[:, sc, :], grank_bc[:], sb_iosc[:, 0:1],
                    float(base_g + 128 * sc), ALU.subtract, ALU.is_equal),
                  waits=emw("psmu") if sc == 0 else [], inc=vS)
                A("vector", lambda e, sc=sc: e.tensor_tensor(
                    psm[:, sc, :], psm[:, sc, :], gate_bc[:], ALU.mult), inc=vS)
            psm_k = (vS, vS.n)
            for half in range(2):
                for g4 in range(4):
                    pw = (half * 4 + g4) % 2
                    A("sync", lambda e, ee=ee, g4=g4, half=half, pw=pw: e.dma_start(
                        out=we2h[pw][:],
                        in_=we2r[ee][:, 8 * g4:8 * (g4 + 1), 512 * half:512 * (half + 1)]),
                      waits=emw(("w2u", pw)), inc=ld, inc_n=16)
                    EM[("w2l", pw)] = (ld, ld.n)
                    for fc8 in range(8):
                        fc = 8 * g4 + fc8
                        for sc in range(4):
                            bank = (4 + sc) if half == 0 else sc
                            A("tensor", lambda e, fc=fc, sc=sc, bank=bank, pw=pw,
                              half=half: e.matmul(
                                PSB[bank][:], mhid[:, fc, 128 * sc:128 * (sc + 1)],
                                we2h[pw][:, fc % 8, :],
                                start=(fc == 0), stop=False),
                              waits=(([relu_k, EM[("w2l", pw)]]
                                      + (emw(("scv", 7)) if (fc == 0 and half == 0)
                                         else []))
                                     if (fc8 == 0 and sc == 0) else []), inc=tS)
                        if fc8 == 7:
                            EM[("w2u", pw)] = (tS, tS.n)
                        if half == 1:
                            EM[("mhu", fc)] = (tS, tS.n)
                for sc in range(4):
                    bank = (4 + sc) if half == 0 else sc
                    A("tensor", lambda e, sc=sc, bank=bank, half=half, eb=eb: e.matmul(
                        PSB[bank][:], sb_ones_r[:],
                        be2rw[eb][0:1, 512 * half:512 * (half + 1)],
                        start=False, stop=True), inc=tS)
                ff2k = (tS, tS.n)
                if half == 1:
                    EM["be2u"] = ff2k
                for sc in range(4):
                    bank = (4 + sc) if half == 0 else sc
                    A("vector", lambda e, sc=sc, bank=bank, half=half: e.tensor_copy(
                        eout[:, sc, 512 * half:512 * (half + 1)], PSB[bank][:]),
                      waits=[ff2k] + (emw("eou") if half + sc == 0 else []), inc=vS)
            eo_k = (vS, vS.n)
            for tp in range(4):
                for m in range(8):
                    for sc in range(4):
                        A("tensor", lambda e, tp=tp, m=m, sc=sc: e.matmul(
                            PSB[m][:], eout[:, sc, 128 * m:128 * (m + 1)],
                            psm[:, sc, 512 * tp:512 * (tp + 1)],
                            start=(sc == 0), stop=(sc == 3)),
                          waits=(([eo_k, psm_k] + emw(("scv", m)))
                                 if sc == 0 else []), inc=tS)
                    sck = (tS, tS.n)
                    A("vector", lambda e, tp=tp, m=m: e.tensor_tensor(
                        moe_acc[:, m, 512 * tp:512 * (tp + 1)],
                        moe_acc[:, m, 512 * tp:512 * (tp + 1)], PSB[m][:], ALU.add),
                      waits=[sck], inc=vS)
                    EM[("scv", m)] = (vS, vS.n)
                    if tp == 3:
                        EM[("sct", m)] = (vS, vS.n)
            EM["psmu"] = (tS, tS.n)
            EM["eou"] = (tS, tS.n)
            k.end_region()
        EM[("bex_used", eb)] = (vS, vS.n)

    k.barrier(drain=[ld, gld, aS])

    if debug_phase == "E6":
        d1 = dbg("moe", [128, 8, TOK], BF16)
        d2 = dbg("cnt", [8, 1])
        d6 = dbg("mgm32", [128, 16, NE])
        A("sync", lambda e: e.dma_start(out=d1[:], in_=moe_acc[:]), inc=st, inc_n=16)
        A("sync", lambda e: e.dma_start(out=d2[:], in_=cntc[:]), inc=st, inc_n=16)
        A("sync", lambda e: e.dma_start(out=d6[:], in_=mgm32s[:]), inc=st, inc_n=16)
        k.wait_at("sync", st, st.n)
        k.emit()
        return nc, dbg_outs

    sE6.close()


    sE7F = ExitStack()
    gT = sE7F.enter_context(nc.sbuf_tensor("f_gT", [128, 8, TOK], F32))
    sE7 = ExitStack()
    SE7 = lambda n, s, d=F32: sE7.enter_context(nc.sbuf_tensor(n, list(s), d))
    h316c = [SE7(f"e_h316c{i}", [128, 8, 512], BF16) for i in range(2)]
    h3c = SE7("e_h3c", [128, 8, 512])
    wof_sb = SE7("e_wof", [128, 8, D], BF16)
    lnB = mk_lnbufs(sE7, "lnB")
    A("gpsimd", lambda e: e.dma_start(
        out=wof_sb[:], in_=wout_f.rearrange("(c p) e -> p c e", p=128)),
      inc=gld, inc_n=16)
    wof_ld = (gld, gld.n)

    for j in range(TOK // 512):
        A("sync", lambda e, j=j: e.dma_start(
            out=h3c[:], in_=h3_park[:, :, 512 * j:512 * (j + 1)]),
          waits=W(tS, sS, vS), inc=ld, inc_n=16)
        A("vector", lambda e, j=j: e.tensor_tensor(
            h3c[:], h3c[:], moe_acc[:, :, 512 * j:512 * (j + 1)],
            ALU.add), waits=W(ld), inc=vS)
        ln_feature(lnB, lambda cch: h3c[:, cch, :], j, 2,
                   lambda cch, j=j: h316c[j % 2][:, cch, :],
                   None, None, None)
        ln_k = (vS, vS.n)
        for m in range(8):
            pm = m % 2
            for cch in range(8):
                A("tensor", lambda e, j=j, m=m, cch=cch, pm=pm: e.matmul(
                    PSB[2 + pm][:], wof_sb[:, cch, 128 * m:128 * (m + 1)],
                    h316c[j % 2][:, cch, :],
                    start=(cch == 0), stop=(cch == 7)),
                  waits=(([wof_ld, ln_k] + emw(("gb", pm)) +
                          ([(sS, sS.n)] if (j + m == 0) else []))
                         if cch == 0 else []), inc=tS)
            A("scalar", lambda e, j=j, m=m, pm=pm: e.activation(
                gT[:, m, 512 * j:512 * (j + 1)], PSB[2 + pm][:], AF.Silu),
              waits=[(tS, tS.n)], inc=sS)
            EM[("gb", pm)] = (sS, sS.n)

    k.barrier(drain=[ld, gld])
    sE7.close()

    sF = ExitStack()
    SF = lambda n, s, d=F32: sF.enter_context(nc.sbuf_tensor(n, list(s), d))
    cnc = [SF(f"f_cnc{i}", [128, 8, 512], BF16) for i in range(2)]
    woa_sb = SF("f_woa", [128, 8, D], BF16)
    prod = SF("f_prod", [128, 8, 512])
    otok = SF("f_otok", [128, D])
    xqt = SF("f_xqt", [128, D])
    A("gpsimd", lambda e: e.dma_start(
        out=woa_sb[:], in_=wout_a.rearrange("(c p) e -> p c e", p=128)),
      inc=gld, inc_n=16)

    woa_ld = (gld, gld.n)
    for j in range(TOK // 512):
        A("sync", lambda e, j=j: e.dma_start(
            out=cnc[j % 2][:], in_=cn_park[:, :, 512 * j:512 * (j + 1)]),
          waits=emw(("cnc_used", j % 2)), inc=ld, inc_n=16)
        cn_ld = (ld, ld.n)
        for m in range(8):
            pm = m % 2
            for cch in range(8):
                A("tensor", lambda e, j=j, m=m, cch=cch, pm=pm: e.matmul(
                    PSB[pm][:], woa_sb[:, cch, 128 * m:128 * (m + 1)],
                    cnc[j % 2][:, cch, :], start=(cch == 0), stop=(cch == 7)),
                  waits=(([cn_ld, woa_ld] + emw(("sb", pm)) +
                          ([(vS, vS.n), (sS, sS.n)] if (j + m == 0) else []))
                         if cch == 0 else []), inc=tS)
            if m == 7:
                EM[("cnc_used", j % 2)] = (tS, tS.n)
            A("vector", lambda e, j=j, m=m, pm=pm: e.tensor_tensor(
                prod[:, m, :], PSB[pm][:], gT[:, m, 512 * j:512 * (j + 1)], ALU.mult),
              waits=[(tS, tS.n)] + emw(("prod_used", m)), inc=vS)
            EM[("sb", pm)] = (vS, vS.n)
        for s2 in range(4):
            A("sync", lambda e, j=j, s2=s2: e.dma_start(
                out=xqt[:], in_=xq[512 * j + 128 * s2:512 * j + 128 * (s2 + 1), :]),
              waits=emw("xqt_used"), inc=ld, inc_n=16)
            xq_ld = (ld, ld.n)
            for m in range(8):
                pm = 2 + (m % 2)
                A("tensor", lambda e, s2=s2, m=m, pm=pm, j=j: e.transpose(
                    PSB[pm][:, 0:128], prod[:, m, 128 * s2:128 * (s2 + 1)], sb_ident[:]),
                  waits=(emw(("sb", 0)) + emw(("sb", 1)) if (s2 == 0 and m == 0) else [])
                        + emw(("fb", pm)), inc=tS)
                if s2 == 3:
                    EM[("prod_used", m)] = (tS, tS.n)
                A("vector", lambda e, m=m, pm=pm: e.tensor_copy(
                    otok[:, 128 * m:128 * (m + 1)], PSB[pm][:, 0:128]),
                  waits=[(tS, tS.n)] + (emw("otok_stored") if m == 0 else []), inc=vS)
                EM[("fb", pm)] = (vS, vS.n)
            A("vector", lambda e: e.tensor_add(otok[:], otok[:], xqt[:]),
              waits=[xq_ld], inc=vS)
            EM["xqt_used"] = (vS, vS.n)
            A("sync", lambda e, j=j, s2=s2: e.dma_start(
                out=out[512 * j + 128 * s2:512 * j + 128 * (s2 + 1), :], in_=otok[:]),
              waits=[(vS, vS.n)], inc=st, inc_n=16)
            EM["otok_stored"] = (st, st.n)

    k.wait_at("sync", st, st.n)
    k.emit()
    return nc, dbg_outs



def _pe_table():
    pos = np.arange(S - 1, -1, -1, dtype=np.float32)
    inv_freq = (1.0 / 10000.0 ** (np.arange(0, D, 2, dtype=np.float32) / D)).astype(np.float32)
    sinusoid = pos[:, None] * inv_freq[None, :]
    return np.concatenate([np.sin(sinusoid), np.cos(sinusoid)], axis=-1).astype(np.float32)


_CACHED = {}


def _get_program():
    if "nc" not in _CACHED:
        _CACHED["nc"] = build_program(None)[0]
    return _CACHED["nc"]


def make_inmaps(inputs):
    x = np.ascontiguousarray(inputs["L0_attn_x"])

    pe = _pe_table()
    peT_full = pe.T
    ii = np.arange(128)[:, None]
    cc = np.arange(256)[None, :]
    ident = np.eye(128, dtype=np.float32)
    rmask = (np.arange(128)[:, None] >= np.arange(128)[None, :]).astype(np.float32)

    gq = np.zeros((64, NH, 512), np.float32)
    gk = np.zeros((64, NH, 512), np.float32)
    jj = np.arange(512) % 128
    for h in range(NH):
        g = GAMMAS[h]
        gq[:, h, :] = 0.125 * (g ** jj)[None, :]
        gk[:, h, :] = (g ** (-jj.astype(np.float64))).astype(np.float32)[None, :]

    com = dict(
        peT=None,
        rmask=rmask, ident=ident,
        gq_t=gq, gk_t=gk,
        wq_a=inputs["L0_attn_Wq"], wk_a=inputs["L0_attn_Wk"], wv_a=inputs["L0_attn_Wv"],
        gnw_t=np.ascontiguousarray(inputs["L0_attn_gn_weight"].reshape(NH, 64).T),
        gnb_t=np.ascontiguousarray(inputs["L0_attn_gn_bias"].reshape(NH, 64).T),
        wout_a=inputs["L0_attn_Wout"],
        wqkv=inputs["L0_ffn_Wqkv"], wr=inputs["L0_ffn_Wr"],
        ub_t=np.ascontiguousarray(inputs["L0_ffn_u_bias"].T),
        vb_t=np.ascontiguousarray(inputs["L0_ffn_v_bias"].T),
        wo=inputs["L0_ffn_Wo"],
        ln_g=np.stack([inputs["L0_ffn_ln1_g"].reshape(8, 128).T,
                       inputs["L0_ffn_ln2_g"].reshape(8, 128).T,
                       inputs["L0_ffn_ln3_g"].reshape(8, 128).T], axis=1),
        ln_b=np.stack([inputs["L0_ffn_ln1_b"].reshape(8, 128).T,
                       inputs["L0_ffn_ln2_b"].reshape(8, 128).T,
                       inputs["L0_ffn_ln3_b"].reshape(8, 128).T], axis=1),
        wff1=inputs["L0_ffn_Wff1"],
        bff1_c=np.ascontiguousarray(inputs["L0_ffn_bff1"].reshape(32, 128).T),
        wff2=inputs["L0_ffn_Wff2"],
        bff2_c=np.ascontiguousarray(inputs["L0_ffn_bff2"].reshape(8, 128).T),
        wg=inputs["L0_ffn_Wg"],
        we1=inputs["L0_ffn_We1"],
        be1_c=np.ascontiguousarray(inputs["L0_ffn_be1"].reshape(NE, 32, 128).transpose(0, 2, 1)),
        we2=inputs["L0_ffn_We2"],
        be2_r=inputs["L0_ffn_be2"],
        wout_f=inputs["L0_ffn_Wout"],
        eoffm=np.broadcast_to(2048.0 * np.arange(NE, dtype=np.float32)[None, None, :],
                              (128, 16, NE)),
        offm=np.broadcast_to(
            np.repeat(np.array([0.0, 512.0, 1024.0, 1e9], np.float32), 32)[None, :],
            (NE, 128)),
        iom=np.broadcast_to(np.arange(512, dtype=np.float32)[None, :], (128, 512)),
        iosc=np.arange(128, dtype=np.float32)[:, None],
        tiebm=np.broadcast_to(
            ((7 - np.arange(NE, dtype=np.float32)) * 1e-6)[None, None, :],
            (128, 16, NE)),
    )
    com = {kk: (np.ascontiguousarray(v, dtype=np.float32) if v is not None else None)
           for kk, v in com.items()}
    com["trio"] = np.triu(np.ones((128, 128)), 1).astype(ml_dtypes.bfloat16)
    com["aon1"] = np.ones((128, 128), dtype=ml_dtypes.bfloat16)
    com["spillq"] = np.where(np.arange(128)[None, :] > np.arange(128)[:, None],
                             NEG, 0.0).astype(ml_dtypes.bfloat16)
    com["negt"] = np.full((128, 128), NEG, dtype=ml_dtypes.bfloat16)

    in_maps = []
    for c in range(8):
        b, p = c // 2, c % 2
        rows = np.concatenate([np.arange(256 * t + 128 * p, 256 * t + 128 * p + 128)
                               for t in range(NTI)])
        xqc = np.ascontiguousarray(x[b][rows])
        peTc = np.zeros((D, S + 256), np.float32)
        peTc[:, 128 * p:128 * p + S] = peT_full
        peTc_in = np.ascontiguousarray(
            peTc.reshape(8, 128, S + 256).transpose(1, 0, 2)).astype(ml_dtypes.bfloat16)
        am = np.where(cc <= ii + 128 * p, 0.0, NEG).astype(np.float32)
        sel = np.zeros((64, 2), np.float32)
        sel[:, p] = 1.0
        m = dict(com)
        m["peT"] = peTc_in
        m["pmq"] = np.full((128, 128), NEG if p == 0 else 0.0,
                           dtype=ml_dtypes.bfloat16)
        m["xb"] = np.ascontiguousarray(x[b])
        m["xq"] = xqc
        m["amask"] = am
        m["selc"] = sel
        in_maps.append(m)
    return in_maps


def kernel(**inputs):
    in_maps = make_inmaps(inputs)
    nc = _get_program()
    res = run_bass_kernel_spmd(nc, in_maps, core_ids=list(range(8)))

    outx = np.zeros((4, S, D), np.float32)
    for c in range(8):
        b, p = c // 2, c % 2
        oc = np.asarray(res.results[c]["out"], dtype=np.float32)
        for t in range(NTI):
            outx[b, 256 * t + 128 * p:256 * t + 128 * p + 128] = oc[128 * t:128 * (t + 1)]
    return outx


if __name__ == "__main__":
    build_program()
